# version 05
# baseline (speedup 1.0000x reference)
import numpy as np
import ml_dtypes
from contextlib import ExitStack

import concourse.bass as bass
import concourse.mybir as mybir
from concourse.bass_utils import run_bass_kernel_spmd

F32 = mybir.dt.float32
BF16 = mybir.dt.bfloat16
I32 = mybir.dt.int32
AF = mybir.ActivationFunctionType
ALU = mybir.AluOpType
AX = mybir.AxisListType

SEQ = 8192
C = 2048
NHEAD = 16
D = 128
NCORES = 8
TLOC = SEQ // NCORES
NTT = TLOC // 128
F_MOE = 1408
F_SH = 5632
I_L1 = 8192
EPS = 1e-5
RSQD = float(1.0 / np.sqrt(np.float64(D)))
BF = ml_dtypes.bfloat16

A_XNT = 0
A_WG = 16384
A_WU = 24576
A_H = 32768
A_PJ = 44032
A_SU = 49664
A_RW = 50176
A_SGW = 50304
A_WN0 = 50320
A_WN1 = 40960
A_Q = 0
A_K = 4096
A_V = 20480
A_WQKV = 37120
A_SLAB = 49408
A_P = 50432
A_PT = 51456
A_WST = 50432
A_CTX = 0
A_WON = 16384
A_WO = 24576

VW = 130
VH = 64 * VW


class K:

    ENGS = ("tensor", "vector", "scalar", "gpsimd", "sync")

    def __init__(self, nc, es):
        self.nc = nc
        self.es = es
        self.blk = es.enter_context(nc.Block())
        self.sems = {}
        for n in ("T", "V", "S", "G", "DL", "DQ", "CC"):
            self.sems[n] = [es.enter_context(nc.semaphore(n)), 0]
        self.dq = {e: [] for e in self.ENGS}

    def cnt(self, name):
        return self.sems[name][1]

    @staticmethod
    def _flat(waits):
        out = []
        for w in waits:
            if w is None:
                continue
            if isinstance(w, list):
                out.extend(t for t in w if t is not None)
            else:
                out.append(w)
        return out

    def _collapse(self, waits):
        acc = {}
        for (sn, v) in self._flat(waits):
            if v > 0 and v > acc.get(sn, 0):
                acc[sn] = v
        return sorted(acc.items())

    def sem_ent(self, name):
        if name not in self.sems:
            self.sems[name] = [self.es.enter_context(self.nc.semaphore(name)), 0]
        return self.sems[name]

    def _emit(self, eng, fn, waits, sem, n):
        ent = self.sem_ent(sem)
        selfwait = ent[1] + n if sem in ("DL", "DQ") else None
        self.dq[eng].append((self._collapse(waits), fn, ent[0], n, selfwait))
        ent[1] += n
        return ent[1]

    def dma(self, ch, fn, waits=(), q="gpsimd"):
        ent = self.sem_ent(ch)
        self.dq[q].append((self._collapse(waits), fn, ent[0], 16, None))
        ent[1] += 16
        return ent[1]

    def T(self, fn, waits=(), n=1):   return self._emit("tensor", fn, waits, "T", n)
    def V(self, fn, waits=(), n=1):   return self._emit("vector", fn, waits, "V", n)
    def S(self, fn, waits=(), n=1):   return self._emit("scalar", fn, waits, "S", n)
    def G(self, fn, waits=(), n=1):   return self._emit("gpsimd", fn, waits, "G", n)
    def DL(self, fn, waits=(), n=16): return self._emit("gpsimd", fn, waits, "DL", n)
    def DQ(self, fn, waits=(), n=16): return self._emit("sync", fn, waits, "DQ", n)
    def CC(self, fn, waits=(), n=1):  return self._emit("gpsimd", fn, waits, "CC", n)

    def DLmulti(self, fns, waits=()):
        ent = self.sems["DL"]
        semh = ent[0]

        total = ent[1] + 16 * len(fns)

        def f(g):
            for fn in fns:
                fn(g).then_inc(semh, 16)
        self.dq["gpsimd"].append((self._collapse(waits), f, None, 0, total))
        ent[1] = total
        return ent[1]

    ENG = dict(T="tensor", V="vector", S="scalar", G="gpsimd", Q="sync")

    def waiter(self, eng, waits):
        self.dq[self.ENG[eng]].append((self._collapse(waits), None, None, 0, None))

    def finalize(self):
        hmap = self.sems
        for eng in self.ENGS:
            items = self.dq[eng]
            if not items:
                continue

            def body(e, items=items):
                for (wl, fn, semh, n, selfwait) in items:
                    for (sn, v) in wl:
                        e.wait_ge(hmap[sn][0], v)
                    if fn is None:
                        continue
                    r = fn(e)
                    if r is not None and semh is not None:
                        r.then_inc(semh, n)
                    if selfwait is not None:
                        sw = hmap["DL"] if (semh is hmap["DL"][0] or semh is None) else hmap["DQ"]
                        e.wait_ge(sw[0], selfwait)
            getattr(self.blk, eng)(body)


class Rot:
    def __init__(self, n):
        self.n = n
        self.free = [[("T", 0)]] * n

    def war(self, i):
        return self.free[i % self.n]

    def done(self, i, sem, val):
        self.free[i % self.n] = [(sem, val)]

    def set_all(self, waits):
        self.free = [list(waits)] * self.n


class NS:
    def __init__(self, **kw):
        self.__dict__.update(kw)


def make_bufs(k):
    nc, es = k.nc, k.es
    b = NS(
        arena=es.enter_context(nc.sbuf_tensor("arena", [128, 53248], BF16)),
        x_sb=es.enter_context(nc.sbuf_tensor("x_sb", [128, 8 * 2048], F32)),
        xn_t=es.enter_context(nc.sbuf_tensor("xn_t", [128, 2048], BF16)),
        sq_t=es.enter_context(nc.sbuf_tensor("sq_t", [128, 2048], F32)),
        stats=es.enter_context(nc.sbuf_tensor("stats", [128, 64], F32)),
        comb=es.enter_context(nc.sbuf_tensor("comb", [128, 80], F32)),
        scl=es.enter_context(nc.sbuf_tensor("scl", [128, 304], F32)),
        bias_sb=es.enter_context(nc.sbuf_tensor("bias_sb", [128, 8192], BF16)),
        att2=es.enter_context(nc.sbuf_tensor("att2", [128, 2048], BF16)),
        ident=es.enter_context(nc.sbuf_tensor("ident", [128, 128], BF16)),
        identF=es.enter_context(nc.sbuf_tensor("identF", [128, 128], F32)),
        ones2=es.enter_context(nc.sbuf_tensor("ones2", [34, 128], BF16)),
        tri=es.enter_context(nc.sbuf_tensor("tri", [128, 4 * 512], BF16)),
        ps=[es.enter_context(nc.psum_tensor("ps%d" % i, [128, 512], F32)) for i in range(4)],
        psE=es.enter_context(nc.psum_tensor("psE", [128, 512], F32)),
        psF=es.enter_context(nc.psum_tensor("psF", [128, 512], F32)),
        psT=es.enter_context(nc.psum_tensor("psT", [128, 2048], BF16)),
        pid_sb=es.enter_context(nc.sbuf_tensor("pid_sb", [1, 2], mybir.dt.uint32)),
    )
    b.psTb = Rot(2)
    b.ps_rot = Rot(4)
    b.psE_rot = Rot(1)
    b.psF_rot = Rot(1)
    b.wg_rot = Rot(2)
    b.wu_rot = Rot(2)
    b.wnat_rot = Rot(2)
    b.pjnat_rot = Rot(2)
    b.pj_war = ("T", 0)
    b.h_war = ("T", 0)
    b.su_war = ("V", 0)
    b.sq_war = ("S", 0)
    b.xn_war = ("T", 0)
    b.xnT_done = ("V", 0)
    b.psT_bid = 0
    b.slab_rot = Rot(2)
    b.p_rot = Rot(2)
    b.pt_rot = Rot(2)
    b.wo_rot = Rot(2)
    b.q_war = ("T", 0)
    b.wst_war = ("T", 0)
    b.won_war = ("T", 0)
    b.ctxbuf_war = ("T", 0)
    b.vones_guard = ("T", 0)
    b.bias_guard = ("T", 0)
    b.att2_guard = ("T", 0)
    b.att2_ready = ("T", 0)
    b.wqkv_ready = []
    b.qkvbias_ready = ("V", 0)
    b.kv_ready = []
    b.ag_guard = None
    b.castdone = {}
    b.sbsb_guard = None
    return b


def phase_boundary(k, b):
    pg = [(s, ent[1]) for s, ent in k.sems.items()]
    for r in (b.psTb, b.ps_rot, b.psE_rot, b.psF_rot, b.wg_rot, b.wu_rot,
              b.wnat_rot, b.pjnat_rot, b.slab_rot, b.p_rot, b.pt_rot, b.wo_rot):
        r.set_all(pg)
    for name in ("pj_war", "h_war", "su_war", "sq_war", "xn_war", "q_war",
                 "wst_war", "won_war", "ctxbuf_war", "vones_guard", "bias_guard"):
        setattr(b, name, list(pg))
    return pg


def AP_(b, c0, w):
    return b.arena[:, c0:c0 + w]



def temove_batch(k, b, srcs, dst_ap, waits, eng, width=128, in_parts=128):
    n = len(srcs)
    pad = width if width % 2 == 0 else width + 1
    assert n * pad <= 1024
    bid = b.psT_bid % 2
    b.psT_bid += 1
    war = b.psTb.war(bid)
    base = 1024 * bid
    t_d = 0
    for i, sfn in enumerate(srcs):
        t_d = k.T(lambda t, i=i, sfn=sfn: t.transpose(
            b.psT[:, base + pad * i: base + pad * i + width],
            sfn(), b.ident[0:in_parts, 0:in_parts]),
            (list(waits) + [war]) if i == 0 else [])
    if pad == width:
        src_ap = b.psT[:, base:base + n * width]
    else:
        src_ap = b.psT[:, base:base + n * pad].rearrange(
            "p (a w) -> p a w", w=pad)[:, :, 0:width]
    if eng == "V":
        e_d = k.V(lambda v: v.tensor_copy(dst_ap, src_ap), [("T", t_d)])
    else:
        e_d = k.S(lambda s: s.copy(dst_ap, src_ap), [("T", t_d)])
    b.psTb.done(bid, eng, e_d)
    return (eng, e_d)



def emit_ln_tile(k, b, tt, x_guards):
    st = b.stats
    SUM, MEAN, M2, E2, VAR, RSTD, NEGB = 0, 1, 2, 3, 4, 5, 6
    xs = b.x_sb[:, 2048 * tt: 2048 * tt + 2048]

    v_sum = k.V(lambda v: v.reduce_sum(st[:, SUM:SUM+1], xs, axis=AX.X), x_guards)
    s_sq = k.S(lambda s: s.activation(b.sq_t[:, :], xs, AF.Square,
                                      accum_out=st[:, M2:M2+1]),
               list(x_guards) + [b.sq_war])
    b.sq_war = [("S", s_sq)]
    v1 = k.V(lambda v: v.tensor_scalar(st[:, MEAN:MEAN+1], st[:, SUM:SUM+1],
                                       1.0 / 2048.0, None, ALU.mult), [("V", v_sum)])
    v2 = k.V(lambda v: v.tensor_scalar(st[:, E2:E2+1], st[:, M2:M2+1],
                                       1.0 / 2048.0, None, ALU.mult),
             [("S", s_sq), ("V", v1)])
    v3 = k.V(lambda v: v.tensor_tensor(st[:, VAR:VAR+1], st[:, MEAN:MEAN+1],
                                       st[:, MEAN:MEAN+1], ALU.mult), [("V", v2)])
    v4 = k.V(lambda v: v.tensor_tensor(st[:, VAR:VAR+1], st[:, E2:E2+1],
                                       st[:, VAR:VAR+1], ALU.subtract), [("V", v3)])
    v4b = k.V(lambda v: v.tensor_scalar(st[:, VAR:VAR+1], st[:, VAR:VAR+1],
                                        float(EPS), None, ALU.add), [("V", v4)])
    s_sd = k.S(lambda s: s.activation(st[:, RSTD:RSTD+1], st[:, VAR:VAR+1],
                                      AF.Sqrt), [("V", v4b)])
    v5 = k.V(lambda v: v.reciprocal(st[:, RSTD:RSTD+1], st[:, RSTD:RSTD+1]),
             [("S", s_sd)])
    v6 = k.V(lambda v: v.tensor_tensor(st[:, NEGB:NEGB+1], st[:, MEAN:MEAN+1],
                                       st[:, RSTD:RSTD+1], ALU.mult), [("V", v5)])
    v7 = k.V(lambda v: v.tensor_scalar(st[:, NEGB:NEGB+1], st[:, NEGB:NEGB+1],
                                       -1.0, None, ALU.mult), [("V", v6)])
    s_xn = k.S(lambda s: s.activation(b.xn_t[:, :], xs, AF.Identity,
                                      bias=st[:, NEGB:NEGB+1],
                                      scale=st[:, RSTD:RSTD+1]),
               [("V", v7), b.xn_war])
    last = ("V", 0)
    for half in range(2):
        c0 = 8 * half
        base = A_XNT + 1024 * c0 + 128 * tt
        dst = b.arena[:, base:base + 8192].rearrange(
            "p (a w) -> p a w", w=1024)[:, :, 0:128]
        srcs = [(lambda cc=cc: b.xn_t[:, 128 * cc:128 * cc + 128])
                for cc in range(c0, c0 + 8)]
        last = temove_batch(k, b, srcs, dst, [("S", s_xn)], "V" if half == 0 else "S")
    b.xn_war = [("T", k.cnt("T"))]
    return last


def emit_ln_all(k, b, x_guards_per_tt):
    evs = []
    for tt in range(NTT):
        evs.append(emit_ln_tile(k, b, tt, x_guards_per_tt[tt]))
    b.xnT_done = [("V", k.cnt("V")), ("S", k.cnt("S"))]
    return b.xnT_done



def _wprep(k, b, src, nat_slot, w_base, w_slot, fj2, rot_w, dq, store_to=None):
    nat_slot = nat_slot % 2
    nat = AP_(b, A_WN0 if nat_slot == 0 else A_WN1, 2048)
    ld = k.dma("wn%d" % nat_slot, lambda g: g.dma_start(out=nat, in_=src),
               [b.wnat_rot.free[nat_slot]])
    nat_ready = ("wn%d" % nat_slot, ld)
    if dq is not None:
        scale_ap, bias_ap, ready = dq
        dqd = k.S(lambda s: s.activation(nat, nat, AF.Identity, bias=bias_ap,
                                         scale=scale_ap), [nat_ready, ready])
        nat_ready = ("S", dqd)
    if store_to is not None:
        k.DL(lambda g: g.dma_start(out=store_to, in_=nat), [nat_ready])
    for half in range(2):
        c0 = 8 * half
        base = w_base + 4096 * w_slot + 256 * c0 + 128 * fj2
        dst = b.arena[:, base:base + 8 * 256].rearrange(
            "p (a w) -> p a w", w=256)[:, :, 0:128]
        srcs = [(lambda cc=cc: nat[:, 128 * cc:128 * cc + 128])
                for cc in range(c0, c0 + 8)]
        temove_batch(k, b, srcs, dst, [nat_ready, rot_w.war(w_slot)],
                     "V" if half == 0 else "S")
    b.wnat_rot.free[nat_slot] = [("T", k.cnt("T"))]
    return [("V", k.cnt("V")), ("S", k.cnt("S"))]


def _wload_pair(k, b, scr_ap, fb, fp, nfj, w_base, slot, ch, cast_w):
    wd = 128 * min(2, nfj - 2 * fp)
    r0 = 128 * (fb + 2 * fp)
    last = 0
    for cc in range(16):
        last = k.dma(ch, lambda q, cc=cc, r0=r0, wd=wd, slot=slot:
                     q.dma_start_transpose(
                         b.arena[:, w_base + 4096*slot + 256*cc:
                                 w_base + 4096*slot + 256*cc + wd],
                         scr_ap[r0:r0 + wd, 128*cc:128*cc + 128]),
                     ([cast_w, b.sbsb_guard,
                       (b.wg_rot if w_base == A_WG else b.wu_rot).war(slot)]
                      if cc == 0 else []), q="sync")
    return [(ch, last)]


def _emit_ffn_group(k, b, tt_done, nfj, scale_mode, gid,
                    gate_scr=None, up_scr=None, down_scr=None, fb=0,
                    cast_g=None, cast_u=None, cast_d=None,
                    gate_src=None, up_src=None, down_src=None,
                    dq_gate=None, dq_up=None, dq_down=None,
                    gate_store=None, up_store=None, down_store=None):
    scratch = gate_scr is not None
    npair = (nfj + 1) // 2
    for fp in range(npair):
        slot = fp % 2
        if scratch:
            ev_g = _wload_pair(k, b, gate_scr, fb, fp, nfj, A_WG, slot, "tg%d" % slot, cast_g)
            ev_u = _wload_pair(k, b, up_scr, fb, fp, nfj, A_WU, slot, "tu%d" % slot, cast_u)
        for fj2 in range(min(2, nfj - 2 * fp)):
            fj = 2 * fp + fj2
            if not scratch:
                ev_g = _wprep(k, b, gate_src(fj), 0, A_WG, slot, fj2,
                              b.wg_rot, dq_gate(fj) if dq_gate else None,
                              gate_store(fj) if gate_store else None)
                ev_u = _wprep(k, b, up_src(fj), 1, A_WU, slot, fj2,
                              b.wu_rot, dq_up(fj) if dq_up else None,
                              up_store(fj) if up_store else None)
            for th in range(2):
                def mmg(t, slot=slot, fj2=fj2, th=th):
                    r = None
                    for cc in range(16):
                        r = t.matmul(b.psE[:, 0:512],
                                     AP_(b, A_WG + 4096*slot + 256*cc + 128*fj2, 128),
                                     AP_(b, A_XNT + 1024*cc + 512*th, 512),
                                     start=(cc == 0), stop=(cc == 15))
                    return r
                g_d = k.T(mmg, ev_g + [b.xnT_done, b.psE_rot.war(0)])
                def mmu(t, slot=slot, fj2=fj2, th=th):
                    r = None
                    for cc in range(16):
                        r = t.matmul(b.psF[:, 0:512],
                                     AP_(b, A_WU + 4096*slot + 256*cc + 128*fj2, 128),
                                     AP_(b, A_XNT + 1024*cc + 512*th, 512),
                                     start=(cc == 0), stop=(cc == 15))
                    return r
                u_d = k.T(mmu, ev_u + [b.xnT_done, b.psF_rot.war(0)])
                hc = A_H + 1024 * fj + 512 * th
                sg = k.S(lambda s, hc=hc: s.activation(AP_(b, hc, 512), b.psE[:, 0:512],
                                                       AF.Silu), [("T", g_d), b.h_war])
                b.psE_rot.done(0, "S", sg)
                ue = k.V(lambda v: v.tensor_copy(AP_(b, A_SU, 512), b.psF[:, 0:512]),
                         [("T", u_d), b.su_war])
                b.psF_rot.done(0, "V", ue)
                hm = k.V(lambda v, hc=hc: v.tensor_tensor(
                    AP_(b, hc, 512), AP_(b, hc, 512), AP_(b, A_SU, 512), ALU.mult),
                    [("V", ue), ("S", sg)])
                b.su_war = [("V", hm)]
        b.wg_rot.done(slot, "T", k.cnt("T"))
        b.wu_rot.done(slot, "T", k.cnt("T"))
    h_all = ("V", k.cnt("V"))

    fw = 128 * nfj
    for cb in range(4):
        if down_scr is not None:
            last = 0
            for fj in range(nfj):
                last = k.dma("pjt", lambda q, fj=fj, cb=cb:
                             q.dma_start_transpose(
                                 AP_(b, A_PJ + 512*fj, 512),
                                 down_scr[512*cb:512*cb + 512,
                                          128*(fb+fj):128*(fb+fj) + 128]),
                             ([cast_d, b.sbsb_guard, b.pj_war] if fj == 0 else []),
                             q="sync")
            pj_ready = [("pjt", last)]
        else:
            for cj in range(4):
                cq = 4 * cb + cj
                nslot = cq % 2
                war = b.wnat_rot.war(cq)
                nat = AP_(b, A_WN0 if nslot == 0 else A_WN1, fw)
                ldd = k.dma("wn%d" % nslot, lambda g, nat=nat, cq=cq: g.dma_start(
                    out=nat, in_=down_src(cq)), [war])
                nat_ready = ("wn%d" % nslot, ldd)
                if dq_down is not None:
                    scale_ap, bias_ap, ready = dq_down(cq)
                    dqd = k.S(lambda s, nat=nat, scale_ap=scale_ap, bias_ap=bias_ap:
                              s.activation(nat, nat, AF.Identity, bias=bias_ap,
                                           scale=scale_ap), [nat_ready, ready])
                    nat_ready = ("S", dqd)
                    if down_store is not None:
                        k.DL(lambda g, nat=nat, cq=cq: g.dma_start(
                            out=down_store(cq), in_=nat), [nat_ready])
                for f0 in range(0, nfj, 8):
                    nf = min(8, nfj - f0)
                    base = A_PJ + 512 * f0 + 128 * cj
                    dst = b.arena[:, base:base + 512 * nf].rearrange(
                        "p (a w) -> p a w", w=512)[:, :, 0:128] if nf > 1 else AP_(b, base, 128)
                    srcs = [(lambda nat=nat, fj=fj: nat[:, 128 * fj:128 * fj + 128])
                            for fj in range(f0, f0 + nf)]
                    temove_batch(k, b, srcs, dst, [nat_ready, b.pj_war],
                                 "V" if (cj + f0) % 2 == 0 else "S")
                b.wnat_rot.done(cq, "T", k.cnt("T"))
            pj_ready = [("V", k.cnt("V")), ("S", k.cnt("S"))]
        for quad in range(2):
            for fj in range(nfj):
                w = []
                if fj == 0:
                    w = pj_ready + [h_all] + [b.ps_rot.war(i) for i in range(4)]
                def mmd(t, fj=fj, quad=quad, nfj=nfj):
                    r = None
                    for ti in range(4):
                        tt = 4 * quad + ti
                        r = t.matmul(b.ps[ti][:, 0:512],
                                     AP_(b, A_H + 1024 * fj + 128 * tt, 128),
                                     AP_(b, A_PJ + 512 * fj, 512),
                                     start=(fj == 0), stop=(fj == nfj - 1))
                    return r
                k.T(mmd, w)
            mm_fin = ("T", k.cnt("T"))
            for ti in range(4):
                tt = 4 * quad + ti
                xs = b.x_sb[:, 2048 * tt + 512 * cb: 2048 * tt + 512 * cb + 512]
                if scale_mode[0] == "comb":
                    sc_ap = b.comb[:, 8 * tt + scale_mode[1]: 8 * tt + scale_mode[1] + 1]
                    w_extra = b.comb_ready
                elif scale_mode[0] == "sgate":
                    sc_ap = b.comb[:, 64 + tt: 64 + tt + 1]
                    w_extra = b.comb_ready
                else:
                    sc_ap = scale_mode[1]
                    w_extra = []
                u_d = k.V(lambda v, ti=ti, xs=xs, sc_ap=sc_ap: v.scalar_tensor_tensor(
                    xs, b.ps[ti][:, 0:512], sc_ap, xs, op0=ALU.mult, op1=ALU.add),
                    [mm_fin, tt_done[tt], w_extra])
                tt_done[tt] = ("V", u_d)
                b.ps_rot.done(ti, "V", u_d)
        b.pj_war = [("T", k.cnt("T"))]
    b.h_war = [("T", k.cnt("T"))]


def _emit_router(k, b, wsrc):
    st = b.stats
    ld = k.dma("cn", lambda g: g.dma_start(out=b.xn_t[0:8, 0:2048], in_=wsrc.gate_w),
               [b.xnT_done, b.xn_war])
    srcs = [(lambda cc=cc: b.xn_t[0:8, 128 * cc:128 * cc + 128]) for cc in range(16)]
    temove_batch(k, b, srcs, AP_(b, A_RW, 128), [("cn", ld)], "V", width=8, in_parts=8)
    ld2 = k.dma("cn", lambda g: g.dma_start(out=b.xn_t[0:1, 0:2048], in_=wsrc.sgw),
                [("T", k.cnt("T"))])
    srcs = [(lambda cc=cc: b.xn_t[0:1, 128 * cc:128 * cc + 128]) for cc in range(16)]
    temove_batch(k, b, srcs, AP_(b, A_SGW, 16), [("cn", ld2)], "V", width=1, in_parts=1)
    b.xn_war = [("T", k.cnt("T"))]
    rw_ready = ("V", k.cnt("V"))

    P0, PM, M1, M2_, L0, SG = 16, 24, 32, 33, 34, 35
    for tt in range(NTT):
        def mm(t, tt=tt):
            r = None
            for cc in range(16):
                t.matmul(b.psE[:, 0:8], AP_(b, A_XNT + 1024 * cc + 128 * tt, 128),
                         AP_(b, A_RW + 8 * cc, 8), start=(cc == 0), stop=(cc == 15))
            for cc in range(16):
                r = t.matmul(b.psE[:, 8:9], AP_(b, A_XNT + 1024 * cc + 128 * tt, 128),
                             AP_(b, A_SGW + cc, 1), start=(cc == 0), stop=(cc == 15))
            return r
        m_d = k.T(mm, [rw_ready, b.xnT_done, b.psE_rot.war(0)])
        v1a = k.V(lambda v: v.reduce_max(st[:, M1:M1+1], b.psE[:, 0:8], axis=AX.X),
                  [("T", m_d)])
        v1 = k.V(lambda v: v.tensor_scalar(st[:, M1:M1+1], st[:, M1:M1+1], -1.0,
                                           None, ALU.mult), [("V", v1a)])
        s1 = k.S(lambda s: s.activation(st[:, P0:P0+8], b.psE[:, 0:8], AF.Exp,
                                        bias=st[:, M1:M1+1],
                                        accum_out=st[:, L0:L0+1]), [("V", v1)])
        v2 = k.V(lambda v: v.reciprocal(st[:, L0:L0+1], st[:, L0:L0+1]), [("S", s1)])
        v3 = k.V(lambda v: v.tensor_scalar(st[:, P0:P0+8], st[:, P0:P0+8],
                                           st[:, L0:L0+1], None, ALU.mult), [("V", v2)])
        v4 = k.V(lambda v: v.reduce_max(st[:, M1:M1+1], st[:, P0:P0+8], axis=AX.X),
                 [("V", v3)])
        v5 = k.V(lambda v: v.scalar_tensor_tensor(
            st[:, PM:PM+8], st[:, P0:P0+8], st[:, M1:M1+1], st[:, P0:P0+8],
            op0=ALU.is_lt, op1=ALU.mult), [("V", v4)])
        v6 = k.V(lambda v: v.reduce_max(st[:, M2_:M2_+1], st[:, PM:PM+8], axis=AX.X),
                 [("V", v5)])
        v7 = k.V(lambda v, tt=tt: v.scalar_tensor_tensor(
            b.comb[:, 8*tt:8*tt+8], st[:, P0:P0+8], st[:, M2_:M2_+1], st[:, P0:P0+8],
            op0=ALU.is_ge, op1=ALU.mult), [("V", v6)])
        v8 = k.V(lambda v, tt=tt: v.tensor_scalar(b.comb[:, 8*tt:8*tt+8],
                                           b.comb[:, 8*tt:8*tt+8], 0.5, None,
                                           ALU.mult), [("V", v7)])
        s2 = k.S(lambda s: s.activation(st[:, SG:SG+1], b.psE[:, 8:9], AF.Sigmoid),
                 [("T", m_d), ("S", s1)])
        v9 = k.V(lambda v, tt=tt: v.tensor_scalar(b.comb[:, 64+tt:64+tt+1], st[:, SG:SG+1],
                                           0.5, None, ALU.mult), [("S", s2), ("V", v8)])
        b.psE_rot.done(0, "V", v9)
    b.comb_ready = [("V", k.cnt("V"))]


def _emit_l1_scales(k, b, wsrc):
    for idx, (src_, col, n) in enumerate([(wsrc.gate_scale, 0, 64),
                                          (wsrc.up_scale, 64, 64),
                                          (wsrc.down_scale, 128, 16)]):
        ld = k.dma("cn", lambda g, src_=src_, n=n: g.dma_start(
            out=b.sq_t[0:n, 0:128], in_=src_.rearrange("(a p) -> a p", p=128)),
            [b.sq_war])
        war = b.psE_rot.war(0)
        t_d = k.T(lambda t, n=n: t.transpose(b.psE[:, 0:n], b.sq_t[0:n, 0:128],
                                             b.identF[0:n, 0:n]),
                  [("cn", ld), war])
        b.sq_war = [("T", t_d)]
        e_d = k.V(lambda v, col=col, n=n: v.tensor_copy(
            b.scl[:, col:col + n], b.psE[:, 0:n]), [("T", t_d)])
        b.psE_rot.done(0, "V", e_d)
    vb = k.V(lambda v: v.tensor_scalar(b.scl[:, 144:288], b.scl[:, 0:144],
                                       -128.0, None, ALU.mult),
             [("V", k.cnt("V"))])
    b.l1s_ready = [("V", vb)]


def emit_cast_l0(k, b, w, scr):
    for e in range(8):
        c1 = k.DL(lambda g, e=e: g.dma_start(out=scr.fc1b[e], in_=w.fc1[e]))
        b.castdone["fc1_%d" % e] = ("DL", c1)
        c2 = k.DL(lambda g, e=e: g.dma_start(out=scr.fc2b[e], in_=w.fc2[e]))
        b.castdone["fc2_%d" % e] = ("DL", c2)
        c3 = k.DL(lambda g, e=e: g.dma_start(out=scr.projb[e], in_=w.proj[e]))
        b.castdone["proj_%d" % e] = ("DL", c3)
    for nm, dst, srcw in (("sfc1", scr.sfc1b, w.sfc1), ("sfc2", scr.sfc2b, w.sfc2),
                          ("sproj", scr.sprojb, w.sproj)):
        c = 0
        for h in range(4):
            n0 = (dst.shape[0] * h) // 4
            n1 = (dst.shape[0] * (h + 1)) // 4
            c = k.DL(lambda g, dst=dst, srcw=srcw, n0=n0, n1=n1: g.dma_start(
                out=dst[n0:n1, :], in_=srcw[n0:n1, :]))
        b.castdone[nm] = ("DL", c)


def emit_ffn(k, b, kind, wsrc, tt_done, scr=None, l1_from_scratch=False):
    if kind == "l0":
        _emit_router(k, b, wsrc)
        for e in range(8):
            _emit_ffn_group(
                k, b, tt_done, nfj=11, scale_mode=("comb", e), gid=("e", e),
                gate_scr=scr.fc1b[e], up_scr=scr.fc2b[e], down_scr=scr.projb[e],
                fb=0, cast_g=b.castdone["fc1_%d" % e], cast_u=b.castdone["fc2_%d" % e],
                cast_d=b.castdone["proj_%d" % e])
        for gi in range(4):
            _emit_ffn_group(
                k, b, tt_done, nfj=11, scale_mode=("sgate", None), gid=("s", gi),
                gate_scr=scr.sfc1b, up_scr=scr.sfc2b, down_scr=scr.sprojb,
                fb=11 * gi, cast_g=b.castdone["sfc1"], cast_u=b.castdone["sfc2"],
                cast_d=b.castdone["sproj"])
    else:
        if l1_from_scratch:
            for gi in range(8):
                _emit_ffn_group(
                    k, b, tt_done, nfj=8, scale_mode=("const", 0.5), gid=("l1", gi),
                    gate_scr=scr.gateb, up_scr=scr.upb, down_scr=scr.downb,
                    fb=8 * gi, cast_g=b.castdone["l1g"], cast_u=b.castdone["l1u"],
                    cast_d=b.castdone["l1d"])
        else:
            _emit_l1_scales(k, b, wsrc)
            store = scr is not None
            for gi in range(8):
                fj0 = 8 * gi
                def dqg(fj, fj0=fj0):
                    i = fj0 + fj
                    return (b.scl[:, i:i+1], b.scl[:, 144+i:144+i+1], b.l1s_ready)
                def dqu(fj, fj0=fj0):
                    i = fj0 + fj
                    return (b.scl[:, 64+i:64+i+1], b.scl[:, 208+i:208+i+1], b.l1s_ready)
                def dqd(cq):
                    return (b.scl[:, 128+cq:128+cq+1], b.scl[:, 272+cq:272+cq+1], b.l1s_ready)
                _emit_ffn_group(
                    k, b, tt_done, nfj=8, scale_mode=("const", 0.5), gid=("l1", gi),
                    gate_src=lambda fj, fj0=fj0: wsrc.gate_w[128*(fj0+fj):128*(fj0+fj)+128, :],
                    up_src=lambda fj, fj0=fj0: wsrc.up_w[128*(fj0+fj):128*(fj0+fj)+128, :],
                    down_src=lambda cq, fj0=fj0: wsrc.down_w[128*cq:128*cq+128,
                                                             128*fj0:128*fj0+1024],
                    dq_gate=dqg, dq_up=dqu, dq_down=dqd,
                    gate_store=(lambda fj, fj0=fj0:
                                scr.gateb[128*(fj0+fj):128*(fj0+fj)+128, :]) if store else None,
                    up_store=(lambda fj, fj0=fj0:
                              scr.upb[128*(fj0+fj):128*(fj0+fj)+128, :]) if store else None,
                    down_store=(lambda cq, fj0=fj0:
                                scr.downb[128*cq:128*cq+128, 128*fj0:128*fj0+1024]) if store else None)
            if store:
                b.castdone["l1g"] = ("DL", k.cnt("DL"))
                b.castdone["l1u"] = ("DL", k.cnt("DL"))
                b.castdone["l1d"] = ("DL", k.cnt("DL"))
    return tt_done



def emit_att2_build(k, b, bias_dram, pbase):
    l1 = k.dma("cn", lambda g: g.dma_start(out=b.x_sb[0:1, 0:2048], in_=bias_dram),
               [b.att2_guard])
    s1 = k.S(lambda s: s.activation(b.xn_t[0:1, 0:2048], b.x_sb[0:1, 0:2048],
                                    AF.Identity), [("cn", l1), b.xn_war])
    l2 = k.dma("cn", lambda g: g.dma_start(out=b.att2[pbase:pbase+1, :],
                                    in_=b.xn_t[0:1, 0:2048]), [("S", s1)])
    v1 = k.V(lambda v: v.tensor_copy(b.x_sb[0:1, 2048:4096], b.xn_t[0:1, 0:2048]),
             [("S", s1)])
    v2 = k.V(lambda v: v.tensor_tensor(b.x_sb[0:1, 0:2048], b.x_sb[0:1, 0:2048],
                                       b.x_sb[0:1, 2048:4096], ALU.subtract),
             [("V", v1)])
    s2 = k.S(lambda s: s.activation(b.xn_t[0:1, 0:2048], b.x_sb[0:1, 0:2048],
                                    AF.Identity), [("V", v2), ("cn", l2)])
    l3 = k.dma("cn", lambda g: g.dma_start(out=b.att2[pbase+1:pbase+2, :],
                                    in_=b.xn_t[0:1, 0:2048]), [("S", s2)])
    b.att2_guard = [("cn", l3)]
    b.xn_war = [("S", s2)]


def emit_attn_wprep(k, b, layer, wsrc, scr=None):
    if layer == 0:
        last = 0
        for j2, wv in enumerate([wsrc.wq_s, wsrc.wk_s, wsrc.wv_s]):
            for cc in range(16):
                dst = b.arena[:, A_WQKV + 768*cc + 256*j2:
                              A_WQKV + 768*cc + 256*j2 + 256]
                last = k.dma("wq", lambda g, wv=wv, cc=cc, dst=dst: g.dma_start(
                    out=dst, in_=wv[128*cc:128*cc+128, :]), [b.wst_war])
        b.wqkv_ready = [("wq", last)]
    else:
        cw = k.DL(lambda g: g.dma_start(out=scr.wqkvb, in_=wsrc.wqkv_s), [])
        last = 0
        for cc in range(16):
            last = k.dma("wq", lambda q, cc=cc: q.dma_start_transpose(
                b.arena[:, A_WQKV + 768*cc: A_WQKV + 768*cc + 768],
                scr.wqkvb[0:768, 128*cc:128*cc+128]),
                ([("DL", cw), b.sbsb_guard, b.wst_war] if cc == 0 else []), q="sync")
        b.wqkv_ready = [("wq", last)]
        bl = 0
        for m in range(6):
            bl = k.dma("cn", lambda g, m=m: g.dma_start(
                out=b.comb[:, 72+m:72+m+1],
                in_=wsrc.bqkv_s[128*m:128*m+128].rearrange("(p a) -> p a", a=1)),
                [b.wst_war])
        vb = k.V(lambda v: v.tensor_scalar(b.comb[:, 78:80], b.comb[:, 72:74],
                                           RSQD, None, ALU.mult), [("cn", bl)])
        b.qkvbias_ready = [("V", vb)]


def emit_attn_qkv(k, b, layer, slab_src, q_dram):
    vo = []
    for h2 in range(2):
        ap = b.arena[:, A_V + VH*h2: A_V + VH*h2 + VH].rearrange(
            "p (a w) -> p a w", w=VW)[:, :, 128:129]
        vo.append(k.V(lambda v, ap=ap: v.memset(ap, 1.0), [b.vones_guard]))
    v_ones = ("V", vo[-1])
    qkv_bias = [b.qkvbias_ready] if layer == 1 else []

    for tb in range(8):
        for th in range(2):
            mm_last = 0
            for cc in range(16):
                slot = cc % 2
                ldw = k.dma("sl%d" % slot, lambda q, cc=cc, tb=tb, th=th, slot=slot:
                            q.dma_start(out=AP_(b, A_SLAB + 512*slot, 512),
                                        in_=slab_src(cc, tb, th)),
                            [b.slab_rot.war(slot), b.ag_guard], q="sync")
                wts = b.wqkv_ready + [("sl%d" % slot, ldw)]
                if cc == 0:
                    wts = wts + [b.ps_rot.war(i) for i in range(4)] +                           [b.psE_rot.war(0), b.psF_rot.war(0)]
                def mm(t, cc=cc, slot=slot):
                    r = None
                    for h2 in range(2):
                        t.matmul(b.ps[h2][:, 0:512],
                                 AP_(b, A_WQKV + 768*cc + 128*(2+h2), 128),
                                 AP_(b, A_SLAB + 512*slot, 512),
                                 start=(cc == 0), stop=(cc == 15))
                        t.matmul(b.ps[2+h2][:, 0:512],
                                 AP_(b, A_WQKV + 768*cc + 128*(4+h2), 128),
                                 AP_(b, A_SLAB + 512*slot, 512),
                                 start=(cc == 0), stop=(cc == 15))
                        r = t.matmul((b.psE if h2 == 0 else b.psF)[:, 0:512],
                                     AP_(b, A_WQKV + 768*cc + 128*h2, 128),
                                     AP_(b, A_SLAB + 512*slot, 512),
                                     start=(cc == 0), stop=(cc == 15))
                    return r
                mm_last = k.T(mm, wts + qkv_bias)
                b.slab_rot.done(slot, "T", mm_last)
            for h2 in range(2):
                kdst = AP_(b, A_K + 8192*h2 + 1024*tb + 512*th, 512)
                if layer == 1:
                    ek = k.S(lambda s, h2=h2, kdst=kdst: s.activation(
                        kdst, b.ps[h2][:, 0:512], AF.Identity,
                        bias=b.comb[:, 74+h2:74+h2+1]), [("T", mm_last)])
                else:
                    ek = k.S(lambda s, h2=h2, kdst=kdst: s.copy(
                        kdst, b.ps[h2][:, 0:512]), [("T", mm_last)])
                b.ps_rot.done(h2, "S", ek)
                ps_ = b.p_rot.war(0)
                stg = AP_(b, A_P, 512)
                if layer == 1:
                    ev = k.S(lambda s, h2=h2, stg=stg: s.activation(
                        stg, b.ps[2+h2][:, 0:512], AF.Identity,
                        bias=b.comb[:, 76+h2:76+h2+1]), [("T", mm_last), ps_])
                else:
                    ev = k.S(lambda s, h2=h2, stg=stg: s.copy(
                        stg, b.ps[2+h2][:, 0:512]), [("T", mm_last), ps_])
                b.ps_rot.done(2+h2, "S", ev)
                tile0 = 8*tb + 4*th
                base = A_V + VH*h2 + VW*tile0
                dst = b.arena[:, base:base + 4*VW].rearrange(
                    "p (a w) -> p a w", w=VW)[:, :, 0:128]
                srcs = [(lambda i=i, stg=stg: stg[:, 128*i:128*i+128])
                        for i in range(4)]
                temove_batch(k, b, srcs, dst, [("S", ev), v_ones], "V")
                b.p_rot.done(0, "T", k.cnt("T"))
                pq = b.pt_rot.war(0)
                qstg = AP_(b, A_PT, 512)
                qps = b.psE if h2 == 0 else b.psF
                if layer == 1:
                    eq = k.S(lambda s, qstg=qstg, qps=qps, h2=h2: s.activation(
                        qstg, qps[:, 0:512], AF.Identity, scale=RSQD,
                        bias=b.comb[:, 78+h2:78+h2+1]), [("T", mm_last), pq])
                else:
                    eq = k.S(lambda s, qstg=qstg, qps=qps: s.activation(
                        qstg, qps[:, 0:512], AF.Identity, scale=RSQD),
                        [("T", mm_last), pq])
                (b.psE_rot if h2 == 0 else b.psF_rot).done(0, "S", eq)
                dq = k.dma("qst", lambda g, h2=h2, tb=tb, th=th, qstg=qstg: g.dma_start(
                    out=q_dram[128*h2:128*h2+128, 1024*tb+512*th:1024*tb+512*th+512],
                    in_=qstg), [("S", eq)])
                b.pt_rot.done(0, "qst", dq)
    b.kv_ready = [("V", k.cnt("V")), ("S", k.cnt("S")), ("qst", k.cnt("qst"))]


def emit_attn_core(k, b, layer, bias_rows, q_dram, ctx_dram):
    bl = k.dma("cn", lambda g: g.dma_start(out=b.bias_sb[0:2, :], in_=bias_rows[0]),
               [b.bias_guard])
    bl = k.dma("cn", lambda g: g.dma_start(out=b.bias_sb[32:34, :], in_=bias_rows[1]),
               [b.bias_guard])
    bias_ready = ("cn", bl)
    ctxi = 0
    for quarter in range(2):
        for h2 in range(2):
            lq = k.dma("qld", lambda q, h2=h2, quarter=quarter: q.dma_start(
                out=AP_(b, A_Q, 4096),
                in_=q_dram[128*h2:128*h2+128, 4096*quarter:4096*quarter+4096]),
                [b.q_war] + b.kv_ready, q="sync")
            q_ready = ("qld", lq)
            for ql in range(32):
                qt = 32 * quarter + ql
                nkc = qt // 4 + 1
                rb = 32 * h2
                red = 0
                for i in range(nkc):
                    bank = b.psE if i % 2 == 0 else b.psF
                    rot = b.psE_rot if i % 2 == 0 else b.psF_rot
                    dvar = qt - 4 * i if i == nkc - 1 else None
                    def mma(t, i=i, bank=bank, ql=ql, rb=rb, dvar=dvar, h2=h2):
                        t.matmul(bank[:, 0:512], b.ones2[rb:rb+2, 0:128],
                                 b.bias_sb[rb:rb+2, 512*i:512*i+512],
                                 start=True, stop=False)
                        if dvar is not None:
                            t.matmul(bank[:, 0:512], b.ident[:, :],
                                     b.tri[:, 512*dvar:512*dvar+512],
                                     start=False, stop=False)
                        return t.matmul(bank[:, 0:512],
                                        AP_(b, A_Q + 128*ql, 128),
                                        AP_(b, A_K + 8192*h2 + 512*i, 512),
                                        start=False, stop=True)
                    m_d = k.T(mma, [q_ready, bias_ready, rot.war(0)])
                    red = k.V(lambda v, i=i, bank=bank: v.reduce_max(
                        b.stats[:, 40+i:40+i+1], bank[:, 0:512], axis=AX.X),
                        [("T", m_d)])
                    rot.done(0, "V", red)
                r1 = k.V(lambda v, nkc=nkc: v.reduce_max(
                    b.stats[:, 56:57], b.stats[:, 40:40+nkc], axis=AX.X),
                    [("V", red)])
                r2 = k.V(lambda v: v.tensor_scalar(
                    b.stats[:, 57:58], b.stats[:, 56:57], -1.0, None, ALU.mult),
                    [("V", r1)])
                negm = ("V", r2)
                pv_last = 0
                for i in range(nkc):
                    bank = b.psE if i % 2 == 0 else b.psF
                    rot = b.psE_rot if i % 2 == 0 else b.psF_rot
                    dvar = qt - 4 * i if i == nkc - 1 else None
                    def mmb(t, i=i, bank=bank, ql=ql, rb=rb, dvar=dvar, h2=h2):
                        t.matmul(bank[:, 0:512], b.ones2[rb:rb+2, 0:128],
                                 b.bias_sb[rb:rb+2, 512*i:512*i+512],
                                 start=True, stop=False)
                        if dvar is not None:
                            t.matmul(bank[:, 0:512], b.ident[:, :],
                                     b.tri[:, 512*dvar:512*dvar+512],
                                     start=False, stop=False)
                        return t.matmul(bank[:, 0:512],
                                        AP_(b, A_Q + 128*ql, 128),
                                        AP_(b, A_K + 8192*h2 + 512*i, 512),
                                        start=False, stop=True)
                    m_d = k.T(mmb, [rot.war(0)])
                    pslot = i % 2
                    pw = b.p_rot.war(pslot)
                    ex = k.S(lambda s, bank=bank, pslot=pslot: s.activation(
                        AP_(b, A_P + 512*pslot, 512), bank[:, 0:512], AF.Exp,
                        bias=b.stats[:, 57:58]), [("T", m_d), negm, pw])
                    rot.done(0, "S", ex)
                    p_ready = ("S", ex)
                    ptw = b.pt_rot.war(pslot)
                    srcs = [(lambda j=j, pslot=pslot: AP_(b, A_P + 512*pslot + 128*j, 128))
                            for j in range(4)]
                    ev = temove_batch(k, b, srcs, AP_(b, A_PT + 512*pslot, 512),
                                      [p_ready, ptw], "V")
                    b.p_rot.done(pslot, "T", k.cnt("T"))
                    def mpv(t, i=i, pslot=pslot, h2=h2, ci=ctxi, nkc=nkc):
                        r = None
                        for j in range(4):
                            tkb = 4*i + j
                            r = t.matmul(b.ps[ci][:, 0:129],
                                         AP_(b, A_PT + 512*pslot + 128*j, 128),
                                         b.arena[:, A_V + VH*h2 + VW*tkb:
                                                 A_V + VH*h2 + VW*tkb + 129],
                                         start=(i == 0 and j == 0),
                                         stop=(i == nkc - 1 and j == 3))
                        return r
                    w = [ev]
                    if i == 0:
                        w.append(b.ps_rot.war(ctxi))
                    pv_last = k.T(mpv, w)
                    b.pt_rot.done(pslot, "T", pv_last)
                rl = k.V(lambda v, ci=ctxi: v.reciprocal(
                    b.stats[:, 58:59], b.ps[ci][:, 128:129]), [("T", pv_last)])
                pw = b.p_rot.war(0)
                fs = k.S(lambda s, ci=ctxi: s.activation(
                    AP_(b, A_P, 128), b.ps[ci][:, 0:128], AF.Identity,
                    scale=b.stats[:, 58:59]), [("V", rl), pw])
                b.ps_rot.done(ctxi, "S", fs)
                ptw = b.pt_rot.war(0)
                ev = temove_batch(k, b, [lambda: AP_(b, A_P, 128)],
                                  AP_(b, A_PT, 128), [("S", fs), ptw], "V")
                b.p_rot.done(0, "T", k.cnt("T"))
                dc = k.dma("cst", lambda g, h2=h2, qt=qt: g.dma_start(
                    out=ctx_dram[128*h2:128*h2+128, 128*qt:128*qt+128],
                    in_=AP_(b, A_PT, 128)), [ev])
                b.pt_rot.done(0, "cst", dc)
                ctxi = (ctxi + 1) % 4
            b.q_war = [("T", k.cnt("T"))]
    b.ctx_done = [("cst", k.cnt("cst"))]


def emit_attn_outproj(k, b, layer, wo_src, ctx_loader, tt_done, wo_cast=None):
    ctx_ready = ctx_loader(k, b)
    rb2 = 32 * layer
    for ob in range(4):
        for quad in range(2):
            for dc in range(16):
                slot = dc % 2
                wdst = AP_(b, A_WO + 512*slot, 512)
                if layer == 0:
                    lw = k.dma("wo%d" % slot, lambda g, ob=ob, dc=dc, wdst=wdst:
                               g.dma_start(out=wdst,
                                           in_=wo_src[128*dc:128*dc+128, 512*ob:512*ob+512]),
                               [b.wo_rot.war(slot)])
                    w_ready = [("wo%d" % slot, lw)]
                else:
                    lw = k.dma("wo%d" % slot, lambda q, ob=ob, dc=dc, wdst=wdst:
                               q.dma_start_transpose(
                                   wdst, wo_src[512*ob:512*ob+512, 128*dc:128*dc+128]),
                               [b.wo_rot.war(slot), b.sbsb_guard, wo_cast], q="sync")
                    w_ready = [("wo%d" % slot, lw)]
                wts = w_ready + [ctx_ready]
                if dc == 0:
                    wts += [b.ps_rot.war(i) for i in range(4)]
                def mmo(t, ob=ob, quad=quad, dc=dc, slot=slot, rb2=rb2):
                    r = None
                    if dc == 0:
                        for ti in range(4):
                            t.matmul(b.ps[ti][:, 0:512], b.ones2[rb2:rb2+2, 0:128],
                                     b.att2[rb2:rb2+2, 512*ob:512*ob+512],
                                     start=True, stop=False)
                    for ti in range(4):
                        tt = 4*quad + ti
                        r = t.matmul(b.ps[ti][:, 0:512],
                                     AP_(b, A_CTX + 1024*dc + 128*tt, 128),
                                     AP_(b, A_WO + 512*slot, 512),
                                     start=False, stop=(dc == 15))
                    return r
                mo = k.T(mmo, wts + [b.att2_ready])
                b.wo_rot.done(slot, "T", mo)
            mm_fin = ("T", k.cnt("T"))
            for ti in range(4):
                tt = 4*quad + ti
                xs = b.x_sb[:, 2048*tt + 512*ob: 2048*tt + 512*ob + 512]
                u_d = k.V(lambda v, ti=ti, xs=xs: v.tensor_tensor(
                    xs, xs, b.ps[ti][:, 0:512], ALU.add),
                    [mm_fin, tt_done[tt]])
                tt_done[tt] = ("V", u_d)
                b.ps_rot.done(ti, "V", u_d)
    b.ctxbuf_war = [("T", k.cnt("T"))]
    return tt_done



def host_consts():
    ident = np.eye(128, dtype=BF)
    ones2 = np.ones((34, 128), dtype=BF)
    tri = np.zeros((128, 4 * 512), dtype=BF)
    for d in range(4):
        i = np.arange(128)[:, None]
        jj = np.arange(512)[None, :]
        tri[:, 512*d:512*d+512] = np.where((jj - i) <= 128 * d, 0.0, -30000.0).astype(BF)
    return dict(cn_ident=ident, cn_ones2=ones2, cn_tri=tri,
                cn_identF=np.eye(128, dtype=np.float32))


def emit_const_loads(k, b, cn):
    l1 = k.DL(lambda g: g.dma_start(out=b.ident[:, :], in_=cn.ident))
    l2 = k.DL(lambda g: g.dma_start(out=b.ones2[:, :], in_=cn.ones2))
    l3 = k.DL(lambda g: g.dma_start(out=b.tri[:, :], in_=cn.tri))
    l4 = k.DL(lambda g: g.dma_start(out=b.identF[:, :], in_=cn.identF))
    return ("DL", l4)



def build_ffn_launch(kind, with_xnT=False):
    nc = bass.Bass("TRN2", target_bir_lowering=False)
    x_in = nc.dram_tensor("x_in", [TLOC, C], F32, kind="ExternalInput").ap()
    x_out = nc.dram_tensor("x_out", [TLOC, C], F32, kind="ExternalOutput").ap()
    xnT_out = (nc.dram_tensor("xnT_out", [C, TLOC], BF16, kind="ExternalOutput").ap()
               if with_xnT else None)
    cn = NS(ident=nc.dram_tensor("cn_ident", [128, 128], BF16, kind="ExternalInput").ap(),
            ones2=nc.dram_tensor("cn_ones2", [34, 128], BF16, kind="ExternalInput").ap(),
            tri=nc.dram_tensor("cn_tri", [128, 2048], BF16, kind="ExternalInput").ap(),
            identF=nc.dram_tensor("cn_identF", [128, 128], F32, kind="ExternalInput").ap())
    if kind == "l0":
        wsrc = NS(
            gate_w=nc.dram_tensor("gate_w", [8, C], F32, kind="ExternalInput").ap(),
            sgw=nc.dram_tensor("sgw", [1, C], F32, kind="ExternalInput").ap(),
            fc1=nc.dram_tensor("fc1", [8, F_MOE, C], F32, kind="ExternalInput").ap(),
            fc2=nc.dram_tensor("fc2", [8, F_MOE, C], F32, kind="ExternalInput").ap(),
            proj=nc.dram_tensor("proj", [8, C, F_MOE], F32, kind="ExternalInput").ap(),
            sfc1=nc.dram_tensor("sfc1", [F_SH, C], F32, kind="ExternalInput").ap(),
            sfc2=nc.dram_tensor("sfc2", [F_SH, C], F32, kind="ExternalInput").ap(),
            sproj=nc.dram_tensor("sproj", [C, F_SH], F32, kind="ExternalInput").ap(),
        )
    else:
        wsrc = NS(
            gate_w=nc.dram_tensor("gate_w", [I_L1, C], I32, kind="ExternalInput").ap(),
            up_w=nc.dram_tensor("up_w", [I_L1, C], I32, kind="ExternalInput").ap(),
            down_w=nc.dram_tensor("down_w", [C, I_L1], I32, kind="ExternalInput").ap(),
            gate_scale=nc.dram_tensor("gate_scale", [I_L1], F32, kind="ExternalInput").ap(),
            up_scale=nc.dram_tensor("up_scale", [I_L1], F32, kind="ExternalInput").ap(),
            down_scale=nc.dram_tensor("down_scale", [C], F32, kind="ExternalInput").ap(),
        )
    scr = None
    if kind == "l0":
        scr = NS(fc1b=nc.dram_tensor("fc1b", [8, F_MOE, C], BF16, kind="Internal").ap(),
                 fc2b=nc.dram_tensor("fc2b", [8, F_MOE, C], BF16, kind="Internal").ap(),
                 projb=nc.dram_tensor("projb", [8, C, F_MOE], BF16, kind="Internal").ap(),
                 sfc1b=nc.dram_tensor("sfc1b", [F_SH, C], BF16, kind="Internal").ap(),
                 sfc2b=nc.dram_tensor("sfc2b", [F_SH, C], BF16, kind="Internal").ap(),
                 sprojb=nc.dram_tensor("sprojb", [C, F_SH], BF16, kind="Internal").ap())
    es = ExitStack()
    k = K(nc, es)
    b = make_bufs(k)
    emit_const_loads(k, b, cn)
    b.sbsb_guard = ("T", 0)
    if kind == "l0":
        emit_cast_l0(k, b, wsrc, scr)
    xl = k.dma("xin", lambda g: g.dma_start(
        out=b.x_sb[:, :].rearrange("p (a c) -> p a c", c=2048),
        in_=x_in.rearrange("(a p) c -> p a c", p=128)))
    tt_done = [("xin", xl)] * NTT
    for tt in range(NTT):
        emit_ln_tile(k, b, tt, [("xin", xl)])
    b.xnT_done = [("V", k.cnt("V")), ("S", k.cnt("S"))]
    tt_done = emit_ffn(k, b, kind, wsrc, tt_done, scr=scr)
    st_last = k.dma("xout", lambda g: g.dma_start(
        out=x_out.rearrange("(a p) c -> p a c", p=128),
        in_=b.x_sb[:, :].rearrange("p (a c) -> p a c", c=2048)), tt_done)
    if with_xnT:
        tguard = ("T", k.cnt("T"))
        for tt in range(NTT):
            emit_ln_tile(k, b, tt, [tt_done[tt], tguard])
        xnT_done = [("V", k.cnt("V")), ("S", k.cnt("S"))]
        st2 = k.dma("xno", lambda g: g.dma_start(
            out=xnT_out.rearrange("(a p) w -> p a w", p=128),
            in_=b.arena[:, A_XNT:A_XNT+16384].rearrange("p (a w) -> p a w", w=1024)),
            xnT_done)
        k.waiter("G", [("xout", st_last), ("xno", st2)])
    else:
        k.waiter("G", [("xout", st_last)])
    k.finalize()
    es.close()
    return nc


def build_attn_core_launch(layer):
    nc = bass.Bass("TRN2", target_bir_lowering=False)
    xnT_full = nc.dram_tensor("xnT_full", [C, SEQ], BF16, kind="ExternalInput").ap()
    brows = nc.dram_tensor("cn_brows", [2, 2, SEQ], BF16, kind="ExternalInput").ap()
    ctx_out = nc.dram_tensor("ctx_out", [2 * D, SEQ], BF16, kind="ExternalOutput").ap()
    cn = NS(ident=nc.dram_tensor("cn_ident", [128, 128], BF16, kind="ExternalInput").ap(),
            ones2=nc.dram_tensor("cn_ones2", [34, 128], BF16, kind="ExternalInput").ap(),
            tri=nc.dram_tensor("cn_tri", [128, 2048], BF16, kind="ExternalInput").ap(),
            identF=nc.dram_tensor("cn_identF", [128, 128], F32, kind="ExternalInput").ap())
    if layer == 0:
        wsrc = NS(wq_s=nc.dram_tensor("wq_s", [C, 256], F32, kind="ExternalInput").ap(),
                  wk_s=nc.dram_tensor("wk_s", [C, 256], F32, kind="ExternalInput").ap(),
                  wv_s=nc.dram_tensor("wv_s", [C, 256], F32, kind="ExternalInput").ap())
    else:
        wsrc = NS(wqkv_s=nc.dram_tensor("wqkv_s", [768, C], F32, kind="ExternalInput").ap(),
                  bqkv_s=nc.dram_tensor("bqkv_s", [768], F32, kind="ExternalInput").ap())
    q_dram = nc.dram_tensor("q_dram", [2 * D, SEQ], BF16, kind="Internal").ap()
    scr = NS(wqkvb=nc.dram_tensor("wqkvb", [768, C], BF16, kind="Internal").ap()) \
        if layer == 1 else None
    es = ExitStack()
    k = K(nc, es)
    b = make_bufs(k)
    emit_const_loads(k, b, cn)
    b.sbsb_guard = ("T", 0)
    emit_attn_wprep(k, b, layer, wsrc, scr)
    slab_src = lambda cc, tb, th: xnT_full[128*cc:128*cc+128,
                                           1024*tb+512*th:1024*tb+512*th+512]
    emit_attn_qkv(k, b, layer, slab_src, q_dram)
    emit_attn_core(k, b, layer, [brows[0], brows[1]], q_dram, ctx_out)
    k.waiter("G", [b.ctx_done])
    es.close()
    return nc


def build_outproj_launch(layer):
    nc = bass.Bass("TRN2", target_bir_lowering=False)
    x_in = nc.dram_tensor("x_in", [TLOC, C], F32, kind="ExternalInput").ap()
    x_out = nc.dram_tensor("x_out", [TLOC, C], F32, kind="ExternalOutput").ap()
    ctx_own = nc.dram_tensor("ctx_own", [C, TLOC], BF16, kind="ExternalInput").ap()
    wo = nc.dram_tensor("wo", [C, C], F32, kind="ExternalInput").ap()
    bias2 = nc.dram_tensor("bias2", [1, C], F32, kind="ExternalInput").ap()
    cn = NS(ident=nc.dram_tensor("cn_ident", [128, 128], BF16, kind="ExternalInput").ap(),
            ones2=nc.dram_tensor("cn_ones2", [34, 128], BF16, kind="ExternalInput").ap(),
            tri=nc.dram_tensor("cn_tri", [128, 2048], BF16, kind="ExternalInput").ap(),
            identF=nc.dram_tensor("cn_identF", [128, 128], F32, kind="ExternalInput").ap())
    woutb = (nc.dram_tensor("woutb", [C, C], BF16, kind="Internal").ap()
             if layer == 1 else None)
    es = ExitStack()
    k = K(nc, es)
    b = make_bufs(k)
    emit_const_loads(k, b, cn)
    wo_cast = None
    if layer == 1:
        cwo = k.DL(lambda g: g.dma_start(out=woutb, in_=wo), [])
        wo_cast = ("DL", cwo)
    emit_att2_build(k, b, bias2, 32 * layer)
    b.att2_ready = b.att2_guard
    b.sbsb_guard = b.att2_guard
    xl = k.dma("xin", lambda g: g.dma_start(
        out=b.x_sb[:, :].rearrange("p (a c) -> p a c", c=2048),
        in_=x_in.rearrange("(a p) c -> p a c", p=128)), [b.att2_guard])
    tt_done = [("xin", xl)] * NTT
    def ctx_loader(k2, b2):
        ld = k2.dma("ctx", lambda g: g.dma_start(
            out=b2.arena[:, A_CTX:A_CTX+16384].rearrange("p (a w) -> p a w", w=1024),
            in_=ctx_own.rearrange("(a p) w -> p a w", p=128)), [b2.ctxbuf_war])
        return ("ctx", ld)
    tt_done = emit_attn_outproj(k, b, layer, woutb if layer == 1 else wo,
                                ctx_loader, tt_done, wo_cast=wo_cast)
    st_last = k.dma("xout", lambda g: g.dma_start(
        out=x_out.rearrange("(a p) c -> p a c", p=128),
        in_=b.x_sb[:, :].rearrange("p (a c) -> p a c", c=2048)), tt_done)
    k.waiter("G", [("xout", st_last)])
    k.finalize()
    es.close()
    return nc


def build_fused(n_sub=6):
    nc = bass.Bass("TRN2", target_bir_lowering=False)
    RG = [list(range(NCORES))]
    x_in = nc.dram_tensor("x_in", [TLOC, C], F32, kind="ExternalInput").ap()
    x_out = nc.dram_tensor("x_out", [TLOC, C], F32, kind="ExternalOutput").ap()
    cn = NS(ident=nc.dram_tensor("cn_ident", [128, 128], BF16, kind="ExternalInput").ap(),
            ones2=nc.dram_tensor("cn_ones2", [34, 128], BF16, kind="ExternalInput").ap(),
            tri=nc.dram_tensor("cn_tri", [128, 2048], BF16, kind="ExternalInput").ap(),
            identF=nc.dram_tensor("cn_identF", [128, 128], F32, kind="ExternalInput").ap())
    brows = nc.dram_tensor("cn_brows", [2, 2, 2, SEQ], BF16, kind="ExternalInput").ap()
    w0 = NS(gate_w=nc.dram_tensor("gate_w", [8, C], F32, kind="ExternalInput").ap(),
            sgw=nc.dram_tensor("sgw", [1, C], F32, kind="ExternalInput").ap(),
            fc1=nc.dram_tensor("fc1", [8, F_MOE, C], F32, kind="ExternalInput").ap(),
            fc2=nc.dram_tensor("fc2", [8, F_MOE, C], F32, kind="ExternalInput").ap(),
            proj=nc.dram_tensor("proj", [8, C, F_MOE], F32, kind="ExternalInput").ap(),
            sfc1=nc.dram_tensor("sfc1", [F_SH, C], F32, kind="ExternalInput").ap(),
            sfc2=nc.dram_tensor("sfc2", [F_SH, C], F32, kind="ExternalInput").ap(),
            sproj=nc.dram_tensor("sproj", [C, F_SH], F32, kind="ExternalInput").ap())
    w1 = NS(gate_w=nc.dram_tensor("l1_gate_w", [I_L1, C], I32, kind="ExternalInput").ap(),
            up_w=nc.dram_tensor("l1_up_w", [I_L1, C], I32, kind="ExternalInput").ap(),
            down_w=nc.dram_tensor("l1_down_w", [C, I_L1], I32, kind="ExternalInput").ap(),
            gate_scale=nc.dram_tensor("l1_gate_scale", [I_L1], F32, kind="ExternalInput").ap(),
            up_scale=nc.dram_tensor("l1_up_scale", [I_L1], F32, kind="ExternalInput").ap(),
            down_scale=nc.dram_tensor("l1_down_scale", [C], F32, kind="ExternalInput").ap())
    wa0 = NS(wq_s=nc.dram_tensor("wq_s", [C, 256], F32, kind="ExternalInput").ap(),
             wk_s=nc.dram_tensor("wk_s", [C, 256], F32, kind="ExternalInput").ap(),
             wv_s=nc.dram_tensor("wv_s", [C, 256], F32, kind="ExternalInput").ap())
    wa1 = NS(wqkv_s=nc.dram_tensor("wqkv_s", [768, C], F32, kind="ExternalInput").ap(),
             bqkv_s=nc.dram_tensor("bqkv_s", [768], F32, kind="ExternalInput").ap())
    wo0 = nc.dram_tensor("wo0", [C, C], F32, kind="ExternalInput").ap()
    wo1 = nc.dram_tensor("wo1", [C, C], F32, kind="ExternalInput").ap()
    bo0 = nc.dram_tensor("bo0", [1, C], F32, kind="ExternalInput").ap()
    bo1 = nc.dram_tensor("bo1", [1, C], F32, kind="ExternalInput").ap()

    scr = NS(fc1b=nc.dram_tensor("fc1b", [8, F_MOE, C], BF16, kind="Internal").ap(),
             fc2b=nc.dram_tensor("fc2b", [8, F_MOE, C], BF16, kind="Internal").ap(),
             projb=nc.dram_tensor("projb", [8, C, F_MOE], BF16, kind="Internal").ap(),
             sfc1b=nc.dram_tensor("sfc1b", [F_SH, C], BF16, kind="Internal").ap(),
             sfc2b=nc.dram_tensor("sfc2b", [F_SH, C], BF16, kind="Internal").ap(),
             sprojb=nc.dram_tensor("sprojb", [C, F_SH], BF16, kind="Internal").ap(),
             gateb=nc.dram_tensor("gateb", [I_L1, C], BF16, kind="Internal").ap(),
             upb=nc.dram_tensor("upb", [I_L1, C], BF16, kind="Internal").ap(),
             downb=nc.dram_tensor("downb", [C, I_L1], BF16, kind="Internal").ap(),
             wqkvb=nc.dram_tensor("wqkvb", [768, C], BF16, kind="Internal").ap(),
             woutb=nc.dram_tensor("woutb", [C, C], BF16, kind="Internal").ap())
    xn_stage = nc.dram_tensor("xn_stage", [C, TLOC], BF16, kind="Internal")
    xnT_all = nc.dram_tensor("xnT_all", [NCORES * C, TLOC], BF16, kind="Internal")
    ctx_stage = nc.dram_tensor("ctx_stage", [2 * D, SEQ], BF16, kind="Internal")
    ctx_all = nc.dram_tensor("ctx_all", [C, SEQ], BF16, kind="Internal")
    q_dram = nc.dram_tensor("q_dram", [2 * D, SEQ], BF16, kind="Internal").ap()

    es = ExitStack()
    k = K(nc, es)
    b = make_bufs(k)
    emit_const_loads(k, b, cn)
    b.pid_ld = k.dma("cn", lambda g: g.dma_start(out=b.pid_sb[0:1, 0:1],
                                                 in_=nc.partition_id_tensor.ap()))
    emit_cast_l0(k, b, w0, scr)
    cwo = k.DL(lambda g: g.dma_start(out=scr.woutb, in_=wo1), [])
    b.castdone["wout"] = ("DL", cwo)
    emit_att2_build(k, b, bo0, 0)
    emit_att2_build(k, b, bo1, 32)
    b.att2_ready = b.att2_guard
    b.sbsb_guard = b.att2_guard
    xl = k.dma("xin", lambda g: g.dma_start(
        out=b.x_sb[:, :].rearrange("p (a c) -> p a c", c=2048),
        in_=x_in.rearrange("(a p) c -> p a c", p=128)), b.att2_guard)
    tt_done = [("xin", xl)] * NTT

    def ln_phase(pg):
        emit_ln_all(k, b, [[tt_done[tt]] + pg for tt in range(NTT)])

    def attn_sublayer(layer, wsrc_a, wo_src):
        pg = phase_boundary(k, b)
        emit_attn_wprep(k, b, layer, wsrc_a, scr)
        ln_phase(pg)
        st = k.dma("xns", lambda g: g.dma_start(
            out=xn_stage.ap().rearrange("(a p) w -> p a w", p=128),
            in_=b.arena[:, A_XNT:A_XNT+16384].rearrange("p (a w) -> p a w", w=1024)),
            b.xnT_done + pg)
        ag1 = k.CC(lambda g: g.collective_compute(
            "AllGather", ALU.bypass, RG, ins=[xn_stage.ap().opt()],
            outs=[xnT_all.ap().opt()]), [("xns", st)] + pg)
        xnT_all_ap = xnT_all.ap()
        slab_src = lambda cc, tb, th: xnT_all_ap[
            2048*tb + 128*cc: 2048*tb + 128*cc + 128,
            512*th: 512*th + 512]
        b.ag_guard = ("CC", k.cnt("CC"))
        pg2 = phase_boundary(k, b)
        emit_attn_qkv(k, b, layer, slab_src, q_dram)
        emit_attn_core(k, b, layer, [brows[layer, 0], brows[layer, 1]],
                       q_dram, ctx_stage.ap())
        pgc = phase_boundary(k, b)
        ag2 = k.CC(lambda g: g.collective_compute(
            "AllGather", ALU.bypass, RG, ins=[ctx_stage.ap().opt()],
            outs=[ctx_all.ap().opt()]), [b.ctx_done] + pgc)
        b.ctxbuf_war = pgc + [("CC", k.cnt("CC"))]
        ctx_all_ap = ctx_all.ap()
        def ctx_loader(k2, b2):
            ent = k2.sem_ent("ctx")
            acc = {}
            for (sn, v) in k2._flat([b2.ctxbuf_war, ("cn", b2.pid_ld)]):
                if v > 0 and v > acc.get(sn, 0):
                    acc[sn] = v
            wl = sorted(acc.items())

            def f(g):
                with g.register() as r:
                    g.reg_load(r, b2.pid_sb[0:1, 0:1])
                    g.reg_mul(r, r, TLOC)
                    off = g.snap(r, min_val=0, max_val=TLOC * (NCORES - 1))
                    src_ap = ctx_all_ap[:, bass.ds(off, TLOC)].rearrange(
                        "(a p) w -> p a w", p=128)
                    dst_ap = b2.arena[:, A_CTX:A_CTX + 16384].rearrange(
                        "p (a w) -> p a w", w=1024)
                    g.dma_start(out=dst_ap, in_=src_ap).then_inc(ent[0], 16)
            k2.dq["gpsimd"].append((wl, f, None, 0, ent[1] + 16))
            ent[1] += 16
            return ("ctx", ent[1])
        emit_attn_outproj(k, b, layer, wo_src, ctx_loader, tt_done,
                          wo_cast=(b.castdone["wout"] if layer == 1 else None))
        phase_boundary(k, b)

    def ffn_sub(kind, w, l1s=False):
        pg = phase_boundary(k, b)
        ln_phase(pg)
        emit_ffn(k, b, kind, w, tt_done, scr=scr, l1_from_scratch=l1s)
    subs = [lambda: ffn_sub("l0", w0),
            lambda: attn_sublayer(0, wa0, wo0),
            lambda: ffn_sub("l0", w0),
            lambda: ffn_sub("l1", w1),
            lambda: attn_sublayer(1, wa1, scr.woutb),
            lambda: ffn_sub("l1", w1, l1s=True)]
    for f in subs[:n_sub]:
        f()

    st_last = k.dma("xout", lambda g: g.dma_start(
        out=x_out.rearrange("(a p) c -> p a c", p=128),
        in_=b.x_sb[:, :].rearrange("p (a c) -> p a c", c=2048)), tt_done)
    k.waiter("G", [("xout", st_last)])
    k.finalize()
    es.close()
    return nc



_L0_KEYS = dict(gate_w="L0_ffn_gate_w", sgw="L0_ffn_shared_gate_w",
                fc1="L0_ffn_expert_fc1", fc2="L0_ffn_expert_fc2",
                proj="L0_ffn_expert_proj", sfc1="L0_ffn_shared_fc1",
                sfc2="L0_ffn_shared_fc2", sproj="L0_ffn_shared_proj")
_L1_KEYS = dict(gate_w="L1_ffn_gate_w", up_w="L1_ffn_up_w", down_w="L1_ffn_down_w",
                gate_scale="L1_ffn_gate_scale", up_scale="L1_ffn_up_scale",
                down_scale="L1_ffn_down_scale")

_CACHE = {}


def _get(name, builder):
    if name not in _CACHE:
        _CACHE[name] = builder()
    return _CACHE[name]


def host_bias_rows(layer, h):
    j = np.arange(SEQ, dtype=np.float64)
    slope = 2.0 ** (-(h + 1) / 2.0)
    if layer == 0:
        row = slope * j * (1.0 / np.sqrt(np.float64(D)))
    else:
        row = (j - (SEQ - 1)) * slope
    hi = row.astype(np.float32).astype(BF)
    lo = (row - hi.astype(np.float64)).astype(np.float32).astype(BF)
    return np.stack([hi, lo])


def run_ffn_step(kind, x_full, inputs, with_xnT=False):
    nc = _get(("ffn", kind, with_xnT), lambda: build_ffn_launch(kind, with_xnT))
    cns = host_consts()
    keys = _L0_KEYS if kind == "l0" else _L1_KEYS
    wmap = {k2: np.ascontiguousarray(np.asarray(inputs[v])) for k2, v in keys.items()}
    if kind == "l0":
        wmap["sgw"] = wmap["sgw"].reshape(1, C)
    in_maps = [dict(x_in=np.ascontiguousarray(x_full[TLOC*c:TLOC*(c+1)]), **wmap, **cns)
               for c in range(NCORES)]
    res = run_bass_kernel_spmd(nc, in_maps, core_ids=list(range(NCORES)))
    x_new = np.concatenate([res.results[c]["x_out"] for c in range(NCORES)], 0)
    if with_xnT:
        xnT = np.concatenate([res.results[c]["xnT_out"] for c in range(NCORES)], 1)
        return x_new, xnT
    return x_new


def run_attn_step(layer, x_full, xnT_full, inputs):
    nc_a = _get(("attn", layer), lambda: build_attn_core_launch(layer))
    cns = host_consts()
    in_maps = []
    for c in range(NCORES):
        m = dict(xnT_full=np.ascontiguousarray(xnT_full), **cns)
        m["cn_brows"] = np.ascontiguousarray(
            np.stack([host_bias_rows(layer, 2*c), host_bias_rows(layer, 2*c+1)]))
        if layer == 0:
            for nm, key in (("wq_s", "L0_attn_Wq"), ("wk_s", "L0_attn_Wk"),
                            ("wv_s", "L0_attn_Wv")):
                m[nm] = np.ascontiguousarray(np.asarray(inputs[key])[:, 256*c:256*c+256])
        else:
            Wqkv = np.asarray(inputs["L1_attn_Wqkv"])
            bq = np.asarray(inputs["L1_attn_b_qkv"])
            rows = np.concatenate([Wqkv[256*c:256*c+256],
                                   Wqkv[2048+256*c:2048+256*c+256],
                                   Wqkv[4096+256*c:4096+256*c+256]], 0)
            m["wqkv_s"] = np.ascontiguousarray(rows)
            m["bqkv_s"] = np.ascontiguousarray(np.concatenate(
                [bq[256*c:256*c+256], bq[2048+256*c:2048+256*c+256],
                 bq[4096+256*c:4096+256*c+256]]))
        in_maps.append(m)
    res = run_bass_kernel_spmd(nc_a, in_maps, core_ids=list(range(NCORES)))
    ctx_full = np.concatenate([res.results[c]["ctx_out"] for c in range(NCORES)], 0)

    nc_o = _get(("outproj", layer), lambda: build_outproj_launch(layer))
    wo = np.ascontiguousarray(np.asarray(
        inputs["L0_attn_Wo" if layer == 0 else "L1_attn_Wout"]))
    b2 = np.ascontiguousarray(np.asarray(
        inputs["L0_attn_bo" if layer == 0 else "L1_attn_b_out"]).reshape(1, C))
    in_maps = [dict(x_in=np.ascontiguousarray(x_full[TLOC*c:TLOC*(c+1)]),
                    ctx_own=np.ascontiguousarray(ctx_full[:, TLOC*c:TLOC*(c+1)]),
                    wo=wo, bias2=b2, **cns)
               for c in range(NCORES)]
    res = run_bass_kernel_spmd(nc_o, in_maps, core_ids=list(range(NCORES)))
    return np.concatenate([res.results[c]["x_out"] for c in range(NCORES)], 0)


def kernel_unfused(**inputs):
    x = np.ascontiguousarray(np.asarray(inputs["L0_attn_x"], np.float32).reshape(SEQ, C))
    x, xnT = run_ffn_step("l0", x, inputs, with_xnT=True)
    x = run_attn_step(0, x, xnT, inputs)
    x = run_ffn_step("l0", x, inputs)
    x, xnT = run_ffn_step("l1", x, inputs, with_xnT=True)
    x = run_attn_step(1, x, xnT, inputs)
    x = run_ffn_step("l1", x, inputs)
    return x.reshape(1, SEQ, C)


def kernel(**inputs, ):
    n_sub = int(inputs.pop("_n_sub", 6))
    x = np.ascontiguousarray(np.asarray(inputs["L0_attn_x"], np.float32).reshape(SEQ, C))
    nc = _get(("fused", n_sub), lambda: build_fused(n_sub))
    cns = host_consts()
    wmap = {}
    for k2, v in _L0_KEYS.items():
        wmap[k2] = np.ascontiguousarray(np.asarray(inputs[v]))
    wmap["sgw"] = wmap["sgw"].reshape(1, C)
    for k2, v in _L1_KEYS.items():
        wmap["l1_" + k2] = np.ascontiguousarray(np.asarray(inputs[v]))
    wmap["wo0"] = np.ascontiguousarray(np.asarray(inputs["L0_attn_Wo"]))
    wmap["wo1"] = np.ascontiguousarray(np.asarray(inputs["L1_attn_Wout"]))
    wmap["bo0"] = np.ascontiguousarray(np.asarray(inputs["L0_attn_bo"]).reshape(1, C))
    wmap["bo1"] = np.ascontiguousarray(np.asarray(inputs["L1_attn_b_out"]).reshape(1, C))
    Wqkv = np.asarray(inputs["L1_attn_Wqkv"])
    bq = np.asarray(inputs["L1_attn_b_qkv"])
    in_maps = []
    for c in range(NCORES):
        m = dict(x_in=np.ascontiguousarray(x[TLOC*c:TLOC*(c+1)]), **wmap, **cns)
        m["cn_brows"] = np.ascontiguousarray(np.stack([
            np.stack([host_bias_rows(0, 2*c), host_bias_rows(0, 2*c+1)]),
            np.stack([host_bias_rows(1, 2*c), host_bias_rows(1, 2*c+1)])]))
        for nm, key in (("wq_s", "L0_attn_Wq"), ("wk_s", "L0_attn_Wk"),
                        ("wv_s", "L0_attn_Wv")):
            m[nm] = np.ascontiguousarray(np.asarray(inputs[key])[:, 256*c:256*c+256])
        m["wqkv_s"] = np.ascontiguousarray(np.concatenate(
            [Wqkv[256*c:256*c+256], Wqkv[2048+256*c:2048+256*c+256],
             Wqkv[4096+256*c:4096+256*c+256]], 0))
        m["bqkv_s"] = np.ascontiguousarray(np.concatenate(
            [bq[256*c:256*c+256], bq[2048+256*c:2048+256*c+256],
             bq[4096+256*c:4096+256*c+256]]))
        in_maps.append(m)
    res = run_bass_kernel_spmd(nc, in_maps, core_ids=list(range(NCORES)))
    out = np.concatenate([res.results[c]["x_out"] for c in range(NCORES)], 0)
    return out.reshape(1, SEQ, C)


# version 06
# speedup vs baseline: 1.0512x; 1.0512x over previous
import numpy as np
import ml_dtypes
from contextlib import ExitStack

import concourse.bass as bass
import concourse.mybir as mybir
from concourse.bass_utils import run_bass_kernel_spmd

F32 = mybir.dt.float32
BF16 = mybir.dt.bfloat16
I32 = mybir.dt.int32
AF = mybir.ActivationFunctionType
ALU = mybir.AluOpType
AX = mybir.AxisListType

SEQ = 8192
C = 2048
NHEAD = 16
D = 128
NCORES = 8
TLOC = SEQ // NCORES
NTT = TLOC // 128
F_MOE = 1408
F_SH = 5632
I_L1 = 8192
EPS = 1e-5
RSQD = float(1.0 / np.sqrt(np.float64(D)))
BF = ml_dtypes.bfloat16

A_XNT = 0
A_WG = 16384
A_WU = 24576
A_H = 32768
A_PJ = 44032
A_SU = 49664
A_RW = 50176
A_SGW = 50304
A_WN0 = 50320
A_Q = 0
A_K = 4096
A_V = 20480
A_WQKV = 37120
A_SLAB = 49408
A_P = 50432
A_PT = 51456
A_WST = 50432
A_CTX = 0
A_WON = 16384
A_WO = 24576

VW = 130
VH = 64 * VW


class K:

    ENGS = ("tensor", "vector", "scalar", "gpsimd", "sync")

    def __init__(self, nc, es):
        self.nc = nc
        self.es = es
        self.blk = es.enter_context(nc.Block())
        self.sems = {}
        for n in ("T", "V", "S", "G", "DL", "DQ", "CC"):
            self.sems[n] = [es.enter_context(nc.semaphore(n)), 0]
        self.dq = {e: [] for e in self.ENGS}

    def cnt(self, name):
        return self.sems[name][1]

    @staticmethod
    def _flat(waits):
        out = []
        for w in waits:
            if w is None:
                continue
            if isinstance(w, list):
                out.extend(t for t in w if t is not None)
            else:
                out.append(w)
        return out

    def _collapse(self, waits):
        acc = {}
        for (sn, v) in self._flat(waits):
            if v > 0 and v > acc.get(sn, 0):
                acc[sn] = v
        return sorted(acc.items())

    def sem_ent(self, name):
        if name not in self.sems:
            self.sems[name] = [self.es.enter_context(self.nc.semaphore(name)), 0]
        return self.sems[name]

    def _emit(self, eng, fn, waits, sem, n):
        ent = self.sem_ent(sem)
        selfwait = ent[1] + n if sem in ("DL", "DQ") else None
        self.dq[eng].append((self._collapse(waits), fn, ent[0], n, selfwait))
        ent[1] += n
        return ent[1]

    def dma(self, ch, fn, waits=(), q="gpsimd"):
        ent = self.sem_ent(ch)
        self.dq[q].append((self._collapse(waits), fn, ent[0], 16, None))
        ent[1] += 16
        return ent[1]

    def T(self, fn, waits=(), n=1):   return self._emit("tensor", fn, waits, "T", n)
    def V(self, fn, waits=(), n=1):   return self._emit("vector", fn, waits, "V", n)
    def S(self, fn, waits=(), n=1):   return self._emit("scalar", fn, waits, "S", n)
    def G(self, fn, waits=(), n=1):   return self._emit("gpsimd", fn, waits, "G", n)
    def DL(self, fn, waits=(), n=16): return self._emit("gpsimd", fn, waits, "DL", n)
    def DQ(self, fn, waits=(), n=16): return self._emit("sync", fn, waits, "DQ", n)
    def CC(self, fn, waits=(), n=1):  return self._emit("gpsimd", fn, waits, "CC", n)

    def DLmulti(self, fns, waits=()):
        ent = self.sems["DL"]
        semh = ent[0]

        total = ent[1] + 16 * len(fns)

        def f(g):
            for fn in fns:
                fn(g).then_inc(semh, 16)
        self.dq["gpsimd"].append((self._collapse(waits), f, None, 0, total))
        ent[1] = total
        return ent[1]

    ENG = dict(T="tensor", V="vector", S="scalar", G="gpsimd", Q="sync")

    def waiter(self, eng, waits):
        self.dq[self.ENG[eng]].append((self._collapse(waits), None, None, 0, None))

    def finalize(self):
        hmap = self.sems
        for eng in self.ENGS:
            items = self.dq[eng]
            if not items:
                continue

            def body(e, items=items):
                for (wl, fn, semh, n, selfwait) in items:
                    for (sn, v) in wl:
                        e.wait_ge(hmap[sn][0], v)
                    if fn is None:
                        continue
                    r = fn(e)
                    if r is not None and semh is not None:
                        r.then_inc(semh, n)
                    if selfwait is not None:
                        sw = hmap["DL"] if (semh is hmap["DL"][0] or semh is None) else hmap["DQ"]
                        e.wait_ge(sw[0], selfwait)
            getattr(self.blk, eng)(body)


class Rot:
    def __init__(self, n):
        self.n = n
        self.free = [[("T", 0)]] * n

    def war(self, i):
        return self.free[i % self.n]

    def done(self, i, sem, val):
        self.free[i % self.n] = [(sem, val)]

    def set_all(self, waits):
        self.free = [list(waits)] * self.n


class NS:
    def __init__(self, **kw):
        self.__dict__.update(kw)


def make_bufs(k):
    nc, es = k.nc, k.es
    b = NS(
        arena=es.enter_context(nc.sbuf_tensor("arena", [128, 53248], BF16)),
        x_sb=es.enter_context(nc.sbuf_tensor("x_sb", [128, 8 * 2048], F32)),
        xn_t=es.enter_context(nc.sbuf_tensor("xn_t", [128, 2048], BF16)),
        sq_t=es.enter_context(nc.sbuf_tensor("sq_t", [128, 2048], F32)),
        stats=es.enter_context(nc.sbuf_tensor("stats", [128, 64], F32)),
        comb=es.enter_context(nc.sbuf_tensor("comb", [128, 80], F32)),
        scl=es.enter_context(nc.sbuf_tensor("scl", [128, 304], F32)),
        bias_sb=es.enter_context(nc.sbuf_tensor("bias_sb", [128, 8192], BF16)),
        att2=es.enter_context(nc.sbuf_tensor("att2", [128, 2048], BF16)),
        ident=es.enter_context(nc.sbuf_tensor("ident", [128, 128], BF16)),
        identF=es.enter_context(nc.sbuf_tensor("identF", [128, 128], F32)),
        ones2=es.enter_context(nc.sbuf_tensor("ones2", [34, 128], BF16)),
        tri=es.enter_context(nc.sbuf_tensor("tri", [128, 4 * 512], BF16)),
        ps=[es.enter_context(nc.psum_tensor("ps%d" % i, [128, 512], F32)) for i in range(4)],
        psE=es.enter_context(nc.psum_tensor("psE", [128, 512], F32)),
        psF=es.enter_context(nc.psum_tensor("psF", [128, 512], F32)),
        psT=es.enter_context(nc.psum_tensor("psT", [128, 2048], BF16)),
        pid_sb=es.enter_context(nc.sbuf_tensor("pid_sb", [1, 2], mybir.dt.uint32)),
    )
    b.psTb = Rot(2)
    b.ps_rot = Rot(4)
    b.psE_rot = Rot(1)
    b.psF_rot = Rot(1)
    b.wg_rot = Rot(2)
    b.wu_rot = Rot(2)
    b.wnat_rot = Rot(2)
    b.pjnat_rot = Rot(2)
    b.pj_war = ("T", 0)
    b.h_war = ("T", 0)
    b.su_war = ("V", 0)
    b.sq_war = ("S", 0)
    b.xn_war = ("T", 0)
    b.xnT_done = ("V", 0)
    b.psT_bid = 0
    b.slab_rot = Rot(2)
    b.p_rot = Rot(2)
    b.pt_rot = Rot(2)
    b.wo_rot = Rot(2)
    b.q_war = ("T", 0)
    b.wst_war = ("T", 0)
    b.won_war = ("T", 0)
    b.ctxbuf_war = ("T", 0)
    b.vones_guard = ("T", 0)
    b.bias_guard = ("T", 0)
    b.att2_guard = ("T", 0)
    b.att2_ready = ("T", 0)
    b.wqkv_ready = []
    b.qkvbias_ready = ("V", 0)
    b.kv_ready = []
    b.ag_guard = None
    b.castdone = {}
    b.sbsb_guard = None
    return b


def phase_boundary(k, b):
    pg = [(s, ent[1]) for s, ent in k.sems.items()]
    for r in (b.psTb, b.ps_rot, b.psE_rot, b.psF_rot, b.wg_rot, b.wu_rot,
              b.wnat_rot, b.pjnat_rot, b.slab_rot, b.p_rot, b.pt_rot, b.wo_rot):
        r.set_all(pg)
    for name in ("pj_war", "h_war", "su_war", "sq_war", "xn_war", "q_war",
                 "wst_war", "won_war", "ctxbuf_war", "vones_guard", "bias_guard"):
        setattr(b, name, list(pg))
    return pg


def AP_(b, c0, w):
    return b.arena[:, c0:c0 + w]



def temove_batch(k, b, srcs, dst_ap, waits, eng, width=128, in_parts=128):
    n = len(srcs)
    pad = width if width % 2 == 0 else width + 1
    assert n * pad <= 1024
    bid = b.psT_bid % 2
    b.psT_bid += 1
    war = b.psTb.war(bid)
    base = 1024 * bid
    t_d = 0
    for i, sfn in enumerate(srcs):
        t_d = k.T(lambda t, i=i, sfn=sfn: t.transpose(
            b.psT[:, base + pad * i: base + pad * i + width],
            sfn(), b.ident[0:in_parts, 0:in_parts]),
            (list(waits) + [war]) if i == 0 else [])
    if pad == width:
        src_ap = b.psT[:, base:base + n * width]
    else:
        src_ap = b.psT[:, base:base + n * pad].rearrange(
            "p (a w) -> p a w", w=pad)[:, :, 0:width]
    if eng == "V":
        e_d = k.V(lambda v: v.tensor_copy(dst_ap, src_ap), [("T", t_d)])
    else:
        e_d = k.S(lambda s: s.copy(dst_ap, src_ap), [("T", t_d)])
    b.psTb.done(bid, eng, e_d)
    return (eng, e_d)



def emit_ln_tile(k, b, tt, x_guards):
    st = b.stats
    SUM, MEAN, M2, E2, VAR, RSTD, NEGB = 0, 1, 2, 3, 4, 5, 6
    xs = b.x_sb[:, 2048 * tt: 2048 * tt + 2048]

    v_sum = k.V(lambda v: v.reduce_sum(st[:, SUM:SUM+1], xs, axis=AX.X), x_guards)
    s_sq = k.S(lambda s: s.activation(b.sq_t[:, :], xs, AF.Square,
                                      accum_out=st[:, M2:M2+1]),
               list(x_guards) + [b.sq_war])
    b.sq_war = [("S", s_sq)]
    v1 = k.V(lambda v: v.tensor_scalar(st[:, MEAN:MEAN+1], st[:, SUM:SUM+1],
                                       1.0 / 2048.0, None, ALU.mult), [("V", v_sum)])
    v2 = k.V(lambda v: v.tensor_scalar(st[:, E2:E2+1], st[:, M2:M2+1],
                                       1.0 / 2048.0, None, ALU.mult),
             [("S", s_sq), ("V", v1)])
    v3 = k.V(lambda v: v.tensor_tensor(st[:, VAR:VAR+1], st[:, MEAN:MEAN+1],
                                       st[:, MEAN:MEAN+1], ALU.mult), [("V", v2)])
    v4 = k.V(lambda v: v.tensor_tensor(st[:, VAR:VAR+1], st[:, E2:E2+1],
                                       st[:, VAR:VAR+1], ALU.subtract), [("V", v3)])
    v4b = k.V(lambda v: v.tensor_scalar(st[:, VAR:VAR+1], st[:, VAR:VAR+1],
                                        float(EPS), None, ALU.add), [("V", v4)])
    s_sd = k.S(lambda s: s.activation(st[:, RSTD:RSTD+1], st[:, VAR:VAR+1],
                                      AF.Sqrt), [("V", v4b)])
    v5 = k.V(lambda v: v.reciprocal(st[:, RSTD:RSTD+1], st[:, RSTD:RSTD+1]),
             [("S", s_sd)])
    v6 = k.V(lambda v: v.tensor_tensor(st[:, NEGB:NEGB+1], st[:, MEAN:MEAN+1],
                                       st[:, RSTD:RSTD+1], ALU.mult), [("V", v5)])
    v7 = k.V(lambda v: v.tensor_scalar(st[:, NEGB:NEGB+1], st[:, NEGB:NEGB+1],
                                       -1.0, None, ALU.mult), [("V", v6)])
    s_xn = k.S(lambda s: s.activation(b.xn_t[:, :], xs, AF.Identity,
                                      bias=st[:, NEGB:NEGB+1],
                                      scale=st[:, RSTD:RSTD+1]),
               [("V", v7), b.xn_war])
    last = ("V", 0)
    for half in range(2):
        c0 = 8 * half
        base = A_XNT + 1024 * c0 + 128 * tt
        dst = b.arena[:, base:base + 8192].rearrange(
            "p (a w) -> p a w", w=1024)[:, :, 0:128]
        srcs = [(lambda cc=cc: b.xn_t[:, 128 * cc:128 * cc + 128])
                for cc in range(c0, c0 + 8)]
        last = temove_batch(k, b, srcs, dst, [("S", s_xn)], "V" if half == 0 else "S")
    b.xn_war = [("T", k.cnt("T"))]
    return last


def emit_ln_all(k, b, x_guards_per_tt):
    evs = []
    for tt in range(NTT):
        evs.append(emit_ln_tile(k, b, tt, x_guards_per_tt[tt]))
    b.xnT_done = [("V", k.cnt("V")), ("S", k.cnt("S"))]
    return b.xnT_done



def _wprep(k, b, src, nat_slot, w_base, w_slot, fj2, rot_w, dq, store_to=None):
    nat_slot = nat_slot % 2
    nat = AP_(b, A_WN0, 2048) if nat_slot == 0 else b.xn_t[:, 0:2048]
    ld = k.dma("wn%d" % nat_slot, lambda g: g.dma_start(out=nat, in_=src),
               [b.wnat_rot.free[nat_slot]])
    nat_ready = ("wn%d" % nat_slot, ld)
    if dq is not None:
        scale_ap, bias_ap, ready = dq
        dqd = k.S(lambda s: s.activation(nat, nat, AF.Identity, bias=bias_ap,
                                         scale=scale_ap), [nat_ready, ready])
        nat_ready = ("S", dqd)
    if store_to is not None:
        k.DL(lambda g: g.dma_start(out=store_to, in_=nat), [nat_ready])
    for half in range(2):
        c0 = 8 * half
        base = w_base + 4096 * w_slot + 256 * c0 + 128 * fj2
        dst = b.arena[:, base:base + 8 * 256].rearrange(
            "p (a w) -> p a w", w=256)[:, :, 0:128]
        srcs = [(lambda cc=cc: nat[:, 128 * cc:128 * cc + 128])
                for cc in range(c0, c0 + 8)]
        temove_batch(k, b, srcs, dst, [nat_ready, rot_w.war(w_slot)],
                     "V" if half == 0 else "S")
    b.wnat_rot.free[nat_slot] = [("T", k.cnt("T"))]
    return [("V", k.cnt("V")), ("S", k.cnt("S"))]


def _wload_slab(k, b, slab_ap, w_base, slot, ch, cast_w):
    last = k.dma(ch, lambda g: g.dma_start(
        out=b.arena[:, w_base + 4096*slot: w_base + 4096*slot + 4096], in_=slab_ap),
        [cast_w, (b.wg_rot if w_base == A_WG else b.wu_rot).war(slot)])
    return [(ch, last)]


def _emit_ffn_group(k, b, tt_done, nfj, scale_mode, gid,
                    gate_src=None, up_src=None, down_src=None,
                    dq_gate=None, dq_up=None, dq_down=None,
                    slab=None, cast_w=None, store=None):
    npair = (nfj + 1) // 2
    for fp in range(npair):
        slot = fp % 2
        if slab is not None:
            ev_g = _wload_slab(k, b, slab.gs[fp], A_WG, slot, "tg%d" % slot, cast_w)
            ev_u = _wload_slab(k, b, slab.us[fp], A_WU, slot, "tu%d" % slot, cast_w)
        else:
            ev_g = ev_u = None
        for fj2 in range(min(2, nfj - 2 * fp)):
            fj = 2 * fp + fj2
            if slab is None:
                ev_g = _wprep(k, b, gate_src(fj), 0, A_WG, slot, fj2,
                              b.wg_rot, dq_gate(fj) if dq_gate else None)
                ev_u = _wprep(k, b, up_src(fj), 1, A_WU, slot, fj2,
                              b.wu_rot, dq_up(fj) if dq_up else None)
            for th in range(2):
                def mmg(t, slot=slot, fj2=fj2, th=th):
                    r = None
                    for cc in range(16):
                        r = t.matmul(b.psE[:, 0:512],
                                     AP_(b, A_WG + 4096*slot + 256*cc + 128*fj2, 128),
                                     AP_(b, A_XNT + 1024*cc + 512*th, 512),
                                     start=(cc == 0), stop=(cc == 15))
                    return r
                g_d = k.T(mmg, ev_g + [b.xnT_done, b.psE_rot.war(0)])
                def mmu(t, slot=slot, fj2=fj2, th=th):
                    r = None
                    for cc in range(16):
                        r = t.matmul(b.psF[:, 0:512],
                                     AP_(b, A_WU + 4096*slot + 256*cc + 128*fj2, 128),
                                     AP_(b, A_XNT + 1024*cc + 512*th, 512),
                                     start=(cc == 0), stop=(cc == 15))
                    return r
                u_d = k.T(mmu, ev_u + [b.xnT_done, b.psF_rot.war(0)])
                hc = A_H + 1024 * fj + 512 * th
                sg = k.S(lambda s, hc=hc: s.activation(AP_(b, hc, 512), b.psE[:, 0:512],
                                                       AF.Silu), [("T", g_d), b.h_war])
                b.psE_rot.done(0, "S", sg)
                ue = k.V(lambda v: v.tensor_copy(AP_(b, A_SU, 512), b.psF[:, 0:512]),
                         [("T", u_d), b.su_war])
                b.psF_rot.done(0, "V", ue)
                hm = k.V(lambda v, hc=hc: v.tensor_tensor(
                    AP_(b, hc, 512), AP_(b, hc, 512), AP_(b, A_SU, 512), ALU.mult),
                    [("V", ue), ("S", sg)])
                b.su_war = [("V", hm)]
        if store is not None:
            wev = [("V", k.cnt("V")), ("S", k.cnt("S"))]
            k.DL(lambda g, fp=fp, slot=slot: g.dma_start(
                out=store.gs[fp],
                in_=b.arena[:, A_WG + 4096*slot: A_WG + 4096*slot + 4096]), wev)
            k.DL(lambda g, fp=fp, slot=slot: g.dma_start(
                out=store.us[fp],
                in_=b.arena[:, A_WU + 4096*slot: A_WU + 4096*slot + 4096]), wev)
            b.wg_rot.done(slot, "DL", k.cnt("DL"))
            b.wu_rot.done(slot, "DL", k.cnt("DL"))
        else:
            b.wg_rot.done(slot, "T", k.cnt("T"))
            b.wu_rot.done(slot, "T", k.cnt("T"))
    h_all = ("V", k.cnt("V"))

    fw = 128 * nfj
    for cb in range(4):
        if slab is not None:
            last = k.dma("pjt", lambda g, cb=cb: g.dma_start(
                out=b.arena[:, A_PJ:A_PJ + 512*nfj], in_=slab.ds[cb]),
                [cast_w, b.pj_war])
            pj_ready = [("pjt", last)]
        else:
            for cj in range(4):
                cq = 4 * cb + cj
                nslot = cq % 2
                war = b.wnat_rot.war(cq)
                nat = (AP_(b, A_WN0, fw) if nslot == 0
                       else b.xn_t[:, 0:fw])
                ldd = k.dma("wn%d" % nslot, lambda g, nat=nat, cq=cq: g.dma_start(
                    out=nat, in_=down_src(cq)), [war])
                nat_ready = ("wn%d" % nslot, ldd)
                if dq_down is not None:
                    scale_ap, bias_ap, ready = dq_down(cq)
                    dqd = k.S(lambda s, nat=nat, scale_ap=scale_ap, bias_ap=bias_ap:
                              s.activation(nat, nat, AF.Identity, bias=bias_ap,
                                           scale=scale_ap), [nat_ready, ready])
                    nat_ready = ("S", dqd)
                for f0 in range(0, nfj, 8):
                    nf = min(8, nfj - f0)
                    base = A_PJ + 512 * f0 + 128 * cj
                    dst = b.arena[:, base:base + 512 * nf].rearrange(
                        "p (a w) -> p a w", w=512)[:, :, 0:128] if nf > 1 else AP_(b, base, 128)
                    srcs = [(lambda nat=nat, fj=fj: nat[:, 128 * fj:128 * fj + 128])
                            for fj in range(f0, f0 + nf)]
                    temove_batch(k, b, srcs, dst, [nat_ready, b.pj_war],
                                 "V" if (cj + f0) % 2 == 0 else "S")
                b.wnat_rot.done(cq, "T", k.cnt("T"))
            pj_ready = [("V", k.cnt("V")), ("S", k.cnt("S"))]
            if store is not None:
                k.DL(lambda g, cb=cb: g.dma_start(
                    out=store.ds[cb], in_=b.arena[:, A_PJ:A_PJ + 512*nfj]), pj_ready)
        for quad in range(2):
            for fj in range(nfj):
                w = []
                if fj == 0:
                    w = pj_ready + [h_all] + [b.ps_rot.war(i) for i in range(4)]
                def mmd(t, fj=fj, quad=quad, nfj=nfj):
                    r = None
                    for ti in range(4):
                        tt = 4 * quad + ti
                        r = t.matmul(b.ps[ti][:, 0:512],
                                     AP_(b, A_H + 1024 * fj + 128 * tt, 128),
                                     AP_(b, A_PJ + 512 * fj, 512),
                                     start=(fj == 0), stop=(fj == nfj - 1))
                    return r
                k.T(mmd, w)
            mm_fin = ("T", k.cnt("T"))
            for ti in range(4):
                tt = 4 * quad + ti
                xs = b.x_sb[:, 2048 * tt + 512 * cb: 2048 * tt + 512 * cb + 512]
                if scale_mode[0] == "comb":
                    sc_ap = b.comb[:, 8 * tt + scale_mode[1]: 8 * tt + scale_mode[1] + 1]
                    w_extra = b.comb_ready
                elif scale_mode[0] == "sgate":
                    sc_ap = b.comb[:, 64 + tt: 64 + tt + 1]
                    w_extra = b.comb_ready
                else:
                    sc_ap = scale_mode[1]
                    w_extra = []
                u_d = k.V(lambda v, ti=ti, xs=xs, sc_ap=sc_ap: v.scalar_tensor_tensor(
                    xs, b.ps[ti][:, 0:512], sc_ap, xs, op0=ALU.mult, op1=ALU.add),
                    [mm_fin, tt_done[tt], w_extra])
                tt_done[tt] = ("V", u_d)
                b.ps_rot.done(ti, "V", u_d)
        if store is not None and slab is None:
            b.pj_war = [("T", k.cnt("T")), ("DL", k.cnt("DL"))]
        else:
            b.pj_war = [("T", k.cnt("T"))]
    b.h_war = [("T", k.cnt("T"))]


def _emit_router(k, b, wsrc):
    st = b.stats
    ld = k.dma("cn", lambda g: g.dma_start(out=b.xn_t[0:8, 0:2048], in_=wsrc.gate_w),
               [b.xnT_done, b.xn_war])
    srcs = [(lambda cc=cc: b.xn_t[0:8, 128 * cc:128 * cc + 128]) for cc in range(16)]
    temove_batch(k, b, srcs, AP_(b, A_RW, 128), [("cn", ld)], "V", width=8, in_parts=8)
    ld2 = k.dma("cn", lambda g: g.dma_start(out=b.xn_t[0:1, 0:2048], in_=wsrc.sgw),
                [("T", k.cnt("T"))])
    srcs = [(lambda cc=cc: b.xn_t[0:1, 128 * cc:128 * cc + 128]) for cc in range(16)]
    temove_batch(k, b, srcs, AP_(b, A_SGW, 16), [("cn", ld2)], "V", width=1, in_parts=1)
    b.xn_war = [("T", k.cnt("T"))]
    b.wnat_rot.free[1] = [("T", k.cnt("T"))] + list(b.xn_war)
    rw_ready = ("V", k.cnt("V"))

    P0, PM, M1, M2_, L0, SG = 16, 24, 32, 33, 34, 35
    for tt in range(NTT):
        def mm(t, tt=tt):
            r = None
            for cc in range(16):
                t.matmul(b.psE[:, 0:8], AP_(b, A_XNT + 1024 * cc + 128 * tt, 128),
                         AP_(b, A_RW + 8 * cc, 8), start=(cc == 0), stop=(cc == 15))
            for cc in range(16):
                r = t.matmul(b.psE[:, 8:9], AP_(b, A_XNT + 1024 * cc + 128 * tt, 128),
                             AP_(b, A_SGW + cc, 1), start=(cc == 0), stop=(cc == 15))
            return r
        m_d = k.T(mm, [rw_ready, b.xnT_done, b.psE_rot.war(0)])
        v1a = k.V(lambda v: v.reduce_max(st[:, M1:M1+1], b.psE[:, 0:8], axis=AX.X),
                  [("T", m_d)])
        v1 = k.V(lambda v: v.tensor_scalar(st[:, M1:M1+1], st[:, M1:M1+1], -1.0,
                                           None, ALU.mult), [("V", v1a)])
        s1 = k.S(lambda s: s.activation(st[:, P0:P0+8], b.psE[:, 0:8], AF.Exp,
                                        bias=st[:, M1:M1+1],
                                        accum_out=st[:, L0:L0+1]), [("V", v1)])
        v2 = k.V(lambda v: v.reciprocal(st[:, L0:L0+1], st[:, L0:L0+1]), [("S", s1)])
        v3 = k.V(lambda v: v.tensor_scalar(st[:, P0:P0+8], st[:, P0:P0+8],
                                           st[:, L0:L0+1], None, ALU.mult), [("V", v2)])
        v4 = k.V(lambda v: v.reduce_max(st[:, M1:M1+1], st[:, P0:P0+8], axis=AX.X),
                 [("V", v3)])
        v5 = k.V(lambda v: v.scalar_tensor_tensor(
            st[:, PM:PM+8], st[:, P0:P0+8], st[:, M1:M1+1], st[:, P0:P0+8],
            op0=ALU.is_lt, op1=ALU.mult), [("V", v4)])
        v6 = k.V(lambda v: v.reduce_max(st[:, M2_:M2_+1], st[:, PM:PM+8], axis=AX.X),
                 [("V", v5)])
        v7 = k.V(lambda v, tt=tt: v.scalar_tensor_tensor(
            b.comb[:, 8*tt:8*tt+8], st[:, P0:P0+8], st[:, M2_:M2_+1], st[:, P0:P0+8],
            op0=ALU.is_ge, op1=ALU.mult), [("V", v6)])
        v8 = k.V(lambda v, tt=tt: v.tensor_scalar(b.comb[:, 8*tt:8*tt+8],
                                           b.comb[:, 8*tt:8*tt+8], 0.5, None,
                                           ALU.mult), [("V", v7)])
        s2 = k.S(lambda s: s.activation(st[:, SG:SG+1], b.psE[:, 8:9], AF.Sigmoid),
                 [("T", m_d), ("S", s1)])
        v9 = k.V(lambda v, tt=tt: v.tensor_scalar(b.comb[:, 64+tt:64+tt+1], st[:, SG:SG+1],
                                           0.5, None, ALU.mult), [("S", s2), ("V", v8)])
        b.psE_rot.done(0, "V", v9)
    b.comb_ready = [("V", k.cnt("V"))]


def _emit_l1_scales(k, b, wsrc):
    for idx, (src_, col, n) in enumerate([(wsrc.gate_scale, 0, 64),
                                          (wsrc.up_scale, 64, 64),
                                          (wsrc.down_scale, 128, 16)]):
        ld = k.dma("cn", lambda g, src_=src_, n=n: g.dma_start(
            out=b.sq_t[0:n, 0:128], in_=src_.rearrange("(a p) -> a p", p=128)),
            [b.sq_war])
        war = b.psE_rot.war(0)
        t_d = k.T(lambda t, n=n: t.transpose(b.psE[:, 0:n], b.sq_t[0:n, 0:128],
                                             b.identF[0:n, 0:n]),
                  [("cn", ld), war])
        b.sq_war = [("T", t_d)]
        e_d = k.V(lambda v, col=col, n=n: v.tensor_copy(
            b.scl[:, col:col + n], b.psE[:, 0:n]), [("T", t_d)])
        b.psE_rot.done(0, "V", e_d)
    vb = k.V(lambda v: v.tensor_scalar(b.scl[:, 144:288], b.scl[:, 0:144],
                                       -128.0, None, ALU.mult),
             [("V", k.cnt("V"))])
    b.l1s_ready = [("V", vb)]


def _slab_ns(scr, kind, idx):
    if kind == "e":
        return NS(gs=[scr.fc1s[idx, fp] for fp in range(6)],
                  us=[scr.fc2s[idx, fp] for fp in range(6)],
                  ds=[scr.projs[idx, cb] for cb in range(4)])
    if kind == "s":
        return NS(gs=[scr.sfc1s[idx, fp] for fp in range(6)],
                  us=[scr.sfc2s[idx, fp] for fp in range(6)],
                  ds=[scr.sprojs[idx, cb] for cb in range(4)])
    return NS(gs=[scr.l1gs[idx, fp] for fp in range(4)],
              us=[scr.l1us[idx, fp] for fp in range(4)],
              ds=[scr.l1ds[idx, cb] for cb in range(4)])


def emit_ffn(k, b, kind, wsrc, tt_done, scr=None, second=False):
    if kind == "l0":
        _emit_router(k, b, wsrc)
        for e in range(8):
            sl = _slab_ns(scr, "e", e) if scr else None
            _emit_ffn_group(
                k, b, tt_done, nfj=11, scale_mode=("comb", e), gid=("e", e),
                gate_src=lambda fj, e=e: wsrc.fc1[e, 128*fj:128*fj+128, :],
                up_src=lambda fj, e=e: wsrc.fc2[e, 128*fj:128*fj+128, :],
                down_src=lambda cq, e=e: wsrc.proj[e, 128*cq:128*cq+128, :],
                slab=sl if second else None,
                cast_w=b.castdone.get("l0") if second else None,
                store=sl if (scr and not second) else None)
        for gi in range(4):
            sl = _slab_ns(scr, "s", gi) if scr else None
            fj0 = 11 * gi
            _emit_ffn_group(
                k, b, tt_done, nfj=11, scale_mode=("sgate", None), gid=("s", gi),
                gate_src=lambda fj, fj0=fj0: wsrc.sfc1[128*(fj0+fj):128*(fj0+fj)+128, :],
                up_src=lambda fj, fj0=fj0: wsrc.sfc2[128*(fj0+fj):128*(fj0+fj)+128, :],
                down_src=lambda cq, fj0=fj0: wsrc.sproj[128*cq:128*cq+128,
                                                        128*fj0:128*fj0+1408],
                slab=sl if second else None,
                cast_w=b.castdone.get("l0") if second else None,
                store=sl if (scr and not second) else None)
        if scr and not second:
            b.castdone["l0"] = ("DL", k.cnt("DL"))
    else:
        b.wnat_rot.free[1] = list(b.wnat_rot.free[1]) + list(b.xn_war)
        if not second:
            _emit_l1_scales(k, b, wsrc)
        for gi in range(8):
            sl = _slab_ns(scr, "l1", gi) if scr else None
            fj0 = 8 * gi
            def dqg(fj, fj0=fj0):
                i2 = fj0 + fj
                return (b.scl[:, i2:i2+1], b.scl[:, 144+i2:144+i2+1], b.l1s_ready)
            def dqu(fj, fj0=fj0):
                i2 = fj0 + fj
                return (b.scl[:, 64+i2:64+i2+1], b.scl[:, 208+i2:208+i2+1], b.l1s_ready)
            def dqd(cq):
                return (b.scl[:, 128+cq:128+cq+1], b.scl[:, 272+cq:272+cq+1], b.l1s_ready)
            _emit_ffn_group(
                k, b, tt_done, nfj=8, scale_mode=("const", 0.5), gid=("l1", gi),
                gate_src=lambda fj, fj0=fj0: wsrc.gate_w[128*(fj0+fj):128*(fj0+fj)+128, :],
                up_src=lambda fj, fj0=fj0: wsrc.up_w[128*(fj0+fj):128*(fj0+fj)+128, :],
                down_src=lambda cq, fj0=fj0: wsrc.down_w[128*cq:128*cq+128,
                                                         128*fj0:128*fj0+1024],
                dq_gate=None if second else dqg, dq_up=None if second else dqu,
                dq_down=None if second else dqd,
                slab=sl if second else None,
                cast_w=b.castdone.get("l1") if second else None,
                store=sl if (scr and not second) else None)
        if scr and not second:
            b.castdone["l1"] = ("DL", k.cnt("DL"))
    b.xn_war = [("T", k.cnt("T"))] + list(b.xn_war)
    return tt_done



def emit_att2_build(k, b, bias_dram, pbase):
    l1 = k.dma("cn", lambda g: g.dma_start(out=b.x_sb[0:1, 0:2048], in_=bias_dram),
               [b.att2_guard])
    s1 = k.S(lambda s: s.activation(b.xn_t[0:1, 0:2048], b.x_sb[0:1, 0:2048],
                                    AF.Identity), [("cn", l1), b.xn_war])
    l2 = k.dma("cn", lambda g: g.dma_start(out=b.att2[pbase:pbase+1, :],
                                    in_=b.xn_t[0:1, 0:2048]), [("S", s1)])
    v1 = k.V(lambda v: v.tensor_copy(b.x_sb[0:1, 2048:4096], b.xn_t[0:1, 0:2048]),
             [("S", s1)])
    v2 = k.V(lambda v: v.tensor_tensor(b.x_sb[0:1, 0:2048], b.x_sb[0:1, 0:2048],
                                       b.x_sb[0:1, 2048:4096], ALU.subtract),
             [("V", v1)])
    s2 = k.S(lambda s: s.activation(b.xn_t[0:1, 0:2048], b.x_sb[0:1, 0:2048],
                                    AF.Identity), [("V", v2), ("cn", l2)])
    l3 = k.dma("cn", lambda g: g.dma_start(out=b.att2[pbase+1:pbase+2, :],
                                    in_=b.xn_t[0:1, 0:2048]), [("S", s2)])
    b.att2_guard = [("cn", l3)]
    b.xn_war = [("S", s2)]


def emit_attn_wprep(k, b, layer, wsrc, scr=None):
    if layer == 0:
        last = 0
        for j2, wv in enumerate([wsrc.wq_s, wsrc.wk_s, wsrc.wv_s]):
            for cc in range(16):
                dst = b.arena[:, A_WQKV + 768*cc + 256*j2:
                              A_WQKV + 768*cc + 256*j2 + 256]
                last = k.dma("wq", lambda g, wv=wv, cc=cc, dst=dst: g.dma_start(
                    out=dst, in_=wv[128*cc:128*cc+128, :]), [b.wst_war])
        b.wqkv_ready = [("wq", last)]
    else:
        cw = k.DL(lambda g: g.dma_start(out=scr.wqkvb, in_=wsrc.wqkv_s), [])
        last = 0
        for cc in range(16):
            last = k.dma("wq", lambda q, cc=cc: q.dma_start_transpose(
                b.arena[:, A_WQKV + 768*cc: A_WQKV + 768*cc + 768],
                scr.wqkvb[0:768, 128*cc:128*cc+128]),
                ([("DL", cw), b.sbsb_guard, b.wst_war] if cc == 0 else []), q="sync")
        b.wqkv_ready = [("wq", last)]
        bl = 0
        for m in range(6):
            bl = k.dma("cn", lambda g, m=m: g.dma_start(
                out=b.comb[:, 72+m:72+m+1],
                in_=wsrc.bqkv_s[128*m:128*m+128].rearrange("(p a) -> p a", a=1)),
                [b.wst_war])
        vb = k.V(lambda v: v.tensor_scalar(b.comb[:, 78:80], b.comb[:, 72:74],
                                           RSQD, None, ALU.mult), [("cn", bl)])
        b.qkvbias_ready = [("V", vb)]


def emit_attn_qkv(k, b, layer, slab_src, q_dram):
    vo = []
    for h2 in range(2):
        ap = b.arena[:, A_V + VH*h2: A_V + VH*h2 + VH].rearrange(
            "p (a w) -> p a w", w=VW)[:, :, 128:129]
        vo.append(k.V(lambda v, ap=ap: v.memset(ap, 1.0), [b.vones_guard]))
    v_ones = ("V", vo[-1])
    qkv_bias = [b.qkvbias_ready] if layer == 1 else []

    for tb in range(8):
        for th in range(2):
            mm_last = 0
            for cc in range(16):
                slot = cc % 2
                ldw = k.dma("sl%d" % slot, lambda q, cc=cc, tb=tb, th=th, slot=slot:
                            q.dma_start(out=AP_(b, A_SLAB + 512*slot, 512),
                                        in_=slab_src(cc, tb, th)),
                            [b.slab_rot.war(slot), b.ag_guard], q="sync")
                wts = b.wqkv_ready + [("sl%d" % slot, ldw)]
                if cc == 0:
                    wts = wts + [b.ps_rot.war(i) for i in range(4)] +                           [b.psE_rot.war(0), b.psF_rot.war(0)]
                def mm(t, cc=cc, slot=slot):
                    r = None
                    for h2 in range(2):
                        t.matmul(b.ps[h2][:, 0:512],
                                 AP_(b, A_WQKV + 768*cc + 128*(2+h2), 128),
                                 AP_(b, A_SLAB + 512*slot, 512),
                                 start=(cc == 0), stop=(cc == 15))
                        t.matmul(b.ps[2+h2][:, 0:512],
                                 AP_(b, A_WQKV + 768*cc + 128*(4+h2), 128),
                                 AP_(b, A_SLAB + 512*slot, 512),
                                 start=(cc == 0), stop=(cc == 15))
                        r = t.matmul((b.psE if h2 == 0 else b.psF)[:, 0:512],
                                     AP_(b, A_WQKV + 768*cc + 128*h2, 128),
                                     AP_(b, A_SLAB + 512*slot, 512),
                                     start=(cc == 0), stop=(cc == 15))
                    return r
                mm_last = k.T(mm, wts + qkv_bias)
                b.slab_rot.done(slot, "T", mm_last)
            for h2 in range(2):
                kdst = AP_(b, A_K + 8192*h2 + 1024*tb + 512*th, 512)
                if layer == 1:
                    ek = k.S(lambda s, h2=h2, kdst=kdst: s.activation(
                        kdst, b.ps[h2][:, 0:512], AF.Identity,
                        bias=b.comb[:, 74+h2:74+h2+1]), [("T", mm_last)])
                else:
                    ek = k.S(lambda s, h2=h2, kdst=kdst: s.copy(
                        kdst, b.ps[h2][:, 0:512]), [("T", mm_last)])
                b.ps_rot.done(h2, "S", ek)
                ps_ = b.p_rot.war(0)
                stg = AP_(b, A_P, 512)
                if layer == 1:
                    ev = k.S(lambda s, h2=h2, stg=stg: s.activation(
                        stg, b.ps[2+h2][:, 0:512], AF.Identity,
                        bias=b.comb[:, 76+h2:76+h2+1]), [("T", mm_last), ps_])
                else:
                    ev = k.S(lambda s, h2=h2, stg=stg: s.copy(
                        stg, b.ps[2+h2][:, 0:512]), [("T", mm_last), ps_])
                b.ps_rot.done(2+h2, "S", ev)
                tile0 = 8*tb + 4*th
                base = A_V + VH*h2 + VW*tile0
                dst = b.arena[:, base:base + 4*VW].rearrange(
                    "p (a w) -> p a w", w=VW)[:, :, 0:128]
                srcs = [(lambda i=i, stg=stg: stg[:, 128*i:128*i+128])
                        for i in range(4)]
                temove_batch(k, b, srcs, dst, [("S", ev), v_ones], "V")
                b.p_rot.done(0, "T", k.cnt("T"))
                pq = b.pt_rot.war(0)
                qstg = AP_(b, A_PT, 512)
                qps = b.psE if h2 == 0 else b.psF
                if layer == 1:
                    eq = k.S(lambda s, qstg=qstg, qps=qps, h2=h2: s.activation(
                        qstg, qps[:, 0:512], AF.Identity, scale=RSQD,
                        bias=b.comb[:, 78+h2:78+h2+1]), [("T", mm_last), pq])
                else:
                    eq = k.S(lambda s, qstg=qstg, qps=qps: s.activation(
                        qstg, qps[:, 0:512], AF.Identity, scale=RSQD),
                        [("T", mm_last), pq])
                (b.psE_rot if h2 == 0 else b.psF_rot).done(0, "S", eq)
                dq = k.dma("qst", lambda g, h2=h2, tb=tb, th=th, qstg=qstg: g.dma_start(
                    out=q_dram[128*h2:128*h2+128, 1024*tb+512*th:1024*tb+512*th+512],
                    in_=qstg), [("S", eq)])
                b.pt_rot.done(0, "qst", dq)
    b.kv_ready = [("V", k.cnt("V")), ("S", k.cnt("S")), ("qst", k.cnt("qst"))]


def emit_attn_core(k, b, layer, bias_rows, q_dram, ctx_dram):
    bl = k.dma("cn", lambda g: g.dma_start(out=b.bias_sb[0:2, :], in_=bias_rows[0]),
               [b.bias_guard])
    bl = k.dma("cn", lambda g: g.dma_start(out=b.bias_sb[32:34, :], in_=bias_rows[1]),
               [b.bias_guard])
    bias_ready = ("cn", bl)
    ctxi = 0
    for quarter in range(2):
        for h2 in range(2):
            lq = k.dma("qld", lambda q, h2=h2, quarter=quarter: q.dma_start(
                out=AP_(b, A_Q, 4096),
                in_=q_dram[128*h2:128*h2+128, 4096*quarter:4096*quarter+4096]),
                [b.q_war] + b.kv_ready, q="sync")
            q_ready = ("qld", lq)
            for ql in range(32):
                qt = 32 * quarter + ql
                nkc = qt // 4 + 1
                rb = 32 * h2
                red = 0
                for i in range(nkc):
                    bank = b.psE if i % 2 == 0 else b.psF
                    rot = b.psE_rot if i % 2 == 0 else b.psF_rot
                    dvar = qt - 4 * i if i == nkc - 1 else None
                    def mma(t, i=i, bank=bank, ql=ql, rb=rb, dvar=dvar, h2=h2):
                        t.matmul(bank[:, 0:512], b.ones2[rb:rb+2, 0:128],
                                 b.bias_sb[rb:rb+2, 512*i:512*i+512],
                                 start=True, stop=False)
                        if dvar is not None:
                            t.matmul(bank[:, 0:512], b.ident[:, :],
                                     b.tri[:, 512*dvar:512*dvar+512],
                                     start=False, stop=False)
                        return t.matmul(bank[:, 0:512],
                                        AP_(b, A_Q + 128*ql, 128),
                                        AP_(b, A_K + 8192*h2 + 512*i, 512),
                                        start=False, stop=True)
                    m_d = k.T(mma, [q_ready, bias_ready, rot.war(0)])
                    red = k.V(lambda v, i=i, bank=bank: v.reduce_max(
                        b.stats[:, 40+i:40+i+1], bank[:, 0:512], axis=AX.X),
                        [("T", m_d)])
                    rot.done(0, "V", red)
                r1 = k.V(lambda v, nkc=nkc: v.reduce_max(
                    b.stats[:, 56:57], b.stats[:, 40:40+nkc], axis=AX.X),
                    [("V", red)])
                r2 = k.V(lambda v: v.tensor_scalar(
                    b.stats[:, 57:58], b.stats[:, 56:57], -1.0, None, ALU.mult),
                    [("V", r1)])
                negm = ("V", r2)
                pv_last = 0
                for i in range(nkc):
                    bank = b.psE if i % 2 == 0 else b.psF
                    rot = b.psE_rot if i % 2 == 0 else b.psF_rot
                    dvar = qt - 4 * i if i == nkc - 1 else None
                    def mmb(t, i=i, bank=bank, ql=ql, rb=rb, dvar=dvar, h2=h2):
                        t.matmul(bank[:, 0:512], b.ones2[rb:rb+2, 0:128],
                                 b.bias_sb[rb:rb+2, 512*i:512*i+512],
                                 start=True, stop=False)
                        if dvar is not None:
                            t.matmul(bank[:, 0:512], b.ident[:, :],
                                     b.tri[:, 512*dvar:512*dvar+512],
                                     start=False, stop=False)
                        return t.matmul(bank[:, 0:512],
                                        AP_(b, A_Q + 128*ql, 128),
                                        AP_(b, A_K + 8192*h2 + 512*i, 512),
                                        start=False, stop=True)
                    m_d = k.T(mmb, [rot.war(0)])
                    pslot = i % 2
                    pw = b.p_rot.war(pslot)
                    ex = k.S(lambda s, bank=bank, pslot=pslot: s.activation(
                        AP_(b, A_P + 512*pslot, 512), bank[:, 0:512], AF.Exp,
                        bias=b.stats[:, 57:58]), [("T", m_d), negm, pw])
                    rot.done(0, "S", ex)
                    p_ready = ("S", ex)
                    ptw = b.pt_rot.war(pslot)
                    srcs = [(lambda j=j, pslot=pslot: AP_(b, A_P + 512*pslot + 128*j, 128))
                            for j in range(4)]
                    ev = temove_batch(k, b, srcs, AP_(b, A_PT + 512*pslot, 512),
                                      [p_ready, ptw], "V")
                    b.p_rot.done(pslot, "T", k.cnt("T"))
                    def mpv(t, i=i, pslot=pslot, h2=h2, ci=ctxi, nkc=nkc):
                        r = None
                        for j in range(4):
                            tkb = 4*i + j
                            r = t.matmul(b.ps[ci][:, 0:129],
                                         AP_(b, A_PT + 512*pslot + 128*j, 128),
                                         b.arena[:, A_V + VH*h2 + VW*tkb:
                                                 A_V + VH*h2 + VW*tkb + 129],
                                         start=(i == 0 and j == 0),
                                         stop=(i == nkc - 1 and j == 3))
                        return r
                    w = [ev]
                    if i == 0:
                        w.append(b.ps_rot.war(ctxi))
                    pv_last = k.T(mpv, w)
                    b.pt_rot.done(pslot, "T", pv_last)
                rl = k.V(lambda v, ci=ctxi: v.reciprocal(
                    b.stats[:, 58:59], b.ps[ci][:, 128:129]), [("T", pv_last)])
                pw = b.p_rot.war(0)
                fs = k.S(lambda s, ci=ctxi: s.activation(
                    AP_(b, A_P, 128), b.ps[ci][:, 0:128], AF.Identity,
                    scale=b.stats[:, 58:59]), [("V", rl), pw])
                b.ps_rot.done(ctxi, "S", fs)
                ptw = b.pt_rot.war(0)
                ev = temove_batch(k, b, [lambda: AP_(b, A_P, 128)],
                                  AP_(b, A_PT, 128), [("S", fs), ptw], "V")
                b.p_rot.done(0, "T", k.cnt("T"))
                dc = k.dma("cst", lambda g, h2=h2, qt=qt: g.dma_start(
                    out=ctx_dram[128*h2:128*h2+128, 128*qt:128*qt+128],
                    in_=AP_(b, A_PT, 128)), [ev])
                b.pt_rot.done(0, "cst", dc)
                ctxi = (ctxi + 1) % 4
            b.q_war = [("T", k.cnt("T"))]
    b.ctx_done = [("cst", k.cnt("cst"))]


def emit_attn_outproj(k, b, layer, wo_src, ctx_loader, tt_done, wo_cast=None):
    ctx_ready = ctx_loader(k, b)
    rb2 = 32 * layer
    for ob in range(4):
        for quad in range(2):
            for dc in range(16):
                slot = dc % 2
                wdst = AP_(b, A_WO + 512*slot, 512)
                if layer == 0:
                    lw = k.dma("wo%d" % slot, lambda g, ob=ob, dc=dc, wdst=wdst:
                               g.dma_start(out=wdst,
                                           in_=wo_src[128*dc:128*dc+128, 512*ob:512*ob+512]),
                               [b.wo_rot.war(slot)])
                    w_ready = [("wo%d" % slot, lw)]
                else:
                    lw = k.dma("wo%d" % slot, lambda q, ob=ob, dc=dc, wdst=wdst:
                               q.dma_start_transpose(
                                   wdst, wo_src[512*ob:512*ob+512, 128*dc:128*dc+128]),
                               [b.wo_rot.war(slot), b.sbsb_guard, wo_cast], q="sync")
                    w_ready = [("wo%d" % slot, lw)]
                wts = w_ready + [ctx_ready]
                if dc == 0:
                    wts += [b.ps_rot.war(i) for i in range(4)]
                def mmo(t, ob=ob, quad=quad, dc=dc, slot=slot, rb2=rb2):
                    r = None
                    if dc == 0:
                        for ti in range(4):
                            t.matmul(b.ps[ti][:, 0:512], b.ones2[rb2:rb2+2, 0:128],
                                     b.att2[rb2:rb2+2, 512*ob:512*ob+512],
                                     start=True, stop=False)
                    for ti in range(4):
                        tt = 4*quad + ti
                        r = t.matmul(b.ps[ti][:, 0:512],
                                     AP_(b, A_CTX + 1024*dc + 128*tt, 128),
                                     AP_(b, A_WO + 512*slot, 512),
                                     start=False, stop=(dc == 15))
                    return r
                mo = k.T(mmo, wts + [b.att2_ready])
                b.wo_rot.done(slot, "T", mo)
            mm_fin = ("T", k.cnt("T"))
            for ti in range(4):
                tt = 4*quad + ti
                xs = b.x_sb[:, 2048*tt + 512*ob: 2048*tt + 512*ob + 512]
                u_d = k.V(lambda v, ti=ti, xs=xs: v.tensor_tensor(
                    xs, xs, b.ps[ti][:, 0:512], ALU.add),
                    [mm_fin, tt_done[tt]])
                tt_done[tt] = ("V", u_d)
                b.ps_rot.done(ti, "V", u_d)
    b.ctxbuf_war = [("T", k.cnt("T"))]
    return tt_done



def host_consts():
    ident = np.eye(128, dtype=BF)
    ones2 = np.ones((34, 128), dtype=BF)
    tri = np.zeros((128, 4 * 512), dtype=BF)
    for d in range(4):
        i = np.arange(128)[:, None]
        jj = np.arange(512)[None, :]
        tri[:, 512*d:512*d+512] = np.where((jj - i) <= 128 * d, 0.0, -30000.0).astype(BF)
    return dict(cn_ident=ident, cn_ones2=ones2, cn_tri=tri,
                cn_identF=np.eye(128, dtype=np.float32))


def emit_const_loads(k, b, cn):
    l1 = k.DL(lambda g: g.dma_start(out=b.ident[:, :], in_=cn.ident))
    l2 = k.DL(lambda g: g.dma_start(out=b.ones2[:, :], in_=cn.ones2))
    l3 = k.DL(lambda g: g.dma_start(out=b.tri[:, :], in_=cn.tri))
    l4 = k.DL(lambda g: g.dma_start(out=b.identF[:, :], in_=cn.identF))
    return ("DL", l4)



def build_ffn_launch(kind, with_xnT=False):
    nc = bass.Bass("TRN2", target_bir_lowering=False)
    x_in = nc.dram_tensor("x_in", [TLOC, C], F32, kind="ExternalInput").ap()
    x_out = nc.dram_tensor("x_out", [TLOC, C], F32, kind="ExternalOutput").ap()
    xnT_out = (nc.dram_tensor("xnT_out", [C, TLOC], BF16, kind="ExternalOutput").ap()
               if with_xnT else None)
    cn = NS(ident=nc.dram_tensor("cn_ident", [128, 128], BF16, kind="ExternalInput").ap(),
            ones2=nc.dram_tensor("cn_ones2", [34, 128], BF16, kind="ExternalInput").ap(),
            tri=nc.dram_tensor("cn_tri", [128, 2048], BF16, kind="ExternalInput").ap(),
            identF=nc.dram_tensor("cn_identF", [128, 128], F32, kind="ExternalInput").ap())
    if kind == "l0":
        wsrc = NS(
            gate_w=nc.dram_tensor("gate_w", [8, C], F32, kind="ExternalInput").ap(),
            sgw=nc.dram_tensor("sgw", [1, C], F32, kind="ExternalInput").ap(),
            fc1=nc.dram_tensor("fc1", [8, F_MOE, C], F32, kind="ExternalInput").ap(),
            fc2=nc.dram_tensor("fc2", [8, F_MOE, C], F32, kind="ExternalInput").ap(),
            proj=nc.dram_tensor("proj", [8, C, F_MOE], F32, kind="ExternalInput").ap(),
            sfc1=nc.dram_tensor("sfc1", [F_SH, C], F32, kind="ExternalInput").ap(),
            sfc2=nc.dram_tensor("sfc2", [F_SH, C], F32, kind="ExternalInput").ap(),
            sproj=nc.dram_tensor("sproj", [C, F_SH], F32, kind="ExternalInput").ap(),
        )
    else:
        wsrc = NS(
            gate_w=nc.dram_tensor("gate_w", [I_L1, C], I32, kind="ExternalInput").ap(),
            up_w=nc.dram_tensor("up_w", [I_L1, C], I32, kind="ExternalInput").ap(),
            down_w=nc.dram_tensor("down_w", [C, I_L1], I32, kind="ExternalInput").ap(),
            gate_scale=nc.dram_tensor("gate_scale", [I_L1], F32, kind="ExternalInput").ap(),
            up_scale=nc.dram_tensor("up_scale", [I_L1], F32, kind="ExternalInput").ap(),
            down_scale=nc.dram_tensor("down_scale", [C], F32, kind="ExternalInput").ap(),
        )
    es = ExitStack()
    k = K(nc, es)
    b = make_bufs(k)
    emit_const_loads(k, b, cn)
    b.sbsb_guard = ("T", 0)
    xl = k.dma("xin", lambda g: g.dma_start(
        out=b.x_sb[:, :].rearrange("p (a c) -> p a c", c=2048),
        in_=x_in.rearrange("(a p) c -> p a c", p=128)))
    tt_done = [("xin", xl)] * NTT
    for tt in range(NTT):
        emit_ln_tile(k, b, tt, [("xin", xl)])
    b.xnT_done = [("V", k.cnt("V")), ("S", k.cnt("S"))]
    tt_done = emit_ffn(k, b, kind, wsrc, tt_done)
    st_last = k.dma("xout", lambda g: g.dma_start(
        out=x_out.rearrange("(a p) c -> p a c", p=128),
        in_=b.x_sb[:, :].rearrange("p (a c) -> p a c", c=2048)), tt_done)
    if with_xnT:
        tguard = ("T", k.cnt("T"))
        for tt in range(NTT):
            emit_ln_tile(k, b, tt, [tt_done[tt], tguard])
        xnT_done = [("V", k.cnt("V")), ("S", k.cnt("S"))]
        st2 = k.dma("xno", lambda g: g.dma_start(
            out=xnT_out.rearrange("(a p) w -> p a w", p=128),
            in_=b.arena[:, A_XNT:A_XNT+16384].rearrange("p (a w) -> p a w", w=1024)),
            xnT_done)
        k.waiter("G", [("xout", st_last), ("xno", st2)])
    else:
        k.waiter("G", [("xout", st_last)])
    k.finalize()
    es.close()
    return nc


def build_attn_core_launch(layer):
    nc = bass.Bass("TRN2", target_bir_lowering=False)
    xnT_full = nc.dram_tensor("xnT_full", [C, SEQ], BF16, kind="ExternalInput").ap()
    brows = nc.dram_tensor("cn_brows", [2, 2, SEQ], BF16, kind="ExternalInput").ap()
    ctx_out = nc.dram_tensor("ctx_out", [2 * D, SEQ], BF16, kind="ExternalOutput").ap()
    cn = NS(ident=nc.dram_tensor("cn_ident", [128, 128], BF16, kind="ExternalInput").ap(),
            ones2=nc.dram_tensor("cn_ones2", [34, 128], BF16, kind="ExternalInput").ap(),
            tri=nc.dram_tensor("cn_tri", [128, 2048], BF16, kind="ExternalInput").ap(),
            identF=nc.dram_tensor("cn_identF", [128, 128], F32, kind="ExternalInput").ap())
    if layer == 0:
        wsrc = NS(wq_s=nc.dram_tensor("wq_s", [C, 256], F32, kind="ExternalInput").ap(),
                  wk_s=nc.dram_tensor("wk_s", [C, 256], F32, kind="ExternalInput").ap(),
                  wv_s=nc.dram_tensor("wv_s", [C, 256], F32, kind="ExternalInput").ap())
    else:
        wsrc = NS(wqkv_s=nc.dram_tensor("wqkv_s", [768, C], F32, kind="ExternalInput").ap(),
                  bqkv_s=nc.dram_tensor("bqkv_s", [768], F32, kind="ExternalInput").ap())
    q_dram = nc.dram_tensor("q_dram", [2 * D, SEQ], BF16, kind="Internal").ap()
    scr = NS(wqkvb=nc.dram_tensor("wqkvb", [768, C], BF16, kind="Internal").ap()) \
        if layer == 1 else None
    es = ExitStack()
    k = K(nc, es)
    b = make_bufs(k)
    emit_const_loads(k, b, cn)
    b.sbsb_guard = ("T", 0)
    emit_attn_wprep(k, b, layer, wsrc, scr)
    slab_src = lambda cc, tb, th: xnT_full[128*cc:128*cc+128,
                                           1024*tb+512*th:1024*tb+512*th+512]
    emit_attn_qkv(k, b, layer, slab_src, q_dram)
    emit_attn_core(k, b, layer, [brows[0], brows[1]], q_dram, ctx_out)
    k.waiter("G", [b.ctx_done])
    es.close()
    return nc


def build_outproj_launch(layer):
    nc = bass.Bass("TRN2", target_bir_lowering=False)
    x_in = nc.dram_tensor("x_in", [TLOC, C], F32, kind="ExternalInput").ap()
    x_out = nc.dram_tensor("x_out", [TLOC, C], F32, kind="ExternalOutput").ap()
    ctx_own = nc.dram_tensor("ctx_own", [C, TLOC], BF16, kind="ExternalInput").ap()
    wo = nc.dram_tensor("wo", [C, C], F32, kind="ExternalInput").ap()
    bias2 = nc.dram_tensor("bias2", [1, C], F32, kind="ExternalInput").ap()
    cn = NS(ident=nc.dram_tensor("cn_ident", [128, 128], BF16, kind="ExternalInput").ap(),
            ones2=nc.dram_tensor("cn_ones2", [34, 128], BF16, kind="ExternalInput").ap(),
            tri=nc.dram_tensor("cn_tri", [128, 2048], BF16, kind="ExternalInput").ap(),
            identF=nc.dram_tensor("cn_identF", [128, 128], F32, kind="ExternalInput").ap())
    woutb = (nc.dram_tensor("woutb", [C, C], BF16, kind="Internal").ap()
             if layer == 1 else None)
    es = ExitStack()
    k = K(nc, es)
    b = make_bufs(k)
    emit_const_loads(k, b, cn)
    wo_cast = None
    if layer == 1:
        cwo = k.DL(lambda g: g.dma_start(out=woutb, in_=wo), [])
        wo_cast = ("DL", cwo)
    emit_att2_build(k, b, bias2, 32 * layer)
    b.att2_ready = b.att2_guard
    b.sbsb_guard = b.att2_guard
    xl = k.dma("xin", lambda g: g.dma_start(
        out=b.x_sb[:, :].rearrange("p (a c) -> p a c", c=2048),
        in_=x_in.rearrange("(a p) c -> p a c", p=128)), [b.att2_guard])
    tt_done = [("xin", xl)] * NTT
    def ctx_loader(k2, b2):
        ld = k2.dma("ctx", lambda g: g.dma_start(
            out=b2.arena[:, A_CTX:A_CTX+16384].rearrange("p (a w) -> p a w", w=1024),
            in_=ctx_own.rearrange("(a p) w -> p a w", p=128)), [b2.ctxbuf_war])
        return ("ctx", ld)
    tt_done = emit_attn_outproj(k, b, layer, woutb if layer == 1 else wo,
                                ctx_loader, tt_done, wo_cast=wo_cast)
    st_last = k.dma("xout", lambda g: g.dma_start(
        out=x_out.rearrange("(a p) c -> p a c", p=128),
        in_=b.x_sb[:, :].rearrange("p (a c) -> p a c", c=2048)), tt_done)
    k.waiter("G", [("xout", st_last)])
    k.finalize()
    es.close()
    return nc


def build_fused(n_sub=6):
    nc = bass.Bass("TRN2", target_bir_lowering=False)
    RG = [list(range(NCORES))]
    x_in = nc.dram_tensor("x_in", [TLOC, C], F32, kind="ExternalInput").ap()
    x_out = nc.dram_tensor("x_out", [TLOC, C], F32, kind="ExternalOutput").ap()
    cn = NS(ident=nc.dram_tensor("cn_ident", [128, 128], BF16, kind="ExternalInput").ap(),
            ones2=nc.dram_tensor("cn_ones2", [34, 128], BF16, kind="ExternalInput").ap(),
            tri=nc.dram_tensor("cn_tri", [128, 2048], BF16, kind="ExternalInput").ap(),
            identF=nc.dram_tensor("cn_identF", [128, 128], F32, kind="ExternalInput").ap())
    brows = nc.dram_tensor("cn_brows", [2, 2, 2, SEQ], BF16, kind="ExternalInput").ap()
    w0 = NS(gate_w=nc.dram_tensor("gate_w", [8, C], F32, kind="ExternalInput").ap(),
            sgw=nc.dram_tensor("sgw", [1, C], F32, kind="ExternalInput").ap(),
            fc1=nc.dram_tensor("fc1", [8, F_MOE, C], F32, kind="ExternalInput").ap(),
            fc2=nc.dram_tensor("fc2", [8, F_MOE, C], F32, kind="ExternalInput").ap(),
            proj=nc.dram_tensor("proj", [8, C, F_MOE], F32, kind="ExternalInput").ap(),
            sfc1=nc.dram_tensor("sfc1", [F_SH, C], F32, kind="ExternalInput").ap(),
            sfc2=nc.dram_tensor("sfc2", [F_SH, C], F32, kind="ExternalInput").ap(),
            sproj=nc.dram_tensor("sproj", [C, F_SH], F32, kind="ExternalInput").ap())
    w1 = NS(gate_w=nc.dram_tensor("l1_gate_w", [I_L1, C], I32, kind="ExternalInput").ap(),
            up_w=nc.dram_tensor("l1_up_w", [I_L1, C], I32, kind="ExternalInput").ap(),
            down_w=nc.dram_tensor("l1_down_w", [C, I_L1], I32, kind="ExternalInput").ap(),
            gate_scale=nc.dram_tensor("l1_gate_scale", [I_L1], F32, kind="ExternalInput").ap(),
            up_scale=nc.dram_tensor("l1_up_scale", [I_L1], F32, kind="ExternalInput").ap(),
            down_scale=nc.dram_tensor("l1_down_scale", [C], F32, kind="ExternalInput").ap())
    wa0 = NS(wq_s=nc.dram_tensor("wq_s", [C, 256], F32, kind="ExternalInput").ap(),
             wk_s=nc.dram_tensor("wk_s", [C, 256], F32, kind="ExternalInput").ap(),
             wv_s=nc.dram_tensor("wv_s", [C, 256], F32, kind="ExternalInput").ap())
    wa1 = NS(wqkv_s=nc.dram_tensor("wqkv_s", [768, C], F32, kind="ExternalInput").ap(),
             bqkv_s=nc.dram_tensor("bqkv_s", [768], F32, kind="ExternalInput").ap())
    wo0 = nc.dram_tensor("wo0", [C, C], F32, kind="ExternalInput").ap()
    wo1 = nc.dram_tensor("wo1", [C, C], F32, kind="ExternalInput").ap()
    bo0 = nc.dram_tensor("bo0", [1, C], F32, kind="ExternalInput").ap()
    bo1 = nc.dram_tensor("bo1", [1, C], F32, kind="ExternalInput").ap()

    scr = NS(fc1s=nc.dram_tensor("fc1s", [8, 6, 128, 4096], BF16, kind="Internal").ap(),
             fc2s=nc.dram_tensor("fc2s", [8, 6, 128, 4096], BF16, kind="Internal").ap(),
             projs=nc.dram_tensor("projs", [8, 4, 128, 5632], BF16, kind="Internal").ap(),
             sfc1s=nc.dram_tensor("sfc1s", [4, 6, 128, 4096], BF16, kind="Internal").ap(),
             sfc2s=nc.dram_tensor("sfc2s", [4, 6, 128, 4096], BF16, kind="Internal").ap(),
             sprojs=nc.dram_tensor("sprojs", [4, 4, 128, 5632], BF16, kind="Internal").ap(),
             l1gs=nc.dram_tensor("l1gs", [8, 4, 128, 4096], BF16, kind="Internal").ap(),
             l1us=nc.dram_tensor("l1us", [8, 4, 128, 4096], BF16, kind="Internal").ap(),
             l1ds=nc.dram_tensor("l1ds", [8, 4, 128, 4096], BF16, kind="Internal").ap(),
             wqkvb=nc.dram_tensor("wqkvb", [768, C], BF16, kind="Internal").ap(),
             woutb=nc.dram_tensor("woutb", [C, C], BF16, kind="Internal").ap())
    xn_stage = nc.dram_tensor("xn_stage", [C, TLOC], BF16, kind="Internal")
    xnT_all = nc.dram_tensor("xnT_all", [NCORES * C, TLOC], BF16, kind="Internal")
    ctx_stage = nc.dram_tensor("ctx_stage", [2 * D, SEQ], BF16, kind="Internal")
    ctx_all = nc.dram_tensor("ctx_all", [C, SEQ], BF16, kind="Internal")
    q_dram = nc.dram_tensor("q_dram", [2 * D, SEQ], BF16, kind="Internal").ap()

    es = ExitStack()
    k = K(nc, es)
    b = make_bufs(k)
    emit_const_loads(k, b, cn)
    b.pid_ld = k.dma("cn", lambda g: g.dma_start(out=b.pid_sb[0:1, 0:1],
                                                 in_=nc.partition_id_tensor.ap()))
    cwo = k.DL(lambda g: g.dma_start(out=scr.woutb, in_=wo1), [])
    b.castdone["wout"] = ("DL", cwo)
    emit_att2_build(k, b, bo0, 0)
    emit_att2_build(k, b, bo1, 32)
    b.att2_ready = b.att2_guard
    b.sbsb_guard = b.att2_guard
    xl = k.dma("xin", lambda g: g.dma_start(
        out=b.x_sb[:, :].rearrange("p (a c) -> p a c", c=2048),
        in_=x_in.rearrange("(a p) c -> p a c", p=128)), b.att2_guard)
    tt_done = [("xin", xl)] * NTT

    def ln_phase(pg):
        emit_ln_all(k, b, [[tt_done[tt]] + pg for tt in range(NTT)])

    def attn_sublayer(layer, wsrc_a, wo_src):
        pg = phase_boundary(k, b)
        emit_attn_wprep(k, b, layer, wsrc_a, scr)
        ln_phase(pg)
        st = k.dma("xns", lambda g: g.dma_start(
            out=xn_stage.ap().rearrange("(a p) w -> p a w", p=128),
            in_=b.arena[:, A_XNT:A_XNT+16384].rearrange("p (a w) -> p a w", w=1024)),
            b.xnT_done + pg)
        ag1 = k.CC(lambda g: g.collective_compute(
            "AllGather", ALU.bypass, RG, ins=[xn_stage.ap().opt()],
            outs=[xnT_all.ap().opt()]), [("xns", st)] + pg)
        xnT_all_ap = xnT_all.ap()
        slab_src = lambda cc, tb, th: xnT_all_ap[
            2048*tb + 128*cc: 2048*tb + 128*cc + 128,
            512*th: 512*th + 512]
        b.ag_guard = ("CC", k.cnt("CC"))
        pg2 = phase_boundary(k, b)
        emit_attn_qkv(k, b, layer, slab_src, q_dram)
        emit_attn_core(k, b, layer, [brows[layer, 0], brows[layer, 1]],
                       q_dram, ctx_stage.ap())
        pgc = phase_boundary(k, b)
        ag2 = k.CC(lambda g: g.collective_compute(
            "AllGather", ALU.bypass, RG, ins=[ctx_stage.ap().opt()],
            outs=[ctx_all.ap().opt()]), [b.ctx_done] + pgc)
        b.ctxbuf_war = pgc + [("CC", k.cnt("CC"))]
        ctx_all_ap = ctx_all.ap()
        def ctx_loader(k2, b2):
            ent = k2.sem_ent("ctx")
            acc = {}
            for (sn, v) in k2._flat([b2.ctxbuf_war, ("cn", b2.pid_ld)]):
                if v > 0 and v > acc.get(sn, 0):
                    acc[sn] = v
            wl = sorted(acc.items())

            def f(g):
                with g.register() as r:
                    g.reg_load(r, b2.pid_sb[0:1, 0:1])
                    g.reg_mul(r, r, TLOC)
                    off = g.snap(r, min_val=0, max_val=TLOC * (NCORES - 1))
                    src_ap = ctx_all_ap[:, bass.ds(off, TLOC)].rearrange(
                        "(a p) w -> p a w", p=128)
                    dst_ap = b2.arena[:, A_CTX:A_CTX + 16384].rearrange(
                        "p (a w) -> p a w", w=1024)
                    g.dma_start(out=dst_ap, in_=src_ap).then_inc(ent[0], 16)
            k2.dq["gpsimd"].append((wl, f, None, 0, ent[1] + 16))
            ent[1] += 16
            return ("ctx", ent[1])
        emit_attn_outproj(k, b, layer, wo_src, ctx_loader, tt_done,
                          wo_cast=(b.castdone["wout"] if layer == 1 else None))
        phase_boundary(k, b)

    def ffn_sub(kind, w, second=False):
        pg = phase_boundary(k, b)
        ln_phase(pg)
        emit_ffn(k, b, kind, w, tt_done, scr=scr, second=second)
    subs = [lambda: ffn_sub("l0", w0),
            lambda: attn_sublayer(0, wa0, wo0),
            lambda: ffn_sub("l0", w0, second=True),
            lambda: ffn_sub("l1", w1),
            lambda: attn_sublayer(1, wa1, scr.woutb),
            lambda: ffn_sub("l1", w1, second=True)]
    for f in subs[:n_sub]:
        f()

    st_last = k.dma("xout", lambda g: g.dma_start(
        out=x_out.rearrange("(a p) c -> p a c", p=128),
        in_=b.x_sb[:, :].rearrange("p (a c) -> p a c", c=2048)), tt_done)
    k.waiter("G", [("xout", st_last)])
    k.finalize()
    es.close()
    return nc



_L0_KEYS = dict(gate_w="L0_ffn_gate_w", sgw="L0_ffn_shared_gate_w",
                fc1="L0_ffn_expert_fc1", fc2="L0_ffn_expert_fc2",
                proj="L0_ffn_expert_proj", sfc1="L0_ffn_shared_fc1",
                sfc2="L0_ffn_shared_fc2", sproj="L0_ffn_shared_proj")
_L1_KEYS = dict(gate_w="L1_ffn_gate_w", up_w="L1_ffn_up_w", down_w="L1_ffn_down_w",
                gate_scale="L1_ffn_gate_scale", up_scale="L1_ffn_up_scale",
                down_scale="L1_ffn_down_scale")

_CACHE = {}


def _get(name, builder):
    if name not in _CACHE:
        _CACHE[name] = builder()
    return _CACHE[name]


def host_bias_rows(layer, h):
    j = np.arange(SEQ, dtype=np.float64)
    slope = 2.0 ** (-(h + 1) / 2.0)
    if layer == 0:
        row = slope * j * (1.0 / np.sqrt(np.float64(D)))
    else:
        row = (j - (SEQ - 1)) * slope
    hi = row.astype(np.float32).astype(BF)
    lo = (row - hi.astype(np.float64)).astype(np.float32).astype(BF)
    return np.stack([hi, lo])


def run_ffn_step(kind, x_full, inputs, with_xnT=False):
    nc = _get(("ffn", kind, with_xnT), lambda: build_ffn_launch(kind, with_xnT))
    cns = host_consts()
    keys = _L0_KEYS if kind == "l0" else _L1_KEYS
    wmap = {k2: np.ascontiguousarray(np.asarray(inputs[v])) for k2, v in keys.items()}
    if kind == "l0":
        wmap["sgw"] = wmap["sgw"].reshape(1, C)
    in_maps = [dict(x_in=np.ascontiguousarray(x_full[TLOC*c:TLOC*(c+1)]), **wmap, **cns)
               for c in range(NCORES)]
    res = run_bass_kernel_spmd(nc, in_maps, core_ids=list(range(NCORES)))
    x_new = np.concatenate([res.results[c]["x_out"] for c in range(NCORES)], 0)
    if with_xnT:
        xnT = np.concatenate([res.results[c]["xnT_out"] for c in range(NCORES)], 1)
        return x_new, xnT
    return x_new


def run_attn_step(layer, x_full, xnT_full, inputs):
    nc_a = _get(("attn", layer), lambda: build_attn_core_launch(layer))
    cns = host_consts()
    in_maps = []
    for c in range(NCORES):
        m = dict(xnT_full=np.ascontiguousarray(xnT_full), **cns)
        m["cn_brows"] = np.ascontiguousarray(
            np.stack([host_bias_rows(layer, 2*c), host_bias_rows(layer, 2*c+1)]))
        if layer == 0:
            for nm, key in (("wq_s", "L0_attn_Wq"), ("wk_s", "L0_attn_Wk"),
                            ("wv_s", "L0_attn_Wv")):
                m[nm] = np.ascontiguousarray(np.asarray(inputs[key])[:, 256*c:256*c+256])
        else:
            Wqkv = np.asarray(inputs["L1_attn_Wqkv"])
            bq = np.asarray(inputs["L1_attn_b_qkv"])
            rows = np.concatenate([Wqkv[256*c:256*c+256],
                                   Wqkv[2048+256*c:2048+256*c+256],
                                   Wqkv[4096+256*c:4096+256*c+256]], 0)
            m["wqkv_s"] = np.ascontiguousarray(rows)
            m["bqkv_s"] = np.ascontiguousarray(np.concatenate(
                [bq[256*c:256*c+256], bq[2048+256*c:2048+256*c+256],
                 bq[4096+256*c:4096+256*c+256]]))
        in_maps.append(m)
    res = run_bass_kernel_spmd(nc_a, in_maps, core_ids=list(range(NCORES)))
    ctx_full = np.concatenate([res.results[c]["ctx_out"] for c in range(NCORES)], 0)

    nc_o = _get(("outproj", layer), lambda: build_outproj_launch(layer))
    wo = np.ascontiguousarray(np.asarray(
        inputs["L0_attn_Wo" if layer == 0 else "L1_attn_Wout"]))
    b2 = np.ascontiguousarray(np.asarray(
        inputs["L0_attn_bo" if layer == 0 else "L1_attn_b_out"]).reshape(1, C))
    in_maps = [dict(x_in=np.ascontiguousarray(x_full[TLOC*c:TLOC*(c+1)]),
                    ctx_own=np.ascontiguousarray(ctx_full[:, TLOC*c:TLOC*(c+1)]),
                    wo=wo, bias2=b2, **cns)
               for c in range(NCORES)]
    res = run_bass_kernel_spmd(nc_o, in_maps, core_ids=list(range(NCORES)))
    return np.concatenate([res.results[c]["x_out"] for c in range(NCORES)], 0)


def kernel_unfused(**inputs):
    x = np.ascontiguousarray(np.asarray(inputs["L0_attn_x"], np.float32).reshape(SEQ, C))
    x, xnT = run_ffn_step("l0", x, inputs, with_xnT=True)
    x = run_attn_step(0, x, xnT, inputs)
    x = run_ffn_step("l0", x, inputs)
    x, xnT = run_ffn_step("l1", x, inputs, with_xnT=True)
    x = run_attn_step(1, x, xnT, inputs)
    x = run_ffn_step("l1", x, inputs)
    return x.reshape(1, SEQ, C)


def kernel(**inputs, ):
    n_sub = int(inputs.pop("_n_sub", 6))
    x = np.ascontiguousarray(np.asarray(inputs["L0_attn_x"], np.float32).reshape(SEQ, C))
    nc = _get(("fused", n_sub), lambda: build_fused(n_sub))
    cns = host_consts()
    wmap = {}
    for k2, v in _L0_KEYS.items():
        wmap[k2] = np.ascontiguousarray(np.asarray(inputs[v]))
    wmap["sgw"] = wmap["sgw"].reshape(1, C)
    for k2, v in _L1_KEYS.items():
        wmap["l1_" + k2] = np.ascontiguousarray(np.asarray(inputs[v]))
    wmap["wo0"] = np.ascontiguousarray(np.asarray(inputs["L0_attn_Wo"]))
    wmap["wo1"] = np.ascontiguousarray(np.asarray(inputs["L1_attn_Wout"]))
    wmap["bo0"] = np.ascontiguousarray(np.asarray(inputs["L0_attn_bo"]).reshape(1, C))
    wmap["bo1"] = np.ascontiguousarray(np.asarray(inputs["L1_attn_b_out"]).reshape(1, C))
    Wqkv = np.asarray(inputs["L1_attn_Wqkv"])
    bq = np.asarray(inputs["L1_attn_b_qkv"])
    in_maps = []
    for c in range(NCORES):
        m = dict(x_in=np.ascontiguousarray(x[TLOC*c:TLOC*(c+1)]), **wmap, **cns)
        m["cn_brows"] = np.ascontiguousarray(np.stack([
            np.stack([host_bias_rows(0, 2*c), host_bias_rows(0, 2*c+1)]),
            np.stack([host_bias_rows(1, 2*c), host_bias_rows(1, 2*c+1)])]))
        for nm, key in (("wq_s", "L0_attn_Wq"), ("wk_s", "L0_attn_Wk"),
                        ("wv_s", "L0_attn_Wv")):
            m[nm] = np.ascontiguousarray(np.asarray(inputs[key])[:, 256*c:256*c+256])
        m["wqkv_s"] = np.ascontiguousarray(np.concatenate(
            [Wqkv[256*c:256*c+256], Wqkv[2048+256*c:2048+256*c+256],
             Wqkv[4096+256*c:4096+256*c+256]], 0))
        m["bqkv_s"] = np.ascontiguousarray(np.concatenate(
            [bq[256*c:256*c+256], bq[2048+256*c:2048+256*c+256],
             bq[4096+256*c:4096+256*c+256]]))
        in_maps.append(m)
    res = run_bass_kernel_spmd(nc, in_maps, core_ids=list(range(NCORES)))
    out = np.concatenate([res.results[c]["x_out"] for c in range(NCORES)], 0)
    return out.reshape(1, SEQ, C)


# version 07
# speedup vs baseline: 1.0621x; 1.0104x over previous
import numpy as np
import ml_dtypes
from contextlib import ExitStack

import concourse.bass as bass
import concourse.mybir as mybir
from concourse.bass_utils import run_bass_kernel_spmd

F32 = mybir.dt.float32
BF16 = mybir.dt.bfloat16
I32 = mybir.dt.int32
AF = mybir.ActivationFunctionType
ALU = mybir.AluOpType
AX = mybir.AxisListType

SEQ = 8192
C = 2048
NHEAD = 16
D = 128
NCORES = 8
TLOC = SEQ // NCORES
NTT = TLOC // 128
F_MOE = 1408
F_SH = 5632
I_L1 = 8192
EPS = 1e-5
RSQD = float(1.0 / np.sqrt(np.float64(D)))
BF = ml_dtypes.bfloat16

A_XNT = 0
A_WG = 16384
A_WU = 24576
A_H = 32768
A_PJ = 44032
A_SU = 49664
A_RW = 50176
A_SGW = 50304
A_WN0 = 50320
A_Q = 0
A_K = 4096
A_V = 20480
A_WQKV = 37120
A_SLAB = 49408
A_P = 50432
A_PT = 51456
A_WST = 50432
A_CTX = 0
A_WON = 16384
A_WO = 24576

VW = 130
VH = 64 * VW


class K:

    ENGS = ("tensor", "vector", "scalar", "gpsimd", "sync")

    def __init__(self, nc, es):
        self.nc = nc
        self.es = es
        self.blk = es.enter_context(nc.Block())
        self.sems = {}
        for n in ("T", "V", "S", "G", "DL", "DQ", "CC"):
            self.sems[n] = [es.enter_context(nc.semaphore(n)), 0]
        self.dq = {e: [] for e in self.ENGS}

    def cnt(self, name):
        return self.sems[name][1]

    @staticmethod
    def _flat(waits):
        out = []
        for w in waits:
            if w is None:
                continue
            if isinstance(w, list):
                out.extend(t for t in w if t is not None)
            else:
                out.append(w)
        return out

    def _collapse(self, waits):
        acc = {}
        for (sn, v) in self._flat(waits):
            if v > 0 and v > acc.get(sn, 0):
                acc[sn] = v
        return sorted(acc.items())

    def sem_ent(self, name):
        if name not in self.sems:
            self.sems[name] = [self.es.enter_context(self.nc.semaphore(name)), 0]
        return self.sems[name]

    def _emit(self, eng, fn, waits, sem, n):
        ent = self.sem_ent(sem)
        selfwait = ent[1] + n if sem in ("DL", "DQ") else None
        self.dq[eng].append((self._collapse(waits), fn, ent[0], n, selfwait))
        ent[1] += n
        return ent[1]

    def dma(self, ch, fn, waits=(), q="gpsimd"):
        ent = self.sem_ent(ch)
        self.dq[q].append((self._collapse(waits), fn, ent[0], 16, None))
        ent[1] += 16
        return ent[1]

    def T(self, fn, waits=(), n=1):   return self._emit("tensor", fn, waits, "T", n)
    def V(self, fn, waits=(), n=1):   return self._emit("vector", fn, waits, "V", n)
    def S(self, fn, waits=(), n=1):   return self._emit("scalar", fn, waits, "S", n)
    def G(self, fn, waits=(), n=1):   return self._emit("gpsimd", fn, waits, "G", n)
    def DL(self, fn, waits=(), n=16): return self._emit("gpsimd", fn, waits, "DL", n)
    def DQ(self, fn, waits=(), n=16): return self._emit("sync", fn, waits, "DQ", n)
    def CC(self, fn, waits=(), n=1):  return self._emit("gpsimd", fn, waits, "CC", n)

    def DLmulti(self, fns, waits=()):
        ent = self.sems["DL"]
        semh = ent[0]

        total = ent[1] + 16 * len(fns)

        def f(g):
            for fn in fns:
                fn(g).then_inc(semh, 16)
        self.dq["gpsimd"].append((self._collapse(waits), f, None, 0, total))
        ent[1] = total
        return ent[1]

    ENG = dict(T="tensor", V="vector", S="scalar", G="gpsimd", Q="sync")

    def waiter(self, eng, waits):
        self.dq[self.ENG[eng]].append((self._collapse(waits), None, None, 0, None))

    def finalize(self):
        hmap = self.sems
        for eng in self.ENGS:
            items = self.dq[eng]
            if not items:
                continue

            def body(e, items=items):
                for (wl, fn, semh, n, selfwait) in items:
                    for (sn, v) in wl:
                        e.wait_ge(hmap[sn][0], v)
                    if fn is None:
                        continue
                    r = fn(e)
                    if r is not None and semh is not None:
                        r.then_inc(semh, n)
                    if selfwait is not None:
                        sw = hmap["DL"] if (semh is hmap["DL"][0] or semh is None) else hmap["DQ"]
                        e.wait_ge(sw[0], selfwait)
            getattr(self.blk, eng)(body)


class Rot:
    def __init__(self, n):
        self.n = n
        self.free = [[("T", 0)]] * n

    def war(self, i):
        return self.free[i % self.n]

    def done(self, i, sem, val):
        self.free[i % self.n] = [(sem, val)]

    def set_all(self, waits):
        self.free = [list(waits)] * self.n


class NS:
    def __init__(self, **kw):
        self.__dict__.update(kw)


def make_bufs(k):
    nc, es = k.nc, k.es
    b = NS(
        arena=es.enter_context(nc.sbuf_tensor("arena", [128, 53248], BF16)),
        x_sb=es.enter_context(nc.sbuf_tensor("x_sb", [128, 8 * 2048], F32)),
        xn_t=es.enter_context(nc.sbuf_tensor("xn_t", [128, 2048], BF16)),
        sq_t=es.enter_context(nc.sbuf_tensor("sq_t", [128, 2048], F32)),
        stats=es.enter_context(nc.sbuf_tensor("stats", [128, 64], F32)),
        comb=es.enter_context(nc.sbuf_tensor("comb", [128, 80], F32)),
        scl=es.enter_context(nc.sbuf_tensor("scl", [128, 304], F32)),
        bias_sb=es.enter_context(nc.sbuf_tensor("bias_sb", [128, 8192], BF16)),
        att2=es.enter_context(nc.sbuf_tensor("att2", [128, 2048], BF16)),
        ident=es.enter_context(nc.sbuf_tensor("ident", [128, 128], BF16)),
        identF=es.enter_context(nc.sbuf_tensor("identF", [128, 128], F32)),
        ones2=es.enter_context(nc.sbuf_tensor("ones2", [34, 128], BF16)),
        tri=es.enter_context(nc.sbuf_tensor("tri", [128, 4 * 512], BF16)),
        ps=[es.enter_context(nc.psum_tensor("ps%d" % i, [128, 512], F32)) for i in range(4)],
        psE=es.enter_context(nc.psum_tensor("psE", [128, 512], F32)),
        psF=es.enter_context(nc.psum_tensor("psF", [128, 512], F32)),
        psT=es.enter_context(nc.psum_tensor("psT", [128, 2048], BF16)),
        pid_sb=es.enter_context(nc.sbuf_tensor("pid_sb", [1, 2], mybir.dt.uint32)),
    )
    b.psTb = Rot(2)
    b.ps_rot = Rot(4)
    b.psE_rot = Rot(1)
    b.psF_rot = Rot(1)
    b.wg_rot = Rot(2)
    b.wu_rot = Rot(2)
    b.wnat_rot = Rot(2)
    b.pjnat_rot = Rot(2)
    b.pj_war = ("T", 0)
    b.h_war = ("T", 0)
    b.su_war = ("V", 0)
    b.sq_war = ("S", 0)
    b.xn_war = ("T", 0)
    b.xnT_done = ("V", 0)
    b.psT_bid = 0
    b.slab_rot = Rot(2)
    b.p_rot = Rot(2)
    b.pt_rot = Rot(2)
    b.wo_rot = Rot(2)
    b.q_war = ("T", 0)
    b.wst_war = ("T", 0)
    b.won_war = ("T", 0)
    b.ctxbuf_war = ("T", 0)
    b.vones_guard = ("T", 0)
    b.bias_guard = ("T", 0)
    b.att2_guard = ("T", 0)
    b.att2_ready = ("T", 0)
    b.wqkv_ready = []
    b.qkvbias_ready = ("V", 0)
    b.kv_ready = []
    b.ag_guard = None
    b.castdone = {}
    b.sbsb_guard = None
    return b


def phase_boundary(k, b):
    pg = [(s, ent[1]) for s, ent in k.sems.items()]
    for r in (b.psTb, b.ps_rot, b.psE_rot, b.psF_rot, b.wg_rot, b.wu_rot,
              b.wnat_rot, b.pjnat_rot, b.slab_rot, b.p_rot, b.pt_rot, b.wo_rot):
        r.set_all(pg)
    for name in ("pj_war", "h_war", "su_war", "sq_war", "xn_war", "q_war",
                 "wst_war", "won_war", "ctxbuf_war", "vones_guard", "bias_guard"):
        setattr(b, name, list(pg))
    return pg


def AP_(b, c0, w):
    return b.arena[:, c0:c0 + w]



def temove_batch(k, b, srcs, dst_ap, waits, eng, width=128, in_parts=128):
    n = len(srcs)
    pad = width if width % 2 == 0 else width + 1
    assert n * pad <= 1024
    bid = b.psT_bid % 2
    b.psT_bid += 1
    war = b.psTb.war(bid)
    base = 1024 * bid
    t_d = 0
    for i, sfn in enumerate(srcs):
        t_d = k.T(lambda t, i=i, sfn=sfn: t.transpose(
            b.psT[:, base + pad * i: base + pad * i + width],
            sfn(), b.ident[0:in_parts, 0:in_parts]),
            (list(waits) + [war]) if i == 0 else [])
    if pad == width:
        src_ap = b.psT[:, base:base + n * width]
    else:
        src_ap = b.psT[:, base:base + n * pad].rearrange(
            "p (a w) -> p a w", w=pad)[:, :, 0:width]
    if eng == "V":
        e_d = k.V(lambda v: v.tensor_copy(dst_ap, src_ap), [("T", t_d)])
    else:
        e_d = k.S(lambda s: s.copy(dst_ap, src_ap), [("T", t_d)])
    b.psTb.done(bid, eng, e_d)
    return (eng, e_d)



def emit_ln_tile(k, b, tt, x_guards):
    st = b.stats
    SUM, MEAN, M2, E2, VAR, RSTD, NEGB = 0, 1, 2, 3, 4, 5, 6
    xs = b.x_sb[:, 2048 * tt: 2048 * tt + 2048]

    v_sum = k.V(lambda v: v.reduce_sum(st[:, SUM:SUM+1], xs, axis=AX.X), x_guards)
    s_sq = k.S(lambda s: s.activation(b.sq_t[:, :], xs, AF.Square,
                                      accum_out=st[:, M2:M2+1]),
               list(x_guards) + [b.sq_war])
    b.sq_war = [("S", s_sq)]
    v1 = k.V(lambda v: v.tensor_scalar(st[:, MEAN:MEAN+1], st[:, SUM:SUM+1],
                                       1.0 / 2048.0, None, ALU.mult), [("V", v_sum)])
    v2 = k.V(lambda v: v.tensor_scalar(st[:, E2:E2+1], st[:, M2:M2+1],
                                       1.0 / 2048.0, None, ALU.mult),
             [("S", s_sq), ("V", v1)])
    v3 = k.V(lambda v: v.tensor_tensor(st[:, VAR:VAR+1], st[:, MEAN:MEAN+1],
                                       st[:, MEAN:MEAN+1], ALU.mult), [("V", v2)])
    v4 = k.V(lambda v: v.tensor_tensor(st[:, VAR:VAR+1], st[:, E2:E2+1],
                                       st[:, VAR:VAR+1], ALU.subtract), [("V", v3)])
    v4b = k.V(lambda v: v.tensor_scalar(st[:, VAR:VAR+1], st[:, VAR:VAR+1],
                                        float(EPS), None, ALU.add), [("V", v4)])
    s_sd = k.S(lambda s: s.activation(st[:, RSTD:RSTD+1], st[:, VAR:VAR+1],
                                      AF.Sqrt), [("V", v4b)])
    v5 = k.V(lambda v: v.reciprocal(st[:, RSTD:RSTD+1], st[:, RSTD:RSTD+1]),
             [("S", s_sd)])
    v6 = k.V(lambda v: v.tensor_tensor(st[:, NEGB:NEGB+1], st[:, MEAN:MEAN+1],
                                       st[:, RSTD:RSTD+1], ALU.mult), [("V", v5)])
    v7 = k.V(lambda v: v.tensor_scalar(st[:, NEGB:NEGB+1], st[:, NEGB:NEGB+1],
                                       -1.0, None, ALU.mult), [("V", v6)])
    s_xn = k.S(lambda s: s.activation(b.xn_t[:, :], xs, AF.Identity,
                                      bias=st[:, NEGB:NEGB+1],
                                      scale=st[:, RSTD:RSTD+1]),
               [("V", v7), b.xn_war])
    last = ("V", 0)
    for half in range(2):
        c0 = 8 * half
        base = A_XNT + 1024 * c0 + 128 * tt
        dst = b.arena[:, base:base + 8192].rearrange(
            "p (a w) -> p a w", w=1024)[:, :, 0:128]
        srcs = [(lambda cc=cc: b.xn_t[:, 128 * cc:128 * cc + 128])
                for cc in range(c0, c0 + 8)]
        last = temove_batch(k, b, srcs, dst, [("S", s_xn)], "V" if half == 0 else "S")
    b.xn_war = [("T", k.cnt("T"))]
    return last


def emit_ln_all(k, b, x_guards_per_tt):
    evs = []
    for tt in range(NTT):
        evs.append(emit_ln_tile(k, b, tt, x_guards_per_tt[tt]))
    b.xnT_done = [("V", k.cnt("V")), ("S", k.cnt("S"))]
    return b.xnT_done



def _wprep(k, b, src, nat_slot, w_base, w_slot, fj2, rot_w, dq, store_to=None):
    nat_slot = nat_slot % 2
    nat = AP_(b, A_WN0, 2048) if nat_slot == 0 else b.xn_t[:, 0:2048]
    ld = k.dma("wn%d" % nat_slot, lambda g: g.dma_start(out=nat, in_=src),
               [b.wnat_rot.free[nat_slot]])
    nat_ready = ("wn%d" % nat_slot, ld)
    if dq is not None:
        scale_ap, bias_ap, ready = dq
        dqd = k.S(lambda s: s.activation(nat, nat, AF.Identity, bias=bias_ap,
                                         scale=scale_ap), [nat_ready, ready])
        nat_ready = ("S", dqd)
    if store_to is not None:
        k.DL(lambda g: g.dma_start(out=store_to, in_=nat), [nat_ready])
    for half in range(2):
        c0 = 8 * half
        base = w_base + 4096 * w_slot + 256 * c0 + 128 * fj2
        dst = b.arena[:, base:base + 8 * 256].rearrange(
            "p (a w) -> p a w", w=256)[:, :, 0:128]
        srcs = [(lambda cc=cc: nat[:, 128 * cc:128 * cc + 128])
                for cc in range(c0, c0 + 8)]
        temove_batch(k, b, srcs, dst, [nat_ready, rot_w.war(w_slot)],
                     "V" if half == 0 else "S")
    b.wnat_rot.free[nat_slot] = [("T", k.cnt("T"))]
    return [("V", k.cnt("V")), ("S", k.cnt("S"))]


def _wload_slab(k, b, slab_ap, w_base, slot, ch, cast_w):
    last = k.dma(ch, lambda g: g.dma_start(
        out=b.arena[:, w_base + 4096*slot: w_base + 4096*slot + 4096], in_=slab_ap),
        [cast_w, (b.wg_rot if w_base == A_WG else b.wu_rot).war(slot)])
    return [(ch, last)]


def _emit_ffn_group(k, b, tt_done, nfj, scale_mode, gid,
                    gate_src=None, up_src=None, down_src=None,
                    dq_gate=None, dq_up=None, dq_down=None,
                    slab=None, cast_w=None, store=None):
    npair = (nfj + 1) // 2
    for fp in range(npair):
        slot = fp % 2
        if slab is not None:
            ev_g = _wload_slab(k, b, slab.gs[fp], A_WG, slot, "tg%d" % slot, cast_w)
            ev_u = _wload_slab(k, b, slab.us[fp], A_WU, slot, "tu%d" % slot, cast_w)
        else:
            ev_g = ev_u = None
        for fj2 in range(min(2, nfj - 2 * fp)):
            fj = 2 * fp + fj2
            if slab is None:
                ev_g = _wprep(k, b, gate_src(fj), 0, A_WG, slot, fj2,
                              b.wg_rot, dq_gate(fj) if dq_gate else None)
                ev_u = _wprep(k, b, up_src(fj), 1, A_WU, slot, fj2,
                              b.wu_rot, dq_up(fj) if dq_up else None)
            for th in range(2):
                def mmg(t, slot=slot, fj2=fj2, th=th):
                    r = None
                    for cc in range(16):
                        r = t.matmul(b.psE[:, 0:512],
                                     AP_(b, A_WG + 4096*slot + 256*cc + 128*fj2, 128),
                                     AP_(b, A_XNT + 1024*cc + 512*th, 512),
                                     start=(cc == 0), stop=(cc == 15))
                    return r
                g_d = k.T(mmg, ev_g + [b.xnT_done, b.psE_rot.war(0)])
                def mmu(t, slot=slot, fj2=fj2, th=th):
                    r = None
                    for cc in range(16):
                        r = t.matmul(b.psF[:, 0:512],
                                     AP_(b, A_WU + 4096*slot + 256*cc + 128*fj2, 128),
                                     AP_(b, A_XNT + 1024*cc + 512*th, 512),
                                     start=(cc == 0), stop=(cc == 15))
                    return r
                u_d = k.T(mmu, ev_u + [b.xnT_done, b.psF_rot.war(0)])
                hc = A_H + 1024 * fj + 512 * th
                sg = k.S(lambda s, hc=hc: s.activation(AP_(b, hc, 512), b.psE[:, 0:512],
                                                       AF.Silu), [("T", g_d), b.h_war])
                b.psE_rot.done(0, "S", sg)
                ue = k.V(lambda v: v.tensor_copy(AP_(b, A_SU, 512), b.psF[:, 0:512]),
                         [("T", u_d), b.su_war])
                b.psF_rot.done(0, "V", ue)
                hm = k.V(lambda v, hc=hc: v.tensor_tensor(
                    AP_(b, hc, 512), AP_(b, hc, 512), AP_(b, A_SU, 512), ALU.mult),
                    [("V", ue), ("S", sg)])
                b.su_war = [("V", hm)]
        if store is not None:
            sb_ch = "sbL0" if gid[0] in ("e", "s") else "sbL1"
            wev = [("V", k.cnt("V")), ("S", k.cnt("S"))]
            k.dma(sb_ch, lambda g, fp=fp, slot=slot: g.dma_start(
                out=store.gs[fp],
                in_=b.arena[:, A_WG + 4096*slot: A_WG + 4096*slot + 4096]), wev)
            g2 = k.dma(sb_ch, lambda g, fp=fp, slot=slot: g.dma_start(
                out=store.us[fp],
                in_=b.arena[:, A_WU + 4096*slot: A_WU + 4096*slot + 4096]), wev)
            b.wg_rot.done(slot, sb_ch, g2)
            b.wu_rot.done(slot, sb_ch, g2)
        else:
            b.wg_rot.done(slot, "T", k.cnt("T"))
            b.wu_rot.done(slot, "T", k.cnt("T"))
    h_all = ("V", k.cnt("V"))

    fw = 128 * nfj
    for cb in range(4):
        if slab is not None:
            last = k.dma("pjt", lambda g, cb=cb: g.dma_start(
                out=b.arena[:, A_PJ:A_PJ + 512*nfj], in_=slab.ds[cb]),
                [cast_w, b.pj_war])
            pj_ready = [("pjt", last)]
        else:
            for cj in range(4):
                cq = 4 * cb + cj
                nslot = cq % 2
                war = b.wnat_rot.war(cq)
                nat = (AP_(b, A_WN0, fw) if nslot == 0
                       else b.xn_t[:, 0:fw])
                ldd = k.dma("wn%d" % nslot, lambda g, nat=nat, cq=cq: g.dma_start(
                    out=nat, in_=down_src(cq)), [war])
                nat_ready = ("wn%d" % nslot, ldd)
                if dq_down is not None:
                    scale_ap, bias_ap, ready = dq_down(cq)
                    dqd = k.S(lambda s, nat=nat, scale_ap=scale_ap, bias_ap=bias_ap:
                              s.activation(nat, nat, AF.Identity, bias=bias_ap,
                                           scale=scale_ap), [nat_ready, ready])
                    nat_ready = ("S", dqd)
                for f0 in range(0, nfj, 8):
                    nf = min(8, nfj - f0)
                    base = A_PJ + 512 * f0 + 128 * cj
                    dst = b.arena[:, base:base + 512 * nf].rearrange(
                        "p (a w) -> p a w", w=512)[:, :, 0:128] if nf > 1 else AP_(b, base, 128)
                    srcs = [(lambda nat=nat, fj=fj: nat[:, 128 * fj:128 * fj + 128])
                            for fj in range(f0, f0 + nf)]
                    temove_batch(k, b, srcs, dst, [nat_ready, b.pj_war],
                                 "V" if (cj + f0) % 2 == 0 else "S")
                b.wnat_rot.done(cq, "T", k.cnt("T"))
            pj_ready = [("V", k.cnt("V")), ("S", k.cnt("S"))]
            if store is not None:
                sb_ch = "sbL0" if gid[0] in ("e", "s") else "sbL1"
                k.dma(sb_ch, lambda g, cb=cb: g.dma_start(
                    out=store.ds[cb], in_=b.arena[:, A_PJ:A_PJ + 512*nfj]), pj_ready)
        for quad in range(2):
            for fj in range(nfj):
                w = []
                if fj == 0:
                    w = pj_ready + [h_all] + [b.ps_rot.war(i) for i in range(4)]
                def mmd(t, fj=fj, quad=quad, nfj=nfj):
                    r = None
                    for ti in range(4):
                        tt = 4 * quad + ti
                        r = t.matmul(b.ps[ti][:, 0:512],
                                     AP_(b, A_H + 1024 * fj + 128 * tt, 128),
                                     AP_(b, A_PJ + 512 * fj, 512),
                                     start=(fj == 0), stop=(fj == nfj - 1))
                    return r
                k.T(mmd, w)
            mm_fin = ("T", k.cnt("T"))
            for ti in range(4):
                tt = 4 * quad + ti
                xs = b.x_sb[:, 2048 * tt + 512 * cb: 2048 * tt + 512 * cb + 512]
                if scale_mode[0] == "comb":
                    sc_ap = b.comb[:, 8 * tt + scale_mode[1]: 8 * tt + scale_mode[1] + 1]
                    w_extra = b.comb_ready
                elif scale_mode[0] == "sgate":
                    sc_ap = b.comb[:, 64 + tt: 64 + tt + 1]
                    w_extra = b.comb_ready
                else:
                    sc_ap = scale_mode[1]
                    w_extra = []
                u_d = k.V(lambda v, ti=ti, xs=xs, sc_ap=sc_ap: v.scalar_tensor_tensor(
                    xs, b.ps[ti][:, 0:512], sc_ap, xs, op0=ALU.mult, op1=ALU.add),
                    [mm_fin, tt_done[tt], w_extra])
                tt_done[tt] = ("V", u_d)
                b.ps_rot.done(ti, "V", u_d)
        if store is not None and slab is None:
            sb_ch = "sbL0" if gid[0] in ("e", "s") else "sbL1"
            b.pj_war = [("T", k.cnt("T")), (sb_ch, k.cnt(sb_ch))]
        else:
            b.pj_war = [("T", k.cnt("T"))]
    b.h_war = [("T", k.cnt("T"))]


def _emit_router(k, b, wsrc):
    st = b.stats
    ld = k.dma("cn", lambda g: g.dma_start(out=b.xn_t[0:8, 0:2048], in_=wsrc.gate_w),
               [b.xnT_done, b.xn_war])
    srcs = [(lambda cc=cc: b.xn_t[0:8, 128 * cc:128 * cc + 128]) for cc in range(16)]
    temove_batch(k, b, srcs, AP_(b, A_RW, 128), [("cn", ld)], "V", width=8, in_parts=8)
    ld2 = k.dma("cn", lambda g: g.dma_start(out=b.xn_t[0:1, 0:2048], in_=wsrc.sgw),
                [("T", k.cnt("T"))])
    srcs = [(lambda cc=cc: b.xn_t[0:1, 128 * cc:128 * cc + 128]) for cc in range(16)]
    temove_batch(k, b, srcs, AP_(b, A_SGW, 16), [("cn", ld2)], "V", width=1, in_parts=1)
    b.xn_war = [("T", k.cnt("T"))]
    b.wnat_rot.free[1] = [("T", k.cnt("T"))] + list(b.xn_war)
    rw_ready = ("V", k.cnt("V"))

    P0, PM, M1, M2_, L0, SG = 16, 24, 32, 33, 34, 35
    for tt in range(NTT):
        def mm(t, tt=tt):
            r = None
            for cc in range(16):
                t.matmul(b.psE[:, 0:8], AP_(b, A_XNT + 1024 * cc + 128 * tt, 128),
                         AP_(b, A_RW + 8 * cc, 8), start=(cc == 0), stop=(cc == 15))
            for cc in range(16):
                r = t.matmul(b.psE[:, 8:9], AP_(b, A_XNT + 1024 * cc + 128 * tt, 128),
                             AP_(b, A_SGW + cc, 1), start=(cc == 0), stop=(cc == 15))
            return r
        m_d = k.T(mm, [rw_ready, b.xnT_done, b.psE_rot.war(0)])
        v1a = k.V(lambda v: v.reduce_max(st[:, M1:M1+1], b.psE[:, 0:8], axis=AX.X),
                  [("T", m_d)])
        v1 = k.V(lambda v: v.tensor_scalar(st[:, M1:M1+1], st[:, M1:M1+1], -1.0,
                                           None, ALU.mult), [("V", v1a)])
        s1 = k.S(lambda s: s.activation(st[:, P0:P0+8], b.psE[:, 0:8], AF.Exp,
                                        bias=st[:, M1:M1+1],
                                        accum_out=st[:, L0:L0+1]), [("V", v1)])
        v2 = k.V(lambda v: v.reciprocal(st[:, L0:L0+1], st[:, L0:L0+1]), [("S", s1)])
        v3 = k.V(lambda v: v.tensor_scalar(st[:, P0:P0+8], st[:, P0:P0+8],
                                           st[:, L0:L0+1], None, ALU.mult), [("V", v2)])
        v4 = k.V(lambda v: v.reduce_max(st[:, M1:M1+1], st[:, P0:P0+8], axis=AX.X),
                 [("V", v3)])
        v5 = k.V(lambda v: v.scalar_tensor_tensor(
            st[:, PM:PM+8], st[:, P0:P0+8], st[:, M1:M1+1], st[:, P0:P0+8],
            op0=ALU.is_lt, op1=ALU.mult), [("V", v4)])
        v6 = k.V(lambda v: v.reduce_max(st[:, M2_:M2_+1], st[:, PM:PM+8], axis=AX.X),
                 [("V", v5)])
        v7 = k.V(lambda v, tt=tt: v.scalar_tensor_tensor(
            b.comb[:, 8*tt:8*tt+8], st[:, P0:P0+8], st[:, M2_:M2_+1], st[:, P0:P0+8],
            op0=ALU.is_ge, op1=ALU.mult), [("V", v6)])
        v8 = k.V(lambda v, tt=tt: v.tensor_scalar(b.comb[:, 8*tt:8*tt+8],
                                           b.comb[:, 8*tt:8*tt+8], 0.5, None,
                                           ALU.mult), [("V", v7)])
        s2 = k.S(lambda s: s.activation(st[:, SG:SG+1], b.psE[:, 8:9], AF.Sigmoid),
                 [("T", m_d), ("S", s1)])
        v9 = k.V(lambda v, tt=tt: v.tensor_scalar(b.comb[:, 64+tt:64+tt+1], st[:, SG:SG+1],
                                           0.5, None, ALU.mult), [("S", s2), ("V", v8)])
        b.psE_rot.done(0, "V", v9)
    b.comb_ready = [("V", k.cnt("V"))]


def _emit_l1_scales(k, b, wsrc):
    for idx, (src_, col, n) in enumerate([(wsrc.gate_scale, 0, 64),
                                          (wsrc.up_scale, 64, 64),
                                          (wsrc.down_scale, 128, 16)]):
        ld = k.dma("cn", lambda g, src_=src_, n=n: g.dma_start(
            out=b.sq_t[0:n, 0:128], in_=src_.rearrange("(a p) -> a p", p=128)),
            [b.sq_war])
        war = b.psE_rot.war(0)
        t_d = k.T(lambda t, n=n: t.transpose(b.psE[:, 0:n], b.sq_t[0:n, 0:128],
                                             b.identF[0:n, 0:n]),
                  [("cn", ld), war])
        b.sq_war = [("T", t_d)]
        e_d = k.V(lambda v, col=col, n=n: v.tensor_copy(
            b.scl[:, col:col + n], b.psE[:, 0:n]), [("T", t_d)])
        b.psE_rot.done(0, "V", e_d)
    vb = k.V(lambda v: v.tensor_scalar(b.scl[:, 144:288], b.scl[:, 0:144],
                                       -128.0, None, ALU.mult),
             [("V", k.cnt("V"))])
    b.l1s_ready = [("V", vb)]


def _slab_ns(scr, kind, idx):
    if kind == "e":
        return NS(gs=[scr.fc1s[idx, fp] for fp in range(6)],
                  us=[scr.fc2s[idx, fp] for fp in range(6)],
                  ds=[scr.projs[idx, cb] for cb in range(4)])
    if kind == "s":
        return NS(gs=[scr.sfc1s[idx, fp] for fp in range(6)],
                  us=[scr.sfc2s[idx, fp] for fp in range(6)],
                  ds=[scr.sprojs[idx, cb] for cb in range(4)])
    return NS(gs=[scr.l1gs[idx, fp] for fp in range(4)],
              us=[scr.l1us[idx, fp] for fp in range(4)],
              ds=[scr.l1ds[idx, cb] for cb in range(4)])


def emit_ffn(k, b, kind, wsrc, tt_done, scr=None, second=False):
    if kind == "l0":
        _emit_router(k, b, wsrc)
        for e in range(8):
            sl = _slab_ns(scr, "e", e) if scr else None
            _emit_ffn_group(
                k, b, tt_done, nfj=11, scale_mode=("comb", e), gid=("e", e),
                gate_src=lambda fj, e=e: wsrc.fc1[e, 128*fj:128*fj+128, :],
                up_src=lambda fj, e=e: wsrc.fc2[e, 128*fj:128*fj+128, :],
                down_src=lambda cq, e=e: wsrc.proj[e, 128*cq:128*cq+128, :],
                slab=sl if second else None,
                cast_w=b.castdone.get("l0") if second else None,
                store=sl if (scr and not second) else None)
        for gi in range(4):
            sl = _slab_ns(scr, "s", gi) if scr else None
            fj0 = 11 * gi
            _emit_ffn_group(
                k, b, tt_done, nfj=11, scale_mode=("sgate", None), gid=("s", gi),
                gate_src=lambda fj, fj0=fj0: wsrc.sfc1[128*(fj0+fj):128*(fj0+fj)+128, :],
                up_src=lambda fj, fj0=fj0: wsrc.sfc2[128*(fj0+fj):128*(fj0+fj)+128, :],
                down_src=lambda cq, fj0=fj0: wsrc.sproj[128*cq:128*cq+128,
                                                        128*fj0:128*fj0+1408],
                slab=sl if second else None,
                cast_w=b.castdone.get("l0") if second else None,
                store=sl if (scr and not second) else None)
        if scr and not second:
            b.castdone["l0"] = ("sbL0", k.cnt("sbL0"))
    else:
        b.wnat_rot.free[1] = list(b.wnat_rot.free[1]) + list(b.xn_war)
        if not second:
            _emit_l1_scales(k, b, wsrc)
        for gi in range(8):
            sl = _slab_ns(scr, "l1", gi) if scr else None
            fj0 = 8 * gi
            def dqg(fj, fj0=fj0):
                i2 = fj0 + fj
                return (b.scl[:, i2:i2+1], b.scl[:, 144+i2:144+i2+1], b.l1s_ready)
            def dqu(fj, fj0=fj0):
                i2 = fj0 + fj
                return (b.scl[:, 64+i2:64+i2+1], b.scl[:, 208+i2:208+i2+1], b.l1s_ready)
            def dqd(cq):
                return (b.scl[:, 128+cq:128+cq+1], b.scl[:, 272+cq:272+cq+1], b.l1s_ready)
            _emit_ffn_group(
                k, b, tt_done, nfj=8, scale_mode=("const", 0.5), gid=("l1", gi),
                gate_src=lambda fj, fj0=fj0: wsrc.gate_w[128*(fj0+fj):128*(fj0+fj)+128, :],
                up_src=lambda fj, fj0=fj0: wsrc.up_w[128*(fj0+fj):128*(fj0+fj)+128, :],
                down_src=lambda cq, fj0=fj0: wsrc.down_w[128*cq:128*cq+128,
                                                         128*fj0:128*fj0+1024],
                dq_gate=None if second else dqg, dq_up=None if second else dqu,
                dq_down=None if second else dqd,
                slab=sl if second else None,
                cast_w=b.castdone.get("l1") if second else None,
                store=sl if (scr and not second) else None)
        if scr and not second:
            b.castdone["l1"] = ("sbL1", k.cnt("sbL1"))
    b.xn_war = [("T", k.cnt("T"))] + list(b.xn_war)
    return tt_done



def emit_att2_build(k, b, bias_dram, pbase):
    l1 = k.dma("cn", lambda g: g.dma_start(out=b.x_sb[0:1, 0:2048], in_=bias_dram),
               [b.att2_guard])
    s1 = k.S(lambda s: s.activation(b.xn_t[0:1, 0:2048], b.x_sb[0:1, 0:2048],
                                    AF.Identity), [("cn", l1), b.xn_war])
    l2 = k.dma("cn", lambda g: g.dma_start(out=b.att2[pbase:pbase+1, :],
                                    in_=b.xn_t[0:1, 0:2048]), [("S", s1)])
    v1 = k.V(lambda v: v.tensor_copy(b.x_sb[0:1, 2048:4096], b.xn_t[0:1, 0:2048]),
             [("S", s1)])
    v2 = k.V(lambda v: v.tensor_tensor(b.x_sb[0:1, 0:2048], b.x_sb[0:1, 0:2048],
                                       b.x_sb[0:1, 2048:4096], ALU.subtract),
             [("V", v1)])
    s2 = k.S(lambda s: s.activation(b.xn_t[0:1, 0:2048], b.x_sb[0:1, 0:2048],
                                    AF.Identity), [("V", v2), ("cn", l2)])
    l3 = k.dma("cn", lambda g: g.dma_start(out=b.att2[pbase+1:pbase+2, :],
                                    in_=b.xn_t[0:1, 0:2048]), [("S", s2)])
    b.att2_guard = [("cn", l3)]
    b.xn_war = [("S", s2)]


def emit_attn_wprep(k, b, layer, wsrc, scr=None):
    if layer == 0:
        last = 0
        for j2, wv in enumerate([wsrc.wq_s, wsrc.wk_s, wsrc.wv_s]):
            for cc in range(16):
                dst = b.arena[:, A_WQKV + 768*cc + 256*j2:
                              A_WQKV + 768*cc + 256*j2 + 256]
                last = k.dma("wq", lambda g, wv=wv, cc=cc, dst=dst: g.dma_start(
                    out=dst, in_=wv[128*cc:128*cc+128, :]), [b.wst_war])
        b.wqkv_ready = [("wq", last)]
    else:
        cw = k.DL(lambda g: g.dma_start(out=scr.wqkvb, in_=wsrc.wqkv_s), [])
        last = 0
        for cc in range(16):
            last = k.dma("wq", lambda q, cc=cc: q.dma_start_transpose(
                b.arena[:, A_WQKV + 768*cc: A_WQKV + 768*cc + 768],
                scr.wqkvb[0:768, 128*cc:128*cc+128]),
                ([("DL", cw), b.sbsb_guard, b.wst_war] if cc == 0 else []), q="sync")
        b.wqkv_ready = [("wq", last)]
        bl = 0
        for m in range(6):
            bl = k.dma("cn", lambda g, m=m: g.dma_start(
                out=b.comb[:, 72+m:72+m+1],
                in_=wsrc.bqkv_s[128*m:128*m+128].rearrange("(p a) -> p a", a=1)),
                [b.wst_war])
        vb = k.V(lambda v: v.tensor_scalar(b.comb[:, 78:80], b.comb[:, 72:74],
                                           RSQD, None, ALU.mult), [("cn", bl)])
        b.qkvbias_ready = [("V", vb)]


def emit_attn_qkv(k, b, layer, slab_src, q_dram):
    vo = []
    for h2 in range(2):
        ap = b.arena[:, A_V + VH*h2: A_V + VH*h2 + VH].rearrange(
            "p (a w) -> p a w", w=VW)[:, :, 128:129]
        vo.append(k.V(lambda v, ap=ap: v.memset(ap, 1.0), [b.vones_guard]))
    v_ones = ("V", vo[-1])
    qkv_bias = [b.qkvbias_ready] if layer == 1 else []

    for tb in range(8):
        for th in range(2):
            mm_last = 0
            for cc in range(16):
                slot = cc % 2
                ldw = k.dma("sl%d" % slot, lambda q, cc=cc, tb=tb, th=th, slot=slot:
                            q.dma_start(out=AP_(b, A_SLAB + 512*slot, 512),
                                        in_=slab_src(cc, tb, th)),
                            [b.slab_rot.war(slot), b.ag_guard], q="sync")
                wts = b.wqkv_ready + [("sl%d" % slot, ldw)]
                if cc == 0:
                    wts = wts + [b.ps_rot.war(i) for i in range(4)] +                           [b.psE_rot.war(0), b.psF_rot.war(0)]
                def mm(t, cc=cc, slot=slot):
                    r = None
                    for h2 in range(2):
                        t.matmul(b.ps[h2][:, 0:512],
                                 AP_(b, A_WQKV + 768*cc + 128*(2+h2), 128),
                                 AP_(b, A_SLAB + 512*slot, 512),
                                 start=(cc == 0), stop=(cc == 15))
                        t.matmul(b.ps[2+h2][:, 0:512],
                                 AP_(b, A_WQKV + 768*cc + 128*(4+h2), 128),
                                 AP_(b, A_SLAB + 512*slot, 512),
                                 start=(cc == 0), stop=(cc == 15))
                        r = t.matmul((b.psE if h2 == 0 else b.psF)[:, 0:512],
                                     AP_(b, A_WQKV + 768*cc + 128*h2, 128),
                                     AP_(b, A_SLAB + 512*slot, 512),
                                     start=(cc == 0), stop=(cc == 15))
                    return r
                mm_last = k.T(mm, wts + qkv_bias)
                b.slab_rot.done(slot, "T", mm_last)
            for h2 in range(2):
                kdst = AP_(b, A_K + 8192*h2 + 1024*tb + 512*th, 512)
                if layer == 1:
                    ek = k.S(lambda s, h2=h2, kdst=kdst: s.activation(
                        kdst, b.ps[h2][:, 0:512], AF.Identity,
                        bias=b.comb[:, 74+h2:74+h2+1]), [("T", mm_last)])
                else:
                    ek = k.S(lambda s, h2=h2, kdst=kdst: s.copy(
                        kdst, b.ps[h2][:, 0:512]), [("T", mm_last)])
                b.ps_rot.done(h2, "S", ek)
                ps_ = b.p_rot.war(0)
                stg = AP_(b, A_P, 512)
                if layer == 1:
                    ev = k.S(lambda s, h2=h2, stg=stg: s.activation(
                        stg, b.ps[2+h2][:, 0:512], AF.Identity,
                        bias=b.comb[:, 76+h2:76+h2+1]), [("T", mm_last), ps_])
                else:
                    ev = k.S(lambda s, h2=h2, stg=stg: s.copy(
                        stg, b.ps[2+h2][:, 0:512]), [("T", mm_last), ps_])
                b.ps_rot.done(2+h2, "S", ev)
                tile0 = 8*tb + 4*th
                base = A_V + VH*h2 + VW*tile0
                dst = b.arena[:, base:base + 4*VW].rearrange(
                    "p (a w) -> p a w", w=VW)[:, :, 0:128]
                srcs = [(lambda i=i, stg=stg: stg[:, 128*i:128*i+128])
                        for i in range(4)]
                temove_batch(k, b, srcs, dst, [("S", ev), v_ones], "V")
                b.p_rot.done(0, "T", k.cnt("T"))
                pq = b.pt_rot.war(0)
                qstg = AP_(b, A_PT, 512)
                qps = b.psE if h2 == 0 else b.psF
                if layer == 1:
                    eq = k.S(lambda s, qstg=qstg, qps=qps, h2=h2: s.activation(
                        qstg, qps[:, 0:512], AF.Identity, scale=RSQD,
                        bias=b.comb[:, 78+h2:78+h2+1]), [("T", mm_last), pq])
                else:
                    eq = k.S(lambda s, qstg=qstg, qps=qps: s.activation(
                        qstg, qps[:, 0:512], AF.Identity, scale=RSQD),
                        [("T", mm_last), pq])
                (b.psE_rot if h2 == 0 else b.psF_rot).done(0, "S", eq)
                dq = k.dma("qst", lambda g, h2=h2, tb=tb, th=th, qstg=qstg: g.dma_start(
                    out=q_dram[128*h2:128*h2+128, 1024*tb+512*th:1024*tb+512*th+512],
                    in_=qstg), [("S", eq)])
                b.pt_rot.done(0, "qst", dq)
    b.kv_ready = [("V", k.cnt("V")), ("S", k.cnt("S")), ("qst", k.cnt("qst"))]


def emit_attn_core(k, b, layer, bias_rows, q_dram, ctx_dram):
    bl = k.dma("cn", lambda g: g.dma_start(out=b.bias_sb[0:2, :], in_=bias_rows[0]),
               [b.bias_guard])
    bl = k.dma("cn", lambda g: g.dma_start(out=b.bias_sb[32:34, :], in_=bias_rows[1]),
               [b.bias_guard])
    bias_ready = ("cn", bl)
    ctxi = 0
    for quarter in range(2):
        for h2 in range(2):
            lq = k.dma("qld", lambda q, h2=h2, quarter=quarter: q.dma_start(
                out=AP_(b, A_Q, 4096),
                in_=q_dram[128*h2:128*h2+128, 4096*quarter:4096*quarter+4096]),
                [b.q_war] + b.kv_ready, q="sync")
            q_ready = ("qld", lq)
            for ql in range(32):
                qt = 32 * quarter + ql
                nkc = qt // 4 + 1
                rb = 32 * h2
                red = 0
                for i in range(nkc):
                    bank = b.psE if i % 2 == 0 else b.psF
                    rot = b.psE_rot if i % 2 == 0 else b.psF_rot
                    dvar = qt - 4 * i if i == nkc - 1 else None
                    def mma(t, i=i, bank=bank, ql=ql, rb=rb, dvar=dvar, h2=h2):
                        t.matmul(bank[:, 0:512], b.ones2[rb:rb+2, 0:128],
                                 b.bias_sb[rb:rb+2, 512*i:512*i+512],
                                 start=True, stop=False)
                        if dvar is not None:
                            t.matmul(bank[:, 0:512], b.ident[:, :],
                                     b.tri[:, 512*dvar:512*dvar+512],
                                     start=False, stop=False)
                        return t.matmul(bank[:, 0:512],
                                        AP_(b, A_Q + 128*ql, 128),
                                        AP_(b, A_K + 8192*h2 + 512*i, 512),
                                        start=False, stop=True)
                    m_d = k.T(mma, [q_ready, bias_ready, rot.war(0)])
                    red = k.V(lambda v, i=i, bank=bank: v.reduce_max(
                        b.stats[:, 40+i:40+i+1], bank[:, 0:512], axis=AX.X),
                        [("T", m_d)])
                    rot.done(0, "V", red)
                r1 = k.V(lambda v, nkc=nkc: v.reduce_max(
                    b.stats[:, 56:57], b.stats[:, 40:40+nkc], axis=AX.X),
                    [("V", red)])
                r2 = k.V(lambda v: v.tensor_scalar(
                    b.stats[:, 57:58], b.stats[:, 56:57], -1.0, None, ALU.mult),
                    [("V", r1)])
                negm = ("V", r2)
                pv_last = 0
                for i in range(nkc):
                    bank = b.psE if i % 2 == 0 else b.psF
                    rot = b.psE_rot if i % 2 == 0 else b.psF_rot
                    dvar = qt - 4 * i if i == nkc - 1 else None
                    def mmb(t, i=i, bank=bank, ql=ql, rb=rb, dvar=dvar, h2=h2):
                        t.matmul(bank[:, 0:512], b.ones2[rb:rb+2, 0:128],
                                 b.bias_sb[rb:rb+2, 512*i:512*i+512],
                                 start=True, stop=False)
                        if dvar is not None:
                            t.matmul(bank[:, 0:512], b.ident[:, :],
                                     b.tri[:, 512*dvar:512*dvar+512],
                                     start=False, stop=False)
                        return t.matmul(bank[:, 0:512],
                                        AP_(b, A_Q + 128*ql, 128),
                                        AP_(b, A_K + 8192*h2 + 512*i, 512),
                                        start=False, stop=True)
                    m_d = k.T(mmb, [rot.war(0)])
                    pslot = i % 2
                    pw = b.p_rot.war(pslot)
                    ex = k.S(lambda s, bank=bank, pslot=pslot: s.activation(
                        AP_(b, A_P + 512*pslot, 512), bank[:, 0:512], AF.Exp,
                        bias=b.stats[:, 57:58]), [("T", m_d), negm, pw])
                    rot.done(0, "S", ex)
                    p_ready = ("S", ex)
                    ptw = b.pt_rot.war(pslot)
                    srcs = [(lambda j=j, pslot=pslot: AP_(b, A_P + 512*pslot + 128*j, 128))
                            for j in range(4)]
                    ev = temove_batch(k, b, srcs, AP_(b, A_PT + 512*pslot, 512),
                                      [p_ready, ptw], "V")
                    b.p_rot.done(pslot, "T", k.cnt("T"))
                    def mpv(t, i=i, pslot=pslot, h2=h2, ci=ctxi, nkc=nkc):
                        r = None
                        for j in range(4):
                            tkb = 4*i + j
                            r = t.matmul(b.ps[ci][:, 0:129],
                                         AP_(b, A_PT + 512*pslot + 128*j, 128),
                                         b.arena[:, A_V + VH*h2 + VW*tkb:
                                                 A_V + VH*h2 + VW*tkb + 129],
                                         start=(i == 0 and j == 0),
                                         stop=(i == nkc - 1 and j == 3))
                        return r
                    w = [ev]
                    if i == 0:
                        w.append(b.ps_rot.war(ctxi))
                    pv_last = k.T(mpv, w)
                    b.pt_rot.done(pslot, "T", pv_last)
                rl = k.V(lambda v, ci=ctxi: v.reciprocal(
                    b.stats[:, 58:59], b.ps[ci][:, 128:129]), [("T", pv_last)])
                pw = b.p_rot.war(0)
                fs = k.S(lambda s, ci=ctxi: s.activation(
                    AP_(b, A_P, 128), b.ps[ci][:, 0:128], AF.Identity,
                    scale=b.stats[:, 58:59]), [("V", rl), pw])
                b.ps_rot.done(ctxi, "S", fs)
                ptw = b.pt_rot.war(0)
                ev = temove_batch(k, b, [lambda: AP_(b, A_P, 128)],
                                  AP_(b, A_PT, 128), [("S", fs), ptw], "V")
                b.p_rot.done(0, "T", k.cnt("T"))
                dc = k.dma("cst", lambda g, h2=h2, qt=qt: g.dma_start(
                    out=ctx_dram[128*h2:128*h2+128, 128*qt:128*qt+128],
                    in_=AP_(b, A_PT, 128)), [ev])
                b.pt_rot.done(0, "cst", dc)
                ctxi = (ctxi + 1) % 4
            b.q_war = [("T", k.cnt("T"))]
    b.ctx_done = [("cst", k.cnt("cst"))]


def emit_attn_outproj(k, b, layer, wo_src, ctx_loader, tt_done, wo_cast=None):
    ctx_ready = ctx_loader(k, b)
    rb2 = 32 * layer
    for ob in range(4):
        for quad in range(2):
            for dc in range(16):
                slot = dc % 2
                wdst = AP_(b, A_WO + 512*slot, 512)
                if layer == 0:
                    lw = k.dma("wo%d" % slot, lambda g, ob=ob, dc=dc, wdst=wdst:
                               g.dma_start(out=wdst,
                                           in_=wo_src[128*dc:128*dc+128, 512*ob:512*ob+512]),
                               [b.wo_rot.war(slot)])
                    w_ready = [("wo%d" % slot, lw)]
                else:
                    lw = k.dma("wo%d" % slot, lambda q, ob=ob, dc=dc, wdst=wdst:
                               q.dma_start_transpose(
                                   wdst, wo_src[512*ob:512*ob+512, 128*dc:128*dc+128]),
                               [b.wo_rot.war(slot), b.sbsb_guard, wo_cast], q="sync")
                    w_ready = [("wo%d" % slot, lw)]
                wts = w_ready + [ctx_ready]
                if dc == 0:
                    wts += [b.ps_rot.war(i) for i in range(4)]
                def mmo(t, ob=ob, quad=quad, dc=dc, slot=slot, rb2=rb2):
                    r = None
                    if dc == 0:
                        for ti in range(4):
                            t.matmul(b.ps[ti][:, 0:512], b.ones2[rb2:rb2+2, 0:128],
                                     b.att2[rb2:rb2+2, 512*ob:512*ob+512],
                                     start=True, stop=False)
                    for ti in range(4):
                        tt = 4*quad + ti
                        r = t.matmul(b.ps[ti][:, 0:512],
                                     AP_(b, A_CTX + 1024*dc + 128*tt, 128),
                                     AP_(b, A_WO + 512*slot, 512),
                                     start=False, stop=(dc == 15))
                    return r
                mo = k.T(mmo, wts + [b.att2_ready])
                b.wo_rot.done(slot, "T", mo)
            mm_fin = ("T", k.cnt("T"))
            for ti in range(4):
                tt = 4*quad + ti
                xs = b.x_sb[:, 2048*tt + 512*ob: 2048*tt + 512*ob + 512]
                u_d = k.V(lambda v, ti=ti, xs=xs: v.tensor_tensor(
                    xs, xs, b.ps[ti][:, 0:512], ALU.add),
                    [mm_fin, tt_done[tt]])
                tt_done[tt] = ("V", u_d)
                b.ps_rot.done(ti, "V", u_d)
    b.ctxbuf_war = [("T", k.cnt("T"))]
    return tt_done



def host_consts():
    ident = np.eye(128, dtype=BF)
    ones2 = np.ones((34, 128), dtype=BF)
    tri = np.zeros((128, 4 * 512), dtype=BF)
    for d in range(4):
        i = np.arange(128)[:, None]
        jj = np.arange(512)[None, :]
        tri[:, 512*d:512*d+512] = np.where((jj - i) <= 128 * d, 0.0, -30000.0).astype(BF)
    return dict(cn_ident=ident, cn_ones2=ones2, cn_tri=tri,
                cn_identF=np.eye(128, dtype=np.float32))


def emit_const_loads(k, b, cn):
    l1 = k.DL(lambda g: g.dma_start(out=b.ident[:, :], in_=cn.ident))
    l2 = k.DL(lambda g: g.dma_start(out=b.ones2[:, :], in_=cn.ones2))
    l3 = k.DL(lambda g: g.dma_start(out=b.tri[:, :], in_=cn.tri))
    l4 = k.DL(lambda g: g.dma_start(out=b.identF[:, :], in_=cn.identF))
    return ("DL", l4)



def build_ffn_launch(kind, with_xnT=False):
    nc = bass.Bass("TRN2", target_bir_lowering=False)
    x_in = nc.dram_tensor("x_in", [TLOC, C], F32, kind="ExternalInput").ap()
    x_out = nc.dram_tensor("x_out", [TLOC, C], F32, kind="ExternalOutput").ap()
    xnT_out = (nc.dram_tensor("xnT_out", [C, TLOC], BF16, kind="ExternalOutput").ap()
               if with_xnT else None)
    cn = NS(ident=nc.dram_tensor("cn_ident", [128, 128], BF16, kind="ExternalInput").ap(),
            ones2=nc.dram_tensor("cn_ones2", [34, 128], BF16, kind="ExternalInput").ap(),
            tri=nc.dram_tensor("cn_tri", [128, 2048], BF16, kind="ExternalInput").ap(),
            identF=nc.dram_tensor("cn_identF", [128, 128], F32, kind="ExternalInput").ap())
    if kind == "l0":
        wsrc = NS(
            gate_w=nc.dram_tensor("gate_w", [8, C], F32, kind="ExternalInput").ap(),
            sgw=nc.dram_tensor("sgw", [1, C], F32, kind="ExternalInput").ap(),
            fc1=nc.dram_tensor("fc1", [8, F_MOE, C], F32, kind="ExternalInput").ap(),
            fc2=nc.dram_tensor("fc2", [8, F_MOE, C], F32, kind="ExternalInput").ap(),
            proj=nc.dram_tensor("proj", [8, C, F_MOE], F32, kind="ExternalInput").ap(),
            sfc1=nc.dram_tensor("sfc1", [F_SH, C], F32, kind="ExternalInput").ap(),
            sfc2=nc.dram_tensor("sfc2", [F_SH, C], F32, kind="ExternalInput").ap(),
            sproj=nc.dram_tensor("sproj", [C, F_SH], F32, kind="ExternalInput").ap(),
        )
    else:
        wsrc = NS(
            gate_w=nc.dram_tensor("gate_w", [I_L1, C], I32, kind="ExternalInput").ap(),
            up_w=nc.dram_tensor("up_w", [I_L1, C], I32, kind="ExternalInput").ap(),
            down_w=nc.dram_tensor("down_w", [C, I_L1], I32, kind="ExternalInput").ap(),
            gate_scale=nc.dram_tensor("gate_scale", [I_L1], F32, kind="ExternalInput").ap(),
            up_scale=nc.dram_tensor("up_scale", [I_L1], F32, kind="ExternalInput").ap(),
            down_scale=nc.dram_tensor("down_scale", [C], F32, kind="ExternalInput").ap(),
        )
    es = ExitStack()
    k = K(nc, es)
    b = make_bufs(k)
    emit_const_loads(k, b, cn)
    b.sbsb_guard = ("T", 0)
    xl = k.dma("xin", lambda g: g.dma_start(
        out=b.x_sb[:, :].rearrange("p (a c) -> p a c", c=2048),
        in_=x_in.rearrange("(a p) c -> p a c", p=128)))
    tt_done = [("xin", xl)] * NTT
    for tt in range(NTT):
        emit_ln_tile(k, b, tt, [("xin", xl)])
    b.xnT_done = [("V", k.cnt("V")), ("S", k.cnt("S"))]
    tt_done = emit_ffn(k, b, kind, wsrc, tt_done)
    st_last = k.dma("xout", lambda g: g.dma_start(
        out=x_out.rearrange("(a p) c -> p a c", p=128),
        in_=b.x_sb[:, :].rearrange("p (a c) -> p a c", c=2048)), tt_done)
    if with_xnT:
        tguard = ("T", k.cnt("T"))
        for tt in range(NTT):
            emit_ln_tile(k, b, tt, [tt_done[tt], tguard])
        xnT_done = [("V", k.cnt("V")), ("S", k.cnt("S"))]
        st2 = k.dma("xno", lambda g: g.dma_start(
            out=xnT_out.rearrange("(a p) w -> p a w", p=128),
            in_=b.arena[:, A_XNT:A_XNT+16384].rearrange("p (a w) -> p a w", w=1024)),
            xnT_done)
        k.waiter("G", [("xout", st_last), ("xno", st2)])
    else:
        k.waiter("G", [("xout", st_last)])
    k.finalize()
    es.close()
    return nc


def build_attn_core_launch(layer):
    nc = bass.Bass("TRN2", target_bir_lowering=False)
    xnT_full = nc.dram_tensor("xnT_full", [C, SEQ], BF16, kind="ExternalInput").ap()
    brows = nc.dram_tensor("cn_brows", [2, 2, SEQ], BF16, kind="ExternalInput").ap()
    ctx_out = nc.dram_tensor("ctx_out", [2 * D, SEQ], BF16, kind="ExternalOutput").ap()
    cn = NS(ident=nc.dram_tensor("cn_ident", [128, 128], BF16, kind="ExternalInput").ap(),
            ones2=nc.dram_tensor("cn_ones2", [34, 128], BF16, kind="ExternalInput").ap(),
            tri=nc.dram_tensor("cn_tri", [128, 2048], BF16, kind="ExternalInput").ap(),
            identF=nc.dram_tensor("cn_identF", [128, 128], F32, kind="ExternalInput").ap())
    if layer == 0:
        wsrc = NS(wq_s=nc.dram_tensor("wq_s", [C, 256], F32, kind="ExternalInput").ap(),
                  wk_s=nc.dram_tensor("wk_s", [C, 256], F32, kind="ExternalInput").ap(),
                  wv_s=nc.dram_tensor("wv_s", [C, 256], F32, kind="ExternalInput").ap())
    else:
        wsrc = NS(wqkv_s=nc.dram_tensor("wqkv_s", [768, C], F32, kind="ExternalInput").ap(),
                  bqkv_s=nc.dram_tensor("bqkv_s", [768], F32, kind="ExternalInput").ap())
    q_dram = nc.dram_tensor("q_dram", [2 * D, SEQ], BF16, kind="Internal").ap()
    scr = NS(wqkvb=nc.dram_tensor("wqkvb", [768, C], BF16, kind="Internal").ap()) \
        if layer == 1 else None
    es = ExitStack()
    k = K(nc, es)
    b = make_bufs(k)
    emit_const_loads(k, b, cn)
    b.sbsb_guard = ("T", 0)
    emit_attn_wprep(k, b, layer, wsrc, scr)
    slab_src = lambda cc, tb, th: xnT_full[128*cc:128*cc+128,
                                           1024*tb+512*th:1024*tb+512*th+512]
    emit_attn_qkv(k, b, layer, slab_src, q_dram)
    emit_attn_core(k, b, layer, [brows[0], brows[1]], q_dram, ctx_out)
    k.waiter("G", [b.ctx_done])
    es.close()
    return nc


def build_outproj_launch(layer):
    nc = bass.Bass("TRN2", target_bir_lowering=False)
    x_in = nc.dram_tensor("x_in", [TLOC, C], F32, kind="ExternalInput").ap()
    x_out = nc.dram_tensor("x_out", [TLOC, C], F32, kind="ExternalOutput").ap()
    ctx_own = nc.dram_tensor("ctx_own", [C, TLOC], BF16, kind="ExternalInput").ap()
    wo = nc.dram_tensor("wo", [C, C], F32, kind="ExternalInput").ap()
    bias2 = nc.dram_tensor("bias2", [1, C], F32, kind="ExternalInput").ap()
    cn = NS(ident=nc.dram_tensor("cn_ident", [128, 128], BF16, kind="ExternalInput").ap(),
            ones2=nc.dram_tensor("cn_ones2", [34, 128], BF16, kind="ExternalInput").ap(),
            tri=nc.dram_tensor("cn_tri", [128, 2048], BF16, kind="ExternalInput").ap(),
            identF=nc.dram_tensor("cn_identF", [128, 128], F32, kind="ExternalInput").ap())
    woutb = (nc.dram_tensor("woutb", [C, C], BF16, kind="Internal").ap()
             if layer == 1 else None)
    es = ExitStack()
    k = K(nc, es)
    b = make_bufs(k)
    emit_const_loads(k, b, cn)
    wo_cast = None
    if layer == 1:
        cwo = k.DL(lambda g: g.dma_start(out=woutb, in_=wo), [])
        wo_cast = ("DL", cwo)
    emit_att2_build(k, b, bias2, 32 * layer)
    b.att2_ready = b.att2_guard
    b.sbsb_guard = b.att2_guard
    xl = k.dma("xin", lambda g: g.dma_start(
        out=b.x_sb[:, :].rearrange("p (a c) -> p a c", c=2048),
        in_=x_in.rearrange("(a p) c -> p a c", p=128)), [b.att2_guard])
    tt_done = [("xin", xl)] * NTT
    def ctx_loader(k2, b2):
        ld = k2.dma("ctx", lambda g: g.dma_start(
            out=b2.arena[:, A_CTX:A_CTX+16384].rearrange("p (a w) -> p a w", w=1024),
            in_=ctx_own.rearrange("(a p) w -> p a w", p=128)), [b2.ctxbuf_war])
        return ("ctx", ld)
    tt_done = emit_attn_outproj(k, b, layer, woutb if layer == 1 else wo,
                                ctx_loader, tt_done, wo_cast=wo_cast)
    st_last = k.dma("xout", lambda g: g.dma_start(
        out=x_out.rearrange("(a p) c -> p a c", p=128),
        in_=b.x_sb[:, :].rearrange("p (a c) -> p a c", c=2048)), tt_done)
    k.waiter("G", [("xout", st_last)])
    k.finalize()
    es.close()
    return nc


def build_fused(n_sub=6):
    nc = bass.Bass("TRN2", target_bir_lowering=False)
    RG = [list(range(NCORES))]
    x_in = nc.dram_tensor("x_in", [TLOC, C], F32, kind="ExternalInput").ap()
    x_out = nc.dram_tensor("x_out", [TLOC, C], F32, kind="ExternalOutput").ap()
    cn = NS(ident=nc.dram_tensor("cn_ident", [128, 128], BF16, kind="ExternalInput").ap(),
            ones2=nc.dram_tensor("cn_ones2", [34, 128], BF16, kind="ExternalInput").ap(),
            tri=nc.dram_tensor("cn_tri", [128, 2048], BF16, kind="ExternalInput").ap(),
            identF=nc.dram_tensor("cn_identF", [128, 128], F32, kind="ExternalInput").ap())
    brows = nc.dram_tensor("cn_brows", [2, 2, 2, SEQ], BF16, kind="ExternalInput").ap()
    w0 = NS(gate_w=nc.dram_tensor("gate_w", [8, C], F32, kind="ExternalInput").ap(),
            sgw=nc.dram_tensor("sgw", [1, C], F32, kind="ExternalInput").ap(),
            fc1=nc.dram_tensor("fc1", [8, F_MOE, C], F32, kind="ExternalInput").ap(),
            fc2=nc.dram_tensor("fc2", [8, F_MOE, C], F32, kind="ExternalInput").ap(),
            proj=nc.dram_tensor("proj", [8, C, F_MOE], F32, kind="ExternalInput").ap(),
            sfc1=nc.dram_tensor("sfc1", [F_SH, C], F32, kind="ExternalInput").ap(),
            sfc2=nc.dram_tensor("sfc2", [F_SH, C], F32, kind="ExternalInput").ap(),
            sproj=nc.dram_tensor("sproj", [C, F_SH], F32, kind="ExternalInput").ap())
    w1 = NS(gate_w=nc.dram_tensor("l1_gate_w", [I_L1, C], I32, kind="ExternalInput").ap(),
            up_w=nc.dram_tensor("l1_up_w", [I_L1, C], I32, kind="ExternalInput").ap(),
            down_w=nc.dram_tensor("l1_down_w", [C, I_L1], I32, kind="ExternalInput").ap(),
            gate_scale=nc.dram_tensor("l1_gate_scale", [I_L1], F32, kind="ExternalInput").ap(),
            up_scale=nc.dram_tensor("l1_up_scale", [I_L1], F32, kind="ExternalInput").ap(),
            down_scale=nc.dram_tensor("l1_down_scale", [C], F32, kind="ExternalInput").ap())
    wa0 = NS(wq_s=nc.dram_tensor("wq_s", [C, 256], F32, kind="ExternalInput").ap(),
             wk_s=nc.dram_tensor("wk_s", [C, 256], F32, kind="ExternalInput").ap(),
             wv_s=nc.dram_tensor("wv_s", [C, 256], F32, kind="ExternalInput").ap())
    wa1 = NS(wqkv_s=nc.dram_tensor("wqkv_s", [768, C], F32, kind="ExternalInput").ap(),
             bqkv_s=nc.dram_tensor("bqkv_s", [768], F32, kind="ExternalInput").ap())
    wo0 = nc.dram_tensor("wo0", [C, C], F32, kind="ExternalInput").ap()
    wo1 = nc.dram_tensor("wo1", [C, C], F32, kind="ExternalInput").ap()
    bo0 = nc.dram_tensor("bo0", [1, C], F32, kind="ExternalInput").ap()
    bo1 = nc.dram_tensor("bo1", [1, C], F32, kind="ExternalInput").ap()

    scr = NS(fc1s=nc.dram_tensor("fc1s", [8, 6, 128, 4096], BF16, kind="Internal").ap(),
             fc2s=nc.dram_tensor("fc2s", [8, 6, 128, 4096], BF16, kind="Internal").ap(),
             projs=nc.dram_tensor("projs", [8, 4, 128, 5632], BF16, kind="Internal").ap(),
             sfc1s=nc.dram_tensor("sfc1s", [4, 6, 128, 4096], BF16, kind="Internal").ap(),
             sfc2s=nc.dram_tensor("sfc2s", [4, 6, 128, 4096], BF16, kind="Internal").ap(),
             sprojs=nc.dram_tensor("sprojs", [4, 4, 128, 5632], BF16, kind="Internal").ap(),
             l1gs=nc.dram_tensor("l1gs", [8, 4, 128, 4096], BF16, kind="Internal").ap(),
             l1us=nc.dram_tensor("l1us", [8, 4, 128, 4096], BF16, kind="Internal").ap(),
             l1ds=nc.dram_tensor("l1ds", [8, 4, 128, 4096], BF16, kind="Internal").ap(),
             wqkvb=nc.dram_tensor("wqkvb", [768, C], BF16, kind="Internal").ap(),
             woutb=nc.dram_tensor("woutb", [C, C], BF16, kind="Internal").ap())
    xn_stage = nc.dram_tensor("xn_stage", [C, TLOC], BF16, kind="Internal")
    xnT_all = nc.dram_tensor("xnT_all", [NCORES * C, TLOC], BF16, kind="Internal")
    ctx_stage = nc.dram_tensor("ctx_stage", [2 * D, SEQ], BF16, kind="Internal")
    ctx_all = nc.dram_tensor("ctx_all", [C, SEQ], BF16, kind="Internal")
    q_dram = nc.dram_tensor("q_dram", [2 * D, SEQ], BF16, kind="Internal").ap()

    es = ExitStack()
    k = K(nc, es)
    b = make_bufs(k)
    emit_const_loads(k, b, cn)
    b.pid_ld = k.dma("cn", lambda g: g.dma_start(out=b.pid_sb[0:1, 0:1],
                                                 in_=nc.partition_id_tensor.ap()))
    cwo = k.DL(lambda g: g.dma_start(out=scr.woutb, in_=wo1), [])
    b.castdone["wout"] = ("DL", cwo)
    emit_att2_build(k, b, bo0, 0)
    emit_att2_build(k, b, bo1, 32)
    b.att2_ready = b.att2_guard
    b.sbsb_guard = b.att2_guard
    xl = k.dma("xin", lambda g: g.dma_start(
        out=b.x_sb[:, :].rearrange("p (a c) -> p a c", c=2048),
        in_=x_in.rearrange("(a p) c -> p a c", p=128)), b.att2_guard)
    tt_done = [("xin", xl)] * NTT

    def ln_phase(pg):
        emit_ln_all(k, b, [[tt_done[tt]] + pg for tt in range(NTT)])

    def attn_sublayer(layer, wsrc_a, wo_src):
        pg = phase_boundary(k, b)
        emit_attn_wprep(k, b, layer, wsrc_a, scr)
        ln_phase(pg)
        st = k.dma("xns", lambda g: g.dma_start(
            out=xn_stage.ap().rearrange("(a p) w -> p a w", p=128),
            in_=b.arena[:, A_XNT:A_XNT+16384].rearrange("p (a w) -> p a w", w=1024)),
            b.xnT_done + pg)
        ag1 = k.CC(lambda g: g.collective_compute(
            "AllGather", ALU.bypass, RG, ins=[xn_stage.ap().opt()],
            outs=[xnT_all.ap().opt()]), [("xns", st)] + pg)
        xnT_all_ap = xnT_all.ap()
        slab_src = lambda cc, tb, th: xnT_all_ap[
            2048*tb + 128*cc: 2048*tb + 128*cc + 128,
            512*th: 512*th + 512]
        b.ag_guard = ("CC", k.cnt("CC"))
        pg2 = phase_boundary(k, b)
        emit_attn_qkv(k, b, layer, slab_src, q_dram)
        emit_attn_core(k, b, layer, [brows[layer, 0], brows[layer, 1]],
                       q_dram, ctx_stage.ap())
        pgc = phase_boundary(k, b)
        ag2 = k.CC(lambda g: g.collective_compute(
            "AllGather", ALU.bypass, RG, ins=[ctx_stage.ap().opt()],
            outs=[ctx_all.ap().opt()]), [b.ctx_done] + pgc)
        b.ctxbuf_war = pgc + [("CC", k.cnt("CC"))]
        ctx_all_ap = ctx_all.ap()
        def ctx_loader(k2, b2):
            ent = k2.sem_ent("ctx")
            acc = {}
            for (sn, v) in k2._flat([b2.ctxbuf_war, ("cn", b2.pid_ld)]):
                if v > 0 and v > acc.get(sn, 0):
                    acc[sn] = v
            wl = sorted(acc.items())

            def f(g):
                with g.register() as r:
                    g.reg_load(r, b2.pid_sb[0:1, 0:1])
                    g.reg_mul(r, r, TLOC)
                    off = g.snap(r, min_val=0, max_val=TLOC * (NCORES - 1))
                    src_ap = ctx_all_ap[:, bass.ds(off, TLOC)].rearrange(
                        "(a p) w -> p a w", p=128)
                    dst_ap = b2.arena[:, A_CTX:A_CTX + 16384].rearrange(
                        "p (a w) -> p a w", w=1024)
                    g.dma_start(out=dst_ap, in_=src_ap).then_inc(ent[0], 16)
            k2.dq["gpsimd"].append((wl, f, None, 0, ent[1] + 16))
            ent[1] += 16
            return ("ctx", ent[1])
        emit_attn_outproj(k, b, layer, wo_src, ctx_loader, tt_done,
                          wo_cast=(b.castdone["wout"] if layer == 1 else None))
        phase_boundary(k, b)

    def ffn_sub(kind, w, second=False):
        pg = phase_boundary(k, b)
        ln_phase(pg)
        emit_ffn(k, b, kind, w, tt_done, scr=scr, second=second)
    subs = [lambda: ffn_sub("l0", w0),
            lambda: attn_sublayer(0, wa0, wo0),
            lambda: ffn_sub("l0", w0, second=True),
            lambda: ffn_sub("l1", w1),
            lambda: attn_sublayer(1, wa1, scr.woutb),
            lambda: ffn_sub("l1", w1, second=True)]
    for f in subs[:n_sub]:
        f()

    st_last = k.dma("xout", lambda g: g.dma_start(
        out=x_out.rearrange("(a p) c -> p a c", p=128),
        in_=b.x_sb[:, :].rearrange("p (a c) -> p a c", c=2048)), tt_done)
    k.waiter("G", [("xout", st_last)])
    k.finalize()
    es.close()
    return nc



_L0_KEYS = dict(gate_w="L0_ffn_gate_w", sgw="L0_ffn_shared_gate_w",
                fc1="L0_ffn_expert_fc1", fc2="L0_ffn_expert_fc2",
                proj="L0_ffn_expert_proj", sfc1="L0_ffn_shared_fc1",
                sfc2="L0_ffn_shared_fc2", sproj="L0_ffn_shared_proj")
_L1_KEYS = dict(gate_w="L1_ffn_gate_w", up_w="L1_ffn_up_w", down_w="L1_ffn_down_w",
                gate_scale="L1_ffn_gate_scale", up_scale="L1_ffn_up_scale",
                down_scale="L1_ffn_down_scale")

_CACHE = {}


def _get(name, builder):
    if name not in _CACHE:
        _CACHE[name] = builder()
    return _CACHE[name]


def host_bias_rows(layer, h):
    j = np.arange(SEQ, dtype=np.float64)
    slope = 2.0 ** (-(h + 1) / 2.0)
    if layer == 0:
        row = slope * j * (1.0 / np.sqrt(np.float64(D)))
    else:
        row = (j - (SEQ - 1)) * slope
    hi = row.astype(np.float32).astype(BF)
    lo = (row - hi.astype(np.float64)).astype(np.float32).astype(BF)
    return np.stack([hi, lo])


def run_ffn_step(kind, x_full, inputs, with_xnT=False):
    nc = _get(("ffn", kind, with_xnT), lambda: build_ffn_launch(kind, with_xnT))
    cns = host_consts()
    keys = _L0_KEYS if kind == "l0" else _L1_KEYS
    wmap = {k2: np.ascontiguousarray(np.asarray(inputs[v])) for k2, v in keys.items()}
    if kind == "l0":
        wmap["sgw"] = wmap["sgw"].reshape(1, C)
    in_maps = [dict(x_in=np.ascontiguousarray(x_full[TLOC*c:TLOC*(c+1)]), **wmap, **cns)
               for c in range(NCORES)]
    res = run_bass_kernel_spmd(nc, in_maps, core_ids=list(range(NCORES)))
    x_new = np.concatenate([res.results[c]["x_out"] for c in range(NCORES)], 0)
    if with_xnT:
        xnT = np.concatenate([res.results[c]["xnT_out"] for c in range(NCORES)], 1)
        return x_new, xnT
    return x_new


def run_attn_step(layer, x_full, xnT_full, inputs):
    nc_a = _get(("attn", layer), lambda: build_attn_core_launch(layer))
    cns = host_consts()
    in_maps = []
    for c in range(NCORES):
        m = dict(xnT_full=np.ascontiguousarray(xnT_full), **cns)
        m["cn_brows"] = np.ascontiguousarray(
            np.stack([host_bias_rows(layer, 2*c), host_bias_rows(layer, 2*c+1)]))
        if layer == 0:
            for nm, key in (("wq_s", "L0_attn_Wq"), ("wk_s", "L0_attn_Wk"),
                            ("wv_s", "L0_attn_Wv")):
                m[nm] = np.ascontiguousarray(np.asarray(inputs[key])[:, 256*c:256*c+256])
        else:
            Wqkv = np.asarray(inputs["L1_attn_Wqkv"])
            bq = np.asarray(inputs["L1_attn_b_qkv"])
            rows = np.concatenate([Wqkv[256*c:256*c+256],
                                   Wqkv[2048+256*c:2048+256*c+256],
                                   Wqkv[4096+256*c:4096+256*c+256]], 0)
            m["wqkv_s"] = np.ascontiguousarray(rows)
            m["bqkv_s"] = np.ascontiguousarray(np.concatenate(
                [bq[256*c:256*c+256], bq[2048+256*c:2048+256*c+256],
                 bq[4096+256*c:4096+256*c+256]]))
        in_maps.append(m)
    res = run_bass_kernel_spmd(nc_a, in_maps, core_ids=list(range(NCORES)))
    ctx_full = np.concatenate([res.results[c]["ctx_out"] for c in range(NCORES)], 0)

    nc_o = _get(("outproj", layer), lambda: build_outproj_launch(layer))
    wo = np.ascontiguousarray(np.asarray(
        inputs["L0_attn_Wo" if layer == 0 else "L1_attn_Wout"]))
    b2 = np.ascontiguousarray(np.asarray(
        inputs["L0_attn_bo" if layer == 0 else "L1_attn_b_out"]).reshape(1, C))
    in_maps = [dict(x_in=np.ascontiguousarray(x_full[TLOC*c:TLOC*(c+1)]),
                    ctx_own=np.ascontiguousarray(ctx_full[:, TLOC*c:TLOC*(c+1)]),
                    wo=wo, bias2=b2, **cns)
               for c in range(NCORES)]
    res = run_bass_kernel_spmd(nc_o, in_maps, core_ids=list(range(NCORES)))
    return np.concatenate([res.results[c]["x_out"] for c in range(NCORES)], 0)


def kernel_unfused(**inputs):
    x = np.ascontiguousarray(np.asarray(inputs["L0_attn_x"], np.float32).reshape(SEQ, C))
    x, xnT = run_ffn_step("l0", x, inputs, with_xnT=True)
    x = run_attn_step(0, x, xnT, inputs)
    x = run_ffn_step("l0", x, inputs)
    x, xnT = run_ffn_step("l1", x, inputs, with_xnT=True)
    x = run_attn_step(1, x, xnT, inputs)
    x = run_ffn_step("l1", x, inputs)
    return x.reshape(1, SEQ, C)


def kernel(**inputs, ):
    n_sub = int(inputs.pop("_n_sub", 6))
    x = np.ascontiguousarray(np.asarray(inputs["L0_attn_x"], np.float32).reshape(SEQ, C))
    nc = _get(("fused", n_sub), lambda: build_fused(n_sub))
    cns = host_consts()
    wmap = {}
    for k2, v in _L0_KEYS.items():
        wmap[k2] = np.ascontiguousarray(np.asarray(inputs[v]))
    wmap["sgw"] = wmap["sgw"].reshape(1, C)
    for k2, v in _L1_KEYS.items():
        wmap["l1_" + k2] = np.ascontiguousarray(np.asarray(inputs[v]))
    wmap["wo0"] = np.ascontiguousarray(np.asarray(inputs["L0_attn_Wo"]))
    wmap["wo1"] = np.ascontiguousarray(np.asarray(inputs["L1_attn_Wout"]))
    wmap["bo0"] = np.ascontiguousarray(np.asarray(inputs["L0_attn_bo"]).reshape(1, C))
    wmap["bo1"] = np.ascontiguousarray(np.asarray(inputs["L1_attn_b_out"]).reshape(1, C))
    Wqkv = np.asarray(inputs["L1_attn_Wqkv"])
    bq = np.asarray(inputs["L1_attn_b_qkv"])
    in_maps = []
    for c in range(NCORES):
        m = dict(x_in=np.ascontiguousarray(x[TLOC*c:TLOC*(c+1)]), **wmap, **cns)
        m["cn_brows"] = np.ascontiguousarray(np.stack([
            np.stack([host_bias_rows(0, 2*c), host_bias_rows(0, 2*c+1)]),
            np.stack([host_bias_rows(1, 2*c), host_bias_rows(1, 2*c+1)])]))
        for nm, key in (("wq_s", "L0_attn_Wq"), ("wk_s", "L0_attn_Wk"),
                        ("wv_s", "L0_attn_Wv")):
            m[nm] = np.ascontiguousarray(np.asarray(inputs[key])[:, 256*c:256*c+256])
        m["wqkv_s"] = np.ascontiguousarray(np.concatenate(
            [Wqkv[256*c:256*c+256], Wqkv[2048+256*c:2048+256*c+256],
             Wqkv[4096+256*c:4096+256*c+256]], 0))
        m["bqkv_s"] = np.ascontiguousarray(np.concatenate(
            [bq[256*c:256*c+256], bq[2048+256*c:2048+256*c+256],
             bq[4096+256*c:4096+256*c+256]]))
        in_maps.append(m)
    res = run_bass_kernel_spmd(nc, in_maps, core_ids=list(range(NCORES)))
    out = np.concatenate([res.results[c]["x_out"] for c in range(NCORES)], 0)
    return out.reshape(1, SEQ, C)


# version 08
# speedup vs baseline: 1.1701x; 1.1017x over previous
import numpy as np
import ml_dtypes
from contextlib import ExitStack

import concourse.bass as bass
import concourse.mybir as mybir
from concourse.bass_utils import run_bass_kernel_spmd

F32 = mybir.dt.float32
BF16 = mybir.dt.bfloat16
I32 = mybir.dt.int32
AF = mybir.ActivationFunctionType
ALU = mybir.AluOpType
AX = mybir.AxisListType

SEQ = 8192
C = 2048
NHEAD = 16
D = 128
NCORES = 8
TLOC = SEQ // NCORES
NTT = TLOC // 128
F_MOE = 1408
F_SH = 5632
I_L1 = 8192
EPS = 1e-5
RSQD = float(1.0 / np.sqrt(np.float64(D)))
BF = ml_dtypes.bfloat16

A_XNT = 0
A_WG = 16384
A_WU = 24576
A_H = 32768
A_PJ = 44032
A_SU = 49664
A_RW = 50176
A_SGW = 50304
A_WN0 = 50320
A_Q = 0
A_K = 4096
A_V = 20480
A_WQKV = 37120
A_SLAB = 49408
A_P = 50432
A_PT = 51456
A_WST = 50432
A_CTX = 0
A_WON = 16384
A_WO = 24576

VW = 130
VH = 64 * VW


class K:

    ENGS = ("tensor", "vector", "scalar", "gpsimd", "sync")

    def __init__(self, nc, es):
        self.nc = nc
        self.es = es
        self.blk = es.enter_context(nc.Block())
        self.sems = {}
        for n in ("T", "V", "S", "G", "DL", "DQ", "CC"):
            self.sems[n] = [es.enter_context(nc.semaphore(n)), 0]
        self.dq = {e: [] for e in self.ENGS}

    def cnt(self, name):
        return self.sems[name][1]

    @staticmethod
    def _flat(waits):
        out = []
        for w in waits:
            if w is None:
                continue
            if isinstance(w, list):
                out.extend(t for t in w if t is not None)
            else:
                out.append(w)
        return out

    def _collapse(self, waits):
        acc = {}
        for (sn, v) in self._flat(waits):
            if v > 0 and v > acc.get(sn, 0):
                acc[sn] = v
        return sorted(acc.items())

    def sem_ent(self, name):
        if name not in self.sems:
            self.sems[name] = [self.es.enter_context(self.nc.semaphore(name)), 0]
        return self.sems[name]

    def _emit(self, eng, fn, waits, sem, n):
        ent = self.sem_ent(sem)
        selfwait = ent[1] + n if sem in ("DL", "DQ") else None
        self.dq[eng].append((self._collapse(waits), fn, ent[0], n, selfwait))
        ent[1] += n
        return ent[1]

    def dma(self, ch, fn, waits=(), q="gpsimd"):
        ent = self.sem_ent(ch)
        self.dq[q].append((self._collapse(waits), fn, ent[0], 16, None))
        ent[1] += 16
        return ent[1]

    def T(self, fn, waits=(), n=1):   return self._emit("tensor", fn, waits, "T", n)
    def V(self, fn, waits=(), n=1):   return self._emit("vector", fn, waits, "V", n)
    def S(self, fn, waits=(), n=1):   return self._emit("scalar", fn, waits, "S", n)
    def G(self, fn, waits=(), n=1):   return self._emit("gpsimd", fn, waits, "G", n)
    def DL(self, fn, waits=(), n=16): return self._emit("gpsimd", fn, waits, "DL", n)
    def DQ(self, fn, waits=(), n=16): return self._emit("sync", fn, waits, "DQ", n)
    def CC(self, fn, waits=(), n=1):  return self._emit("gpsimd", fn, waits, "CC", n)

    def DLmulti(self, fns, waits=()):
        ent = self.sems["DL"]
        semh = ent[0]

        total = ent[1] + 16 * len(fns)

        def f(g):
            for fn in fns:
                fn(g).then_inc(semh, 16)
        self.dq["gpsimd"].append((self._collapse(waits), f, None, 0, total))
        ent[1] = total
        return ent[1]

    ENG = dict(T="tensor", V="vector", S="scalar", G="gpsimd", Q="sync")

    def waiter(self, eng, waits):
        self.dq[self.ENG[eng]].append((self._collapse(waits), None, None, 0, None))

    def finalize(self):
        hmap = self.sems
        for eng in self.ENGS:
            items = self.dq[eng]
            if not items:
                continue

            def body(e, items=items):
                for (wl, fn, semh, n, selfwait) in items:
                    for (sn, v) in wl:
                        e.wait_ge(hmap[sn][0], v)
                    if fn is None:
                        continue
                    r = fn(e)
                    if r is not None and semh is not None:
                        r.then_inc(semh, n)
                    if selfwait is not None:
                        sw = hmap["DL"] if (semh is hmap["DL"][0] or semh is None) else hmap["DQ"]
                        e.wait_ge(sw[0], selfwait)
            getattr(self.blk, eng)(body)


class Rot:
    def __init__(self, n):
        self.n = n
        self.free = [[("T", 0)]] * n

    def war(self, i):
        return self.free[i % self.n]

    def done(self, i, sem, val):
        self.free[i % self.n] = [(sem, val)]

    def set_all(self, waits):
        self.free = [list(waits)] * self.n


class NS:
    def __init__(self, **kw):
        self.__dict__.update(kw)


def make_bufs(k):
    nc, es = k.nc, k.es
    b = NS(
        arena=es.enter_context(nc.sbuf_tensor("arena", [128, 53248], BF16)),
        x_sb=es.enter_context(nc.sbuf_tensor("x_sb", [128, 8 * 2048], F32)),
        xn_t=es.enter_context(nc.sbuf_tensor("xn_t", [128, 2048], BF16)),
        sq_t=es.enter_context(nc.sbuf_tensor("sq_t", [128, 2048], F32)),
        stats=es.enter_context(nc.sbuf_tensor("stats", [128, 64], F32)),
        comb=es.enter_context(nc.sbuf_tensor("comb", [128, 80], F32)),
        scl=es.enter_context(nc.sbuf_tensor("scl", [128, 304], F32)),
        bias_sb=es.enter_context(nc.sbuf_tensor("bias_sb", [128, 8192], BF16)),
        bmax_sb=es.enter_context(nc.sbuf_tensor("bmax_sb", [128, 32], F32)),
        att2=es.enter_context(nc.sbuf_tensor("att2", [128, 2048], BF16)),
        ident=es.enter_context(nc.sbuf_tensor("ident", [128, 128], BF16)),
        identF=es.enter_context(nc.sbuf_tensor("identF", [128, 128], F32)),
        ones2=es.enter_context(nc.sbuf_tensor("ones2", [34, 128], BF16)),
        tri=es.enter_context(nc.sbuf_tensor("tri", [128, 4 * 512], BF16)),
        ps=[es.enter_context(nc.psum_tensor("ps%d" % i, [128, 512], F32)) for i in range(4)],
        psE=es.enter_context(nc.psum_tensor("psE", [128, 512], F32)),
        psF=es.enter_context(nc.psum_tensor("psF", [128, 512], F32)),
        psT=es.enter_context(nc.psum_tensor("psT", [128, 2048], BF16)),
        pid_sb=es.enter_context(nc.sbuf_tensor("pid_sb", [1, 2], mybir.dt.uint32)),
    )
    b.psTb = Rot(2)
    b.ps_rot = Rot(4)
    b.psE_rot = Rot(1)
    b.psF_rot = Rot(1)
    b.wg_rot = Rot(2)
    b.wu_rot = Rot(2)
    b.wnat_rot = Rot(2)
    b.pjnat_rot = Rot(2)
    b.pj_war = ("T", 0)
    b.h_war = ("T", 0)
    b.su_war = ("V", 0)
    b.sq_war = ("S", 0)
    b.xn_war = ("T", 0)
    b.xnT_done = ("V", 0)
    b.psT_bid = 0
    b.slab_rot = Rot(2)
    b.p_rot = Rot(2)
    b.pt_rot = Rot(2)
    b.wo_rot = Rot(2)
    b.q_war = ("T", 0)
    b.wst_war = ("T", 0)
    b.won_war = ("T", 0)
    b.ctxbuf_war = ("T", 0)
    b.vones_guard = ("T", 0)
    b.bias_guard = ("T", 0)
    b.att2_guard = ("T", 0)
    b.att2_ready = ("T", 0)
    b.wqkv_ready = []
    b.qkvbias_ready = ("V", 0)
    b.kv_ready = []
    b.ag_guard = None
    b.castdone = {}
    b.sbsb_guard = None
    return b


def phase_boundary(k, b):
    pg = [(s, ent[1]) for s, ent in k.sems.items()]
    for r in (b.psTb, b.ps_rot, b.psE_rot, b.psF_rot, b.wg_rot, b.wu_rot,
              b.wnat_rot, b.pjnat_rot, b.slab_rot, b.p_rot, b.pt_rot, b.wo_rot):
        r.set_all(pg)
    for name in ("pj_war", "h_war", "su_war", "sq_war", "xn_war", "q_war",
                 "wst_war", "won_war", "ctxbuf_war", "vones_guard", "bias_guard"):
        setattr(b, name, list(pg))
    return pg


def AP_(b, c0, w):
    return b.arena[:, c0:c0 + w]



def temove_batch(k, b, srcs, dst_ap, waits, eng, width=128, in_parts=128):
    n = len(srcs)
    pad = width if width % 2 == 0 else width + 1
    assert n * pad <= 1024
    bid = b.psT_bid % 2
    b.psT_bid += 1
    war = b.psTb.war(bid)
    base = 1024 * bid
    t_d = 0
    for i, sfn in enumerate(srcs):
        t_d = k.T(lambda t, i=i, sfn=sfn: t.transpose(
            b.psT[:, base + pad * i: base + pad * i + width],
            sfn(), b.ident[0:in_parts, 0:in_parts]),
            (list(waits) + [war]) if i == 0 else [])
    if pad == width:
        src_ap = b.psT[:, base:base + n * width]
    else:
        src_ap = b.psT[:, base:base + n * pad].rearrange(
            "p (a w) -> p a w", w=pad)[:, :, 0:width]
    if eng == "V":
        e_d = k.V(lambda v: v.tensor_copy(dst_ap, src_ap), [("T", t_d)])
    else:
        e_d = k.S(lambda s: s.copy(dst_ap, src_ap), [("T", t_d)])
    b.psTb.done(bid, eng, e_d)
    return (eng, e_d)



def emit_ln_tile(k, b, tt, x_guards):
    st = b.stats
    SUM, MEAN, M2, E2, VAR, RSTD, NEGB = 0, 1, 2, 3, 4, 5, 6
    xs = b.x_sb[:, 2048 * tt: 2048 * tt + 2048]

    v_sum = k.V(lambda v: v.reduce_sum(st[:, SUM:SUM+1], xs, axis=AX.X), x_guards)
    s_sq = k.S(lambda s: s.activation(b.sq_t[:, :], xs, AF.Square,
                                      accum_out=st[:, M2:M2+1]),
               list(x_guards) + [b.sq_war])
    b.sq_war = [("S", s_sq)]
    v1 = k.V(lambda v: v.tensor_scalar(st[:, MEAN:MEAN+1], st[:, SUM:SUM+1],
                                       1.0 / 2048.0, None, ALU.mult), [("V", v_sum)])
    v2 = k.V(lambda v: v.tensor_scalar(st[:, E2:E2+1], st[:, M2:M2+1],
                                       1.0 / 2048.0, None, ALU.mult),
             [("S", s_sq), ("V", v1)])
    v3 = k.V(lambda v: v.tensor_tensor(st[:, VAR:VAR+1], st[:, MEAN:MEAN+1],
                                       st[:, MEAN:MEAN+1], ALU.mult), [("V", v2)])
    v4 = k.V(lambda v: v.tensor_tensor(st[:, VAR:VAR+1], st[:, E2:E2+1],
                                       st[:, VAR:VAR+1], ALU.subtract), [("V", v3)])
    v4b = k.V(lambda v: v.tensor_scalar(st[:, VAR:VAR+1], st[:, VAR:VAR+1],
                                        float(EPS), None, ALU.add), [("V", v4)])
    s_sd = k.S(lambda s: s.activation(st[:, RSTD:RSTD+1], st[:, VAR:VAR+1],
                                      AF.Sqrt), [("V", v4b)])
    v5 = k.V(lambda v: v.reciprocal(st[:, RSTD:RSTD+1], st[:, RSTD:RSTD+1]),
             [("S", s_sd)])
    v6 = k.V(lambda v: v.tensor_tensor(st[:, NEGB:NEGB+1], st[:, MEAN:MEAN+1],
                                       st[:, RSTD:RSTD+1], ALU.mult), [("V", v5)])
    v7 = k.V(lambda v: v.tensor_scalar(st[:, NEGB:NEGB+1], st[:, NEGB:NEGB+1],
                                       -1.0, None, ALU.mult), [("V", v6)])
    s_xn = k.S(lambda s: s.activation(b.xn_t[:, :], xs, AF.Identity,
                                      bias=st[:, NEGB:NEGB+1],
                                      scale=st[:, RSTD:RSTD+1]),
               [("V", v7), b.xn_war])
    last = ("V", 0)
    for half in range(2):
        c0 = 8 * half
        base = A_XNT + 1024 * c0 + 128 * tt
        dst = b.arena[:, base:base + 8192].rearrange(
            "p (a w) -> p a w", w=1024)[:, :, 0:128]
        srcs = [(lambda cc=cc: b.xn_t[:, 128 * cc:128 * cc + 128])
                for cc in range(c0, c0 + 8)]
        last = temove_batch(k, b, srcs, dst, [("S", s_xn)], "V" if half == 0 else "S")
    b.xn_war = [("T", k.cnt("T"))]
    return last


def emit_ln_all(k, b, x_guards_per_tt):
    evs = []
    for tt in range(NTT):
        evs.append(emit_ln_tile(k, b, tt, x_guards_per_tt[tt]))
    b.xnT_done = [("V", k.cnt("V")), ("S", k.cnt("S"))]
    return b.xnT_done



def _wprep(k, b, src, nat_slot, w_base, w_slot, fj2, rot_w, dq, store_to=None):
    nat_slot = nat_slot % 2
    nat = AP_(b, A_WN0, 2048) if nat_slot == 0 else b.xn_t[:, 0:2048]
    ld = k.dma("wn%d" % nat_slot, lambda g: g.dma_start(out=nat, in_=src),
               [b.wnat_rot.free[nat_slot]])
    nat_ready = ("wn%d" % nat_slot, ld)
    if dq is not None:
        scale_ap, bias_ap, ready = dq
        dqd = k.S(lambda s: s.activation(nat, nat, AF.Identity, bias=bias_ap,
                                         scale=scale_ap), [nat_ready, ready])
        nat_ready = ("S", dqd)
    if store_to is not None:
        k.DL(lambda g: g.dma_start(out=store_to, in_=nat), [nat_ready])
    for half in range(2):
        c0 = 8 * half
        base = w_base + 4096 * w_slot + 256 * c0 + 128 * fj2
        dst = b.arena[:, base:base + 8 * 256].rearrange(
            "p (a w) -> p a w", w=256)[:, :, 0:128]
        srcs = [(lambda cc=cc: nat[:, 128 * cc:128 * cc + 128])
                for cc in range(c0, c0 + 8)]
        temove_batch(k, b, srcs, dst, [nat_ready, rot_w.war(w_slot)],
                     "V" if half == 0 else "S")
    b.wnat_rot.free[nat_slot] = [("T", k.cnt("T"))]
    return [("V", k.cnt("V")), ("S", k.cnt("S"))]


def _wload_slab(k, b, slab_ap, w_base, slot, ch, cast_w):
    last = k.dma(ch, lambda g: g.dma_start(
        out=b.arena[:, w_base + 4096*slot: w_base + 4096*slot + 4096], in_=slab_ap),
        [cast_w, (b.wg_rot if w_base == A_WG else b.wu_rot).war(slot)])
    return [(ch, last)]


def _emit_ffn_group(k, b, tt_done, nfj, scale_mode, gid,
                    gate_src=None, up_src=None, down_src=None,
                    dq_gate=None, dq_up=None, dq_down=None,
                    slab=None, cast_w=None, store=None):
    npair = (nfj + 1) // 2
    for fp in range(npair):
        slot = fp % 2
        if slab is not None:
            ev_g = _wload_slab(k, b, slab.gs[fp], A_WG, slot, "tg%d" % slot, cast_w)
            ev_u = _wload_slab(k, b, slab.us[fp], A_WU, slot, "tu%d" % slot, cast_w)
        else:
            ev_g = ev_u = None
        for fj2 in range(min(2, nfj - 2 * fp)):
            fj = 2 * fp + fj2
            if slab is None:
                ev_g = _wprep(k, b, gate_src(fj), 0, A_WG, slot, fj2,
                              b.wg_rot, dq_gate(fj) if dq_gate else None)
                ev_u = _wprep(k, b, up_src(fj), 1, A_WU, slot, fj2,
                              b.wu_rot, dq_up(fj) if dq_up else None)
            for th in range(2):
                def mmg(t, slot=slot, fj2=fj2, th=th):
                    r = None
                    for cc in range(16):
                        r = t.matmul(b.psE[:, 0:512],
                                     AP_(b, A_WG + 4096*slot + 256*cc + 128*fj2, 128),
                                     AP_(b, A_XNT + 1024*cc + 512*th, 512),
                                     start=(cc == 0), stop=(cc == 15))
                    return r
                g_d = k.T(mmg, ev_g + [b.xnT_done, b.psE_rot.war(0)])
                def mmu(t, slot=slot, fj2=fj2, th=th):
                    r = None
                    for cc in range(16):
                        r = t.matmul(b.psF[:, 0:512],
                                     AP_(b, A_WU + 4096*slot + 256*cc + 128*fj2, 128),
                                     AP_(b, A_XNT + 1024*cc + 512*th, 512),
                                     start=(cc == 0), stop=(cc == 15))
                    return r
                u_d = k.T(mmu, ev_u + [b.xnT_done, b.psF_rot.war(0)])
                hc = A_H + 1024 * fj + 512 * th
                sg = k.S(lambda s, hc=hc: s.activation(AP_(b, hc, 512), b.psE[:, 0:512],
                                                       AF.Silu), [("T", g_d), b.h_war])
                b.psE_rot.done(0, "S", sg)
                ue = k.V(lambda v: v.tensor_copy(AP_(b, A_SU, 512), b.psF[:, 0:512]),
                         [("T", u_d), b.su_war])
                b.psF_rot.done(0, "V", ue)
                hm = k.V(lambda v, hc=hc: v.tensor_tensor(
                    AP_(b, hc, 512), AP_(b, hc, 512), AP_(b, A_SU, 512), ALU.mult),
                    [("V", ue), ("S", sg)])
                b.su_war = [("V", hm)]
        if store is not None:
            sb_ch = "sbL0" if gid[0] in ("e", "s") else "sbL1"
            wev = [("V", k.cnt("V")), ("S", k.cnt("S"))]
            k.dma(sb_ch, lambda g, fp=fp, slot=slot: g.dma_start(
                out=store.gs[fp],
                in_=b.arena[:, A_WG + 4096*slot: A_WG + 4096*slot + 4096]), wev)
            g2 = k.dma(sb_ch, lambda g, fp=fp, slot=slot: g.dma_start(
                out=store.us[fp],
                in_=b.arena[:, A_WU + 4096*slot: A_WU + 4096*slot + 4096]), wev)
            b.wg_rot.done(slot, sb_ch, g2)
            b.wu_rot.done(slot, sb_ch, g2)
        else:
            b.wg_rot.done(slot, "T", k.cnt("T"))
            b.wu_rot.done(slot, "T", k.cnt("T"))
    h_all = ("V", k.cnt("V"))

    fw = 128 * nfj
    for cb in range(4):
        if slab is not None:
            last = k.dma("pjt", lambda g, cb=cb: g.dma_start(
                out=b.arena[:, A_PJ:A_PJ + 512*nfj], in_=slab.ds[cb]),
                [cast_w, b.pj_war])
            pj_ready = [("pjt", last)]
        else:
            for cj in range(4):
                cq = 4 * cb + cj
                nslot = cq % 2
                war = b.wnat_rot.war(cq)
                nat = (AP_(b, A_WN0, fw) if nslot == 0
                       else b.xn_t[:, 0:fw])
                ldd = k.dma("wn%d" % nslot, lambda g, nat=nat, cq=cq: g.dma_start(
                    out=nat, in_=down_src(cq)), [war])
                nat_ready = ("wn%d" % nslot, ldd)
                if dq_down is not None:
                    scale_ap, bias_ap, ready = dq_down(cq)
                    dqd = k.S(lambda s, nat=nat, scale_ap=scale_ap, bias_ap=bias_ap:
                              s.activation(nat, nat, AF.Identity, bias=bias_ap,
                                           scale=scale_ap), [nat_ready, ready])
                    nat_ready = ("S", dqd)
                for f0 in range(0, nfj, 8):
                    nf = min(8, nfj - f0)
                    base = A_PJ + 512 * f0 + 128 * cj
                    dst = b.arena[:, base:base + 512 * nf].rearrange(
                        "p (a w) -> p a w", w=512)[:, :, 0:128] if nf > 1 else AP_(b, base, 128)
                    srcs = [(lambda nat=nat, fj=fj: nat[:, 128 * fj:128 * fj + 128])
                            for fj in range(f0, f0 + nf)]
                    temove_batch(k, b, srcs, dst, [nat_ready, b.pj_war],
                                 "V" if (cj + f0) % 2 == 0 else "S")
                b.wnat_rot.done(cq, "T", k.cnt("T"))
            pj_ready = [("V", k.cnt("V")), ("S", k.cnt("S"))]
            if store is not None:
                sb_ch = "sbL0" if gid[0] in ("e", "s") else "sbL1"
                k.dma(sb_ch, lambda g, cb=cb: g.dma_start(
                    out=store.ds[cb], in_=b.arena[:, A_PJ:A_PJ + 512*nfj]), pj_ready)
        for quad in range(2):
            for fj in range(nfj):
                w = []
                if fj == 0:
                    w = pj_ready + [h_all] + [b.ps_rot.war(i) for i in range(4)]
                def mmd(t, fj=fj, quad=quad, nfj=nfj):
                    r = None
                    for ti in range(4):
                        tt = 4 * quad + ti
                        r = t.matmul(b.ps[ti][:, 0:512],
                                     AP_(b, A_H + 1024 * fj + 128 * tt, 128),
                                     AP_(b, A_PJ + 512 * fj, 512),
                                     start=(fj == 0), stop=(fj == nfj - 1))
                    return r
                k.T(mmd, w)
            mm_fin = ("T", k.cnt("T"))
            for ti in range(4):
                tt = 4 * quad + ti
                xs = b.x_sb[:, 2048 * tt + 512 * cb: 2048 * tt + 512 * cb + 512]
                if scale_mode[0] == "comb":
                    sc_ap = b.comb[:, 8 * tt + scale_mode[1]: 8 * tt + scale_mode[1] + 1]
                    w_extra = b.comb_ready
                elif scale_mode[0] == "sgate":
                    sc_ap = b.comb[:, 64 + tt: 64 + tt + 1]
                    w_extra = b.comb_ready
                else:
                    sc_ap = scale_mode[1]
                    w_extra = []
                u_d = k.V(lambda v, ti=ti, xs=xs, sc_ap=sc_ap: v.scalar_tensor_tensor(
                    xs, b.ps[ti][:, 0:512], sc_ap, xs, op0=ALU.mult, op1=ALU.add),
                    [mm_fin, tt_done[tt], w_extra])
                tt_done[tt] = ("V", u_d)
                b.ps_rot.done(ti, "V", u_d)
        if store is not None and slab is None:
            sb_ch = "sbL0" if gid[0] in ("e", "s") else "sbL1"
            b.pj_war = [("T", k.cnt("T")), (sb_ch, k.cnt(sb_ch))]
        else:
            b.pj_war = [("T", k.cnt("T"))]
    b.h_war = [("T", k.cnt("T"))]


def _emit_router(k, b, wsrc):
    st = b.stats
    ld = k.dma("cn", lambda g: g.dma_start(out=b.xn_t[0:8, 0:2048], in_=wsrc.gate_w),
               [b.xnT_done, b.xn_war])
    srcs = [(lambda cc=cc: b.xn_t[0:8, 128 * cc:128 * cc + 128]) for cc in range(16)]
    temove_batch(k, b, srcs, AP_(b, A_RW, 128), [("cn", ld)], "V", width=8, in_parts=8)
    ld2 = k.dma("cn", lambda g: g.dma_start(out=b.xn_t[0:1, 0:2048], in_=wsrc.sgw),
                [("T", k.cnt("T"))])
    srcs = [(lambda cc=cc: b.xn_t[0:1, 128 * cc:128 * cc + 128]) for cc in range(16)]
    temove_batch(k, b, srcs, AP_(b, A_SGW, 16), [("cn", ld2)], "V", width=1, in_parts=1)
    b.xn_war = [("T", k.cnt("T"))]
    b.wnat_rot.free[1] = [("T", k.cnt("T"))] + list(b.xn_war)
    rw_ready = ("V", k.cnt("V"))

    P0, PM, M1, M2_, L0, SG = 16, 24, 32, 33, 34, 35
    for tt in range(NTT):
        def mm(t, tt=tt):
            r = None
            for cc in range(16):
                t.matmul(b.psE[:, 0:8], AP_(b, A_XNT + 1024 * cc + 128 * tt, 128),
                         AP_(b, A_RW + 8 * cc, 8), start=(cc == 0), stop=(cc == 15))
            for cc in range(16):
                r = t.matmul(b.psE[:, 8:9], AP_(b, A_XNT + 1024 * cc + 128 * tt, 128),
                             AP_(b, A_SGW + cc, 1), start=(cc == 0), stop=(cc == 15))
            return r
        m_d = k.T(mm, [rw_ready, b.xnT_done, b.psE_rot.war(0)])
        v1a = k.V(lambda v: v.reduce_max(st[:, M1:M1+1], b.psE[:, 0:8], axis=AX.X),
                  [("T", m_d)])
        v1 = k.V(lambda v: v.tensor_scalar(st[:, M1:M1+1], st[:, M1:M1+1], -1.0,
                                           None, ALU.mult), [("V", v1a)])
        s1 = k.S(lambda s: s.activation(st[:, P0:P0+8], b.psE[:, 0:8], AF.Exp,
                                        bias=st[:, M1:M1+1],
                                        accum_out=st[:, L0:L0+1]), [("V", v1)])
        v2 = k.V(lambda v: v.reciprocal(st[:, L0:L0+1], st[:, L0:L0+1]), [("S", s1)])
        v3 = k.V(lambda v: v.tensor_scalar(st[:, P0:P0+8], st[:, P0:P0+8],
                                           st[:, L0:L0+1], None, ALU.mult), [("V", v2)])
        v4 = k.V(lambda v: v.reduce_max(st[:, M1:M1+1], st[:, P0:P0+8], axis=AX.X),
                 [("V", v3)])
        v5 = k.V(lambda v: v.scalar_tensor_tensor(
            st[:, PM:PM+8], st[:, P0:P0+8], st[:, M1:M1+1], st[:, P0:P0+8],
            op0=ALU.is_lt, op1=ALU.mult), [("V", v4)])
        v6 = k.V(lambda v: v.reduce_max(st[:, M2_:M2_+1], st[:, PM:PM+8], axis=AX.X),
                 [("V", v5)])
        v7 = k.V(lambda v, tt=tt: v.scalar_tensor_tensor(
            b.comb[:, 8*tt:8*tt+8], st[:, P0:P0+8], st[:, M2_:M2_+1], st[:, P0:P0+8],
            op0=ALU.is_ge, op1=ALU.mult), [("V", v6)])
        v8 = k.V(lambda v, tt=tt: v.tensor_scalar(b.comb[:, 8*tt:8*tt+8],
                                           b.comb[:, 8*tt:8*tt+8], 0.5, None,
                                           ALU.mult), [("V", v7)])
        s2 = k.S(lambda s: s.activation(st[:, SG:SG+1], b.psE[:, 8:9], AF.Sigmoid),
                 [("T", m_d), ("S", s1)])
        v9 = k.V(lambda v, tt=tt: v.tensor_scalar(b.comb[:, 64+tt:64+tt+1], st[:, SG:SG+1],
                                           0.5, None, ALU.mult), [("S", s2), ("V", v8)])
        b.psE_rot.done(0, "V", v9)
    b.comb_ready = [("V", k.cnt("V"))]


def _emit_l1_scales(k, b, wsrc):
    for idx, (src_, col, n) in enumerate([(wsrc.gate_scale, 0, 64),
                                          (wsrc.up_scale, 64, 64),
                                          (wsrc.down_scale, 128, 16)]):
        ld = k.dma("cn", lambda g, src_=src_, n=n: g.dma_start(
            out=b.sq_t[0:n, 0:128], in_=src_.rearrange("(a p) -> a p", p=128)),
            [b.sq_war])
        war = b.psE_rot.war(0)
        t_d = k.T(lambda t, n=n: t.transpose(b.psE[:, 0:n], b.sq_t[0:n, 0:128],
                                             b.identF[0:n, 0:n]),
                  [("cn", ld), war])
        b.sq_war = [("T", t_d)]
        e_d = k.V(lambda v, col=col, n=n: v.tensor_copy(
            b.scl[:, col:col + n], b.psE[:, 0:n]), [("T", t_d)])
        b.psE_rot.done(0, "V", e_d)
    vb = k.V(lambda v: v.tensor_scalar(b.scl[:, 144:288], b.scl[:, 0:144],
                                       -128.0, None, ALU.mult),
             [("V", k.cnt("V"))])
    b.l1s_ready = [("V", vb)]


def _slab_ns(scr, kind, idx):
    if kind == "e":
        return NS(gs=[scr.fc1s[idx, fp] for fp in range(6)],
                  us=[scr.fc2s[idx, fp] for fp in range(6)],
                  ds=[scr.projs[idx, cb] for cb in range(4)])
    if kind == "s":
        return NS(gs=[scr.sfc1s[idx, fp] for fp in range(6)],
                  us=[scr.sfc2s[idx, fp] for fp in range(6)],
                  ds=[scr.sprojs[idx, cb] for cb in range(4)])
    return NS(gs=[scr.l1gs[idx, fp] for fp in range(4)],
              us=[scr.l1us[idx, fp] for fp in range(4)],
              ds=[scr.l1ds[idx, cb] for cb in range(4)])


def emit_ffn(k, b, kind, wsrc, tt_done, scr=None, second=False):
    if kind == "l0":
        _emit_router(k, b, wsrc)
        for e in range(8):
            sl = _slab_ns(scr, "e", e) if scr else None
            _emit_ffn_group(
                k, b, tt_done, nfj=11, scale_mode=("comb", e), gid=("e", e),
                gate_src=lambda fj, e=e: wsrc.fc1[e, 128*fj:128*fj+128, :],
                up_src=lambda fj, e=e: wsrc.fc2[e, 128*fj:128*fj+128, :],
                down_src=lambda cq, e=e: wsrc.proj[e, 128*cq:128*cq+128, :],
                slab=sl if second else None,
                cast_w=b.castdone.get("l0") if second else None,
                store=sl if (scr and not second) else None)
        for gi in range(4):
            sl = _slab_ns(scr, "s", gi) if scr else None
            fj0 = 11 * gi
            _emit_ffn_group(
                k, b, tt_done, nfj=11, scale_mode=("sgate", None), gid=("s", gi),
                gate_src=lambda fj, fj0=fj0: wsrc.sfc1[128*(fj0+fj):128*(fj0+fj)+128, :],
                up_src=lambda fj, fj0=fj0: wsrc.sfc2[128*(fj0+fj):128*(fj0+fj)+128, :],
                down_src=lambda cq, fj0=fj0: wsrc.sproj[128*cq:128*cq+128,
                                                        128*fj0:128*fj0+1408],
                slab=sl if second else None,
                cast_w=b.castdone.get("l0") if second else None,
                store=sl if (scr and not second) else None)
        if scr and not second:
            b.castdone["l0"] = ("sbL0", k.cnt("sbL0"))
    else:
        b.wnat_rot.free[1] = list(b.wnat_rot.free[1]) + list(b.xn_war)
        if not second:
            _emit_l1_scales(k, b, wsrc)
        for gi in range(8):
            sl = _slab_ns(scr, "l1", gi) if scr else None
            fj0 = 8 * gi
            def dqg(fj, fj0=fj0):
                i2 = fj0 + fj
                return (b.scl[:, i2:i2+1], b.scl[:, 144+i2:144+i2+1], b.l1s_ready)
            def dqu(fj, fj0=fj0):
                i2 = fj0 + fj
                return (b.scl[:, 64+i2:64+i2+1], b.scl[:, 208+i2:208+i2+1], b.l1s_ready)
            def dqd(cq):
                return (b.scl[:, 128+cq:128+cq+1], b.scl[:, 272+cq:272+cq+1], b.l1s_ready)
            _emit_ffn_group(
                k, b, tt_done, nfj=8, scale_mode=("const", 0.5), gid=("l1", gi),
                gate_src=lambda fj, fj0=fj0: wsrc.gate_w[128*(fj0+fj):128*(fj0+fj)+128, :],
                up_src=lambda fj, fj0=fj0: wsrc.up_w[128*(fj0+fj):128*(fj0+fj)+128, :],
                down_src=lambda cq, fj0=fj0: wsrc.down_w[128*cq:128*cq+128,
                                                         128*fj0:128*fj0+1024],
                dq_gate=None if second else dqg, dq_up=None if second else dqu,
                dq_down=None if second else dqd,
                slab=sl if second else None,
                cast_w=b.castdone.get("l1") if second else None,
                store=sl if (scr and not second) else None)
        if scr and not second:
            b.castdone["l1"] = ("sbL1", k.cnt("sbL1"))
    b.xn_war = [("T", k.cnt("T"))] + list(b.xn_war)
    return tt_done



def emit_att2_build(k, b, bias_dram, pbase):
    l1 = k.dma("cn", lambda g: g.dma_start(out=b.x_sb[0:1, 0:2048], in_=bias_dram),
               [b.att2_guard])
    s1 = k.S(lambda s: s.activation(b.xn_t[0:1, 0:2048], b.x_sb[0:1, 0:2048],
                                    AF.Identity), [("cn", l1), b.xn_war])
    l2 = k.dma("cn", lambda g: g.dma_start(out=b.att2[pbase:pbase+1, :],
                                    in_=b.xn_t[0:1, 0:2048]), [("S", s1)])
    v1 = k.V(lambda v: v.tensor_copy(b.x_sb[0:1, 2048:4096], b.xn_t[0:1, 0:2048]),
             [("S", s1)])
    v2 = k.V(lambda v: v.tensor_tensor(b.x_sb[0:1, 0:2048], b.x_sb[0:1, 0:2048],
                                       b.x_sb[0:1, 2048:4096], ALU.subtract),
             [("V", v1)])
    s2 = k.S(lambda s: s.activation(b.xn_t[0:1, 0:2048], b.x_sb[0:1, 0:2048],
                                    AF.Identity), [("V", v2), ("cn", l2)])
    l3 = k.dma("cn", lambda g: g.dma_start(out=b.att2[pbase+1:pbase+2, :],
                                    in_=b.xn_t[0:1, 0:2048]), [("S", s2)])
    b.att2_guard = [("cn", l3)]
    b.xn_war = [("S", s2)]


def emit_attn_wprep(k, b, layer, wsrc, scr=None):
    if layer == 0:
        last = 0
        for j2, wv in enumerate([wsrc.wq_s, wsrc.wk_s, wsrc.wv_s]):
            for cc in range(16):
                dst = b.arena[:, A_WQKV + 768*cc + 256*j2:
                              A_WQKV + 768*cc + 256*j2 + 256]
                last = k.dma("wq", lambda g, wv=wv, cc=cc, dst=dst: g.dma_start(
                    out=dst, in_=wv[128*cc:128*cc+128, :]), [b.wst_war])
        b.wqkv_ready = [("wq", last)]
    else:
        cw = k.DL(lambda g: g.dma_start(out=scr.wqkvb, in_=wsrc.wqkv_s), [])
        last = 0
        for cc in range(16):
            last = k.dma("wq", lambda q, cc=cc: q.dma_start_transpose(
                b.arena[:, A_WQKV + 768*cc: A_WQKV + 768*cc + 768],
                scr.wqkvb[0:768, 128*cc:128*cc+128]),
                ([("DL", cw), b.sbsb_guard, b.wst_war] if cc == 0 else []), q="sync")
        b.wqkv_ready = [("wq", last)]
        bl = 0
        for m in range(6):
            bl = k.dma("cn", lambda g, m=m: g.dma_start(
                out=b.comb[:, 72+m:72+m+1],
                in_=wsrc.bqkv_s[128*m:128*m+128].rearrange("(p a) -> p a", a=1)),
                [b.wst_war])
        vb = k.V(lambda v: v.tensor_scalar(b.comb[:, 78:80], b.comb[:, 72:74],
                                           RSQD, None, ALU.mult), [("cn", bl)])
        b.qkvbias_ready = [("V", vb)]


def emit_attn_qkv(k, b, layer, slab_src, q_dram):
    vo = []
    for h2 in range(2):
        ap = b.arena[:, A_V + VH*h2: A_V + VH*h2 + VH].rearrange(
            "p (a w) -> p a w", w=VW)[:, :, 128:129]
        vo.append(k.V(lambda v, ap=ap: v.memset(ap, 1.0), [b.vones_guard]))
    v_ones = ("V", vo[-1])
    qkv_bias = [b.qkvbias_ready] if layer == 1 else []

    for tb in range(8):
        for th in range(2):
            mm_last = 0
            for cc in range(16):
                slot = cc % 2
                ldw = k.dma("sl%d" % slot, lambda q, cc=cc, tb=tb, th=th, slot=slot:
                            q.dma_start(out=AP_(b, A_SLAB + 512*slot, 512),
                                        in_=slab_src(cc, tb, th)),
                            [b.slab_rot.war(slot), b.ag_guard], q="sync")
                wts = b.wqkv_ready + [("sl%d" % slot, ldw)]
                if cc == 0:
                    wts = wts + [b.ps_rot.war(i) for i in range(4)] +                           [b.psE_rot.war(0), b.psF_rot.war(0)]
                def mm(t, cc=cc, slot=slot):
                    r = None
                    for h2 in range(2):
                        t.matmul(b.ps[h2][:, 0:512],
                                 AP_(b, A_WQKV + 768*cc + 128*(2+h2), 128),
                                 AP_(b, A_SLAB + 512*slot, 512),
                                 start=(cc == 0), stop=(cc == 15))
                        t.matmul(b.ps[2+h2][:, 0:512],
                                 AP_(b, A_WQKV + 768*cc + 128*(4+h2), 128),
                                 AP_(b, A_SLAB + 512*slot, 512),
                                 start=(cc == 0), stop=(cc == 15))
                        r = t.matmul((b.psE if h2 == 0 else b.psF)[:, 0:512],
                                     AP_(b, A_WQKV + 768*cc + 128*h2, 128),
                                     AP_(b, A_SLAB + 512*slot, 512),
                                     start=(cc == 0), stop=(cc == 15))
                    return r
                mm_last = k.T(mm, wts + qkv_bias)
                b.slab_rot.done(slot, "T", mm_last)
            for h2 in range(2):
                kdst = AP_(b, A_K + 8192*h2 + 1024*tb + 512*th, 512)
                if layer == 1:
                    ek = k.S(lambda s, h2=h2, kdst=kdst: s.activation(
                        kdst, b.ps[h2][:, 0:512], AF.Identity,
                        bias=b.comb[:, 74+h2:74+h2+1]), [("T", mm_last)])
                else:
                    ek = k.S(lambda s, h2=h2, kdst=kdst: s.copy(
                        kdst, b.ps[h2][:, 0:512]), [("T", mm_last)])
                b.ps_rot.done(h2, "S", ek)
                ps_ = b.p_rot.war(0)
                stg = AP_(b, A_P, 512)
                if layer == 1:
                    ev = k.S(lambda s, h2=h2, stg=stg: s.activation(
                        stg, b.ps[2+h2][:, 0:512], AF.Identity,
                        bias=b.comb[:, 76+h2:76+h2+1]), [("T", mm_last), ps_])
                else:
                    ev = k.S(lambda s, h2=h2, stg=stg: s.copy(
                        stg, b.ps[2+h2][:, 0:512]), [("T", mm_last), ps_])
                b.ps_rot.done(2+h2, "S", ev)
                tile0 = 8*tb + 4*th
                base = A_V + VH*h2 + VW*tile0
                dst = b.arena[:, base:base + 4*VW].rearrange(
                    "p (a w) -> p a w", w=VW)[:, :, 0:128]
                srcs = [(lambda i=i, stg=stg: stg[:, 128*i:128*i+128])
                        for i in range(4)]
                temove_batch(k, b, srcs, dst, [("S", ev), v_ones], "V")
                b.p_rot.done(0, "T", k.cnt("T"))
                pq = b.pt_rot.war(0)
                qstg = AP_(b, A_PT, 512)
                qps = b.psE if h2 == 0 else b.psF
                if layer == 1:
                    eq = k.S(lambda s, qstg=qstg, qps=qps, h2=h2: s.activation(
                        qstg, qps[:, 0:512], AF.Identity, scale=RSQD,
                        bias=b.comb[:, 78+h2:78+h2+1]), [("T", mm_last), pq])
                else:
                    eq = k.S(lambda s, qstg=qstg, qps=qps: s.activation(
                        qstg, qps[:, 0:512], AF.Identity, scale=RSQD),
                        [("T", mm_last), pq])
                (b.psE_rot if h2 == 0 else b.psF_rot).done(0, "S", eq)
                dq = k.dma("qst", lambda g, h2=h2, tb=tb, th=th, qstg=qstg: g.dma_start(
                    out=q_dram[128*h2:128*h2+128, 1024*tb+512*th:1024*tb+512*th+512],
                    in_=qstg), [("S", eq)])
                b.pt_rot.done(0, "qst", dq)
    b.kv_ready = [("V", k.cnt("V")), ("S", k.cnt("S")), ("qst", k.cnt("qst"))]


def emit_attn_core(k, b, layer, bias_rows, q_dram, ctx_dram, bmax0=None):
    k.dma("cn", lambda g: g.dma_start(out=b.bias_sb[0:2, :], in_=bias_rows[0]),
          [b.bias_guard])
    k.dma("cn", lambda g: g.dma_start(out=b.bias_sb[32:34, :], in_=bias_rows[1]),
          [b.bias_guard])
    if layer == 0:
        k.dma("cn", lambda g: g.dma_start(out=b.bmax_sb[:, :], in_=bmax0),
              [b.bias_guard])
    bias_ready = bmax_ready = ("cn", k.cnt("cn"))
    ctxi = 0
    for quarter in range(2):
        for h2 in range(2):
            lq = k.dma("qld", lambda q, h2=h2, quarter=quarter: q.dma_start(
                out=AP_(b, A_Q, 4096),
                in_=q_dram[128*h2:128*h2+128, 4096*quarter:4096*quarter+4096]),
                [b.q_war] + b.kv_ready, q="sync")
            q_ready = ("qld", lq)
            for ql in range(32):
                qt = 32 * quarter + ql
                nkc = qt // 4 + 1
                rb = 32 * h2
                red = 0
                for i in range(nkc):
                    bank = b.psE if i % 2 == 0 else b.psF
                    rot = b.psE_rot if i % 2 == 0 else b.psF_rot
                    dvar = qt - 4 * i if i == nkc - 1 else None
                    def mma(t, i=i, bank=bank, ql=ql, rb=rb, dvar=dvar, h2=h2):
                        r = None
                        if layer == 1:
                            t.matmul(bank[:, 0:512], b.ones2[rb:rb+2, 0:128],
                                     b.bias_sb[rb:rb+2, 512*i:512*i+512],
                                     start=True, stop=False)
                        if dvar is not None:
                            t.matmul(bank[:, 0:512], b.ident[:, :],
                                     b.tri[:, 512*dvar:512*dvar+512],
                                     start=(layer == 0), stop=False)
                        return t.matmul(bank[:, 0:512],
                                        AP_(b, A_Q + 128*ql, 128),
                                        AP_(b, A_K + 8192*h2 + 512*i, 512),
                                        start=(layer == 0 and dvar is None),
                                        stop=True)
                    m_d = k.T(mma, [q_ready, bias_ready, rot.war(0)])
                    red = k.V(lambda v, i=i, bank=bank: v.reduce_max(
                        b.stats[:, 40+i:40+i+1], bank[:, 0:512], axis=AX.X),
                        [("T", m_d)])
                    rot.done(0, "V", red)
                if layer == 0:
                    for i in range(nkc):
                        red = k.V(lambda v, i=i, h2=h2: v.tensor_scalar(
                            b.stats[:, 40+i:40+i+1], b.stats[:, 40+i:40+i+1],
                            b.bmax_sb[:, 16*h2+i:16*h2+i+1], None, ALU.add),
                            [("V", red), bmax_ready])
                r1 = k.V(lambda v, nkc=nkc: v.reduce_max(
                    b.stats[:, 56:57], b.stats[:, 40:40+nkc], axis=AX.X),
                    [("V", red)])
                r2 = k.V(lambda v: v.tensor_scalar(
                    b.stats[:, 57:58], b.stats[:, 56:57], -1.0, None, ALU.mult),
                    [("V", r1)])
                negm = ("V", r2)
                pend = None
                pv_last = 0
                for i in range(nkc):
                    bank = b.psE if i % 2 == 0 else b.psF
                    rot = b.psE_rot if i % 2 == 0 else b.psF_rot
                    dvar = qt - 4 * i if i == nkc - 1 else None
                    def mmb(t, i=i, bank=bank, ql=ql, rb=rb, dvar=dvar, h2=h2):
                        t.matmul(bank[:, 0:512], b.ones2[rb:rb+2, 0:128],
                                 b.bias_sb[rb:rb+2, 512*i:512*i+512],
                                 start=True, stop=False)
                        if dvar is not None:
                            t.matmul(bank[:, 0:512], b.ident[:, :],
                                     b.tri[:, 512*dvar:512*dvar+512],
                                     start=False, stop=False)
                        return t.matmul(bank[:, 0:512],
                                        AP_(b, A_Q + 128*ql, 128),
                                        AP_(b, A_K + 8192*h2 + 512*i, 512),
                                        start=False, stop=True)
                    m_d = k.T(mmb, [rot.war(0)])
                    if pend is not None:
                        pv_last = _attn_pb_tail(k, b, pend, ctxi, h2, nkc)
                    pslot = i % 2
                    pw = b.p_rot.war(pslot)
                    ex = k.S(lambda s, bank=bank, pslot=pslot: s.activation(
                        AP_(b, A_P + 512*pslot, 512), bank[:, 0:512], AF.Exp,
                        bias=b.stats[:, 57:58]), [("T", m_d), negm, pw])
                    rot.done(0, "S", ex)
                    pend = (i, pslot, ex)
                pv_last = _attn_pb_tail(k, b, pend, ctxi, h2, nkc)
                rl = k.V(lambda v, ci=ctxi: v.reciprocal(
                    b.stats[:, 58:59], b.ps[ci][:, 128:129]), [("T", pv_last)])
                pw = b.p_rot.war(0)
                fs = k.S(lambda s, ci=ctxi: s.activation(
                    AP_(b, A_P, 128), b.ps[ci][:, 0:128], AF.Identity,
                    scale=b.stats[:, 58:59]), [("V", rl), pw])
                b.ps_rot.done(ctxi, "S", fs)
                ptw = b.pt_rot.war(0)
                ev = temove_batch(k, b, [lambda: AP_(b, A_P, 128)],
                                  AP_(b, A_PT, 128), [("S", fs), ptw], "V")
                b.p_rot.done(0, "T", k.cnt("T"))
                dc = k.dma("cst", lambda g, h2=h2, qt=qt: g.dma_start(
                    out=ctx_dram[128*h2:128*h2+128, 128*qt:128*qt+128],
                    in_=AP_(b, A_PT, 128)), [ev])
                b.pt_rot.done(0, "cst", dc)
                ctxi = (ctxi + 1) % 4
            b.q_war = [("T", k.cnt("T"))]
    b.ctx_done = [("cst", k.cnt("cst"))]


def _attn_pb_tail(k, b, pend, ctxi, h2, nkc):
    i, pslot, ex = pend
    p_ready = ("S", ex)
    ptw = b.pt_rot.war(pslot)
    srcs = [(lambda j=j, pslot=pslot: AP_(b, A_P + 512*pslot + 128*j, 128))
            for j in range(4)]
    ev = temove_batch(k, b, srcs, AP_(b, A_PT + 512*pslot, 512),
                      [p_ready, ptw], "V")
    b.p_rot.done(pslot, "T", k.cnt("T"))
    def mpv(t, i=i, pslot=pslot, h2=h2, ci=ctxi, nkc=nkc):
        r = None
        for j in range(4):
            tkb = 4*i + j
            r = t.matmul(b.ps[ci][:, 0:129],
                         AP_(b, A_PT + 512*pslot + 128*j, 128),
                         b.arena[:, A_V + VH*h2 + VW*tkb:
                                 A_V + VH*h2 + VW*tkb + 129],
                         start=(i == 0 and j == 0),
                         stop=(i == nkc - 1 and j == 3))
        return r
    w = [ev]
    if i == 0:
        w.append(b.ps_rot.war(ctxi))
    pv_last = k.T(mpv, w)
    b.pt_rot.done(pslot, "T", pv_last)
    return pv_last


def emit_attn_outproj(k, b, layer, wo_src, ctx_loader, tt_done, wo_cast=None):
    ctx_ready = ctx_loader(k, b)
    rb2 = 32 * layer
    for ob in range(4):
        for quad in range(2):
            for dc in range(16):
                slot = dc % 2
                wdst = AP_(b, A_WO + 512*slot, 512)
                if layer == 0:
                    lw = k.dma("wo%d" % slot, lambda g, ob=ob, dc=dc, wdst=wdst:
                               g.dma_start(out=wdst,
                                           in_=wo_src[128*dc:128*dc+128, 512*ob:512*ob+512]),
                               [b.wo_rot.war(slot)])
                    w_ready = [("wo%d" % slot, lw)]
                else:
                    lw = k.dma("wo%d" % slot, lambda q, ob=ob, dc=dc, wdst=wdst:
                               q.dma_start_transpose(
                                   wdst, wo_src[512*ob:512*ob+512, 128*dc:128*dc+128]),
                               [b.wo_rot.war(slot), b.sbsb_guard, wo_cast], q="sync")
                    w_ready = [("wo%d" % slot, lw)]
                wts = w_ready + [ctx_ready]
                if dc == 0:
                    wts += [b.ps_rot.war(i) for i in range(4)]
                def mmo(t, ob=ob, quad=quad, dc=dc, slot=slot, rb2=rb2):
                    r = None
                    if dc == 0:
                        for ti in range(4):
                            t.matmul(b.ps[ti][:, 0:512], b.ones2[rb2:rb2+2, 0:128],
                                     b.att2[rb2:rb2+2, 512*ob:512*ob+512],
                                     start=True, stop=False)
                    for ti in range(4):
                        tt = 4*quad + ti
                        r = t.matmul(b.ps[ti][:, 0:512],
                                     AP_(b, A_CTX + 1024*dc + 128*tt, 128),
                                     AP_(b, A_WO + 512*slot, 512),
                                     start=False, stop=(dc == 15))
                    return r
                mo = k.T(mmo, wts + [b.att2_ready])
                b.wo_rot.done(slot, "T", mo)
            mm_fin = ("T", k.cnt("T"))
            for ti in range(4):
                tt = 4*quad + ti
                xs = b.x_sb[:, 2048*tt + 512*ob: 2048*tt + 512*ob + 512]
                u_d = k.V(lambda v, ti=ti, xs=xs: v.tensor_tensor(
                    xs, xs, b.ps[ti][:, 0:512], ALU.add),
                    [mm_fin, tt_done[tt]])
                tt_done[tt] = ("V", u_d)
                b.ps_rot.done(ti, "V", u_d)
    b.ctxbuf_war = [("T", k.cnt("T"))]
    return tt_done



def host_consts():
    ident = np.eye(128, dtype=BF)
    ones2 = np.ones((34, 128), dtype=BF)
    tri = np.zeros((128, 4 * 512), dtype=BF)
    for d in range(4):
        i = np.arange(128)[:, None]
        jj = np.arange(512)[None, :]
        tri[:, 512*d:512*d+512] = np.where((jj - i) <= 128 * d, 0.0, -30000.0).astype(BF)
    return dict(cn_ident=ident, cn_ones2=ones2, cn_tri=tri,
                cn_identF=np.eye(128, dtype=np.float32))


def emit_const_loads(k, b, cn):
    l1 = k.DL(lambda g: g.dma_start(out=b.ident[:, :], in_=cn.ident))
    l2 = k.DL(lambda g: g.dma_start(out=b.ones2[:, :], in_=cn.ones2))
    l3 = k.DL(lambda g: g.dma_start(out=b.tri[:, :], in_=cn.tri))
    l4 = k.DL(lambda g: g.dma_start(out=b.identF[:, :], in_=cn.identF))
    return ("DL", l4)



def build_ffn_launch(kind, with_xnT=False):
    nc = bass.Bass("TRN2", target_bir_lowering=False)
    x_in = nc.dram_tensor("x_in", [TLOC, C], F32, kind="ExternalInput").ap()
    x_out = nc.dram_tensor("x_out", [TLOC, C], F32, kind="ExternalOutput").ap()
    xnT_out = (nc.dram_tensor("xnT_out", [C, TLOC], BF16, kind="ExternalOutput").ap()
               if with_xnT else None)
    cn = NS(ident=nc.dram_tensor("cn_ident", [128, 128], BF16, kind="ExternalInput").ap(),
            ones2=nc.dram_tensor("cn_ones2", [34, 128], BF16, kind="ExternalInput").ap(),
            tri=nc.dram_tensor("cn_tri", [128, 2048], BF16, kind="ExternalInput").ap(),
            identF=nc.dram_tensor("cn_identF", [128, 128], F32, kind="ExternalInput").ap())
    if kind == "l0":
        wsrc = NS(
            gate_w=nc.dram_tensor("gate_w", [8, C], F32, kind="ExternalInput").ap(),
            sgw=nc.dram_tensor("sgw", [1, C], F32, kind="ExternalInput").ap(),
            fc1=nc.dram_tensor("fc1", [8, F_MOE, C], F32, kind="ExternalInput").ap(),
            fc2=nc.dram_tensor("fc2", [8, F_MOE, C], F32, kind="ExternalInput").ap(),
            proj=nc.dram_tensor("proj", [8, C, F_MOE], F32, kind="ExternalInput").ap(),
            sfc1=nc.dram_tensor("sfc1", [F_SH, C], F32, kind="ExternalInput").ap(),
            sfc2=nc.dram_tensor("sfc2", [F_SH, C], F32, kind="ExternalInput").ap(),
            sproj=nc.dram_tensor("sproj", [C, F_SH], F32, kind="ExternalInput").ap(),
        )
    else:
        wsrc = NS(
            gate_w=nc.dram_tensor("gate_w", [I_L1, C], I32, kind="ExternalInput").ap(),
            up_w=nc.dram_tensor("up_w", [I_L1, C], I32, kind="ExternalInput").ap(),
            down_w=nc.dram_tensor("down_w", [C, I_L1], I32, kind="ExternalInput").ap(),
            gate_scale=nc.dram_tensor("gate_scale", [I_L1], F32, kind="ExternalInput").ap(),
            up_scale=nc.dram_tensor("up_scale", [I_L1], F32, kind="ExternalInput").ap(),
            down_scale=nc.dram_tensor("down_scale", [C], F32, kind="ExternalInput").ap(),
        )
    es = ExitStack()
    k = K(nc, es)
    b = make_bufs(k)
    emit_const_loads(k, b, cn)
    b.sbsb_guard = ("T", 0)
    xl = k.dma("xin", lambda g: g.dma_start(
        out=b.x_sb[:, :].rearrange("p (a c) -> p a c", c=2048),
        in_=x_in.rearrange("(a p) c -> p a c", p=128)))
    tt_done = [("xin", xl)] * NTT
    for tt in range(NTT):
        emit_ln_tile(k, b, tt, [("xin", xl)])
    b.xnT_done = [("V", k.cnt("V")), ("S", k.cnt("S"))]
    tt_done = emit_ffn(k, b, kind, wsrc, tt_done)
    st_last = k.dma("xout", lambda g: g.dma_start(
        out=x_out.rearrange("(a p) c -> p a c", p=128),
        in_=b.x_sb[:, :].rearrange("p (a c) -> p a c", c=2048)), tt_done)
    if with_xnT:
        tguard = ("T", k.cnt("T"))
        for tt in range(NTT):
            emit_ln_tile(k, b, tt, [tt_done[tt], tguard])
        xnT_done = [("V", k.cnt("V")), ("S", k.cnt("S"))]
        st2 = k.dma("xno", lambda g: g.dma_start(
            out=xnT_out.rearrange("(a p) w -> p a w", p=128),
            in_=b.arena[:, A_XNT:A_XNT+16384].rearrange("p (a w) -> p a w", w=1024)),
            xnT_done)
        k.waiter("G", [("xout", st_last), ("xno", st2)])
    else:
        k.waiter("G", [("xout", st_last)])
    k.finalize()
    es.close()
    return nc


def build_attn_core_launch(layer):
    nc = bass.Bass("TRN2", target_bir_lowering=False)
    xnT_full = nc.dram_tensor("xnT_full", [C, SEQ], BF16, kind="ExternalInput").ap()
    brows = nc.dram_tensor("cn_brows", [2, 2, SEQ], BF16, kind="ExternalInput").ap()
    bmax0 = nc.dram_tensor("cn_bmax0", [128, 32], F32, kind="ExternalInput").ap() \
        if layer == 0 else None
    ctx_out = nc.dram_tensor("ctx_out", [2 * D, SEQ], BF16, kind="ExternalOutput").ap()
    cn = NS(ident=nc.dram_tensor("cn_ident", [128, 128], BF16, kind="ExternalInput").ap(),
            ones2=nc.dram_tensor("cn_ones2", [34, 128], BF16, kind="ExternalInput").ap(),
            tri=nc.dram_tensor("cn_tri", [128, 2048], BF16, kind="ExternalInput").ap(),
            identF=nc.dram_tensor("cn_identF", [128, 128], F32, kind="ExternalInput").ap())
    if layer == 0:
        wsrc = NS(wq_s=nc.dram_tensor("wq_s", [C, 256], F32, kind="ExternalInput").ap(),
                  wk_s=nc.dram_tensor("wk_s", [C, 256], F32, kind="ExternalInput").ap(),
                  wv_s=nc.dram_tensor("wv_s", [C, 256], F32, kind="ExternalInput").ap())
    else:
        wsrc = NS(wqkv_s=nc.dram_tensor("wqkv_s", [768, C], F32, kind="ExternalInput").ap(),
                  bqkv_s=nc.dram_tensor("bqkv_s", [768], F32, kind="ExternalInput").ap())
    q_dram = nc.dram_tensor("q_dram", [2 * D, SEQ], BF16, kind="Internal").ap()
    scr = NS(wqkvb=nc.dram_tensor("wqkvb", [768, C], BF16, kind="Internal").ap()) \
        if layer == 1 else None
    es = ExitStack()
    k = K(nc, es)
    b = make_bufs(k)
    emit_const_loads(k, b, cn)
    b.sbsb_guard = ("T", 0)
    emit_attn_wprep(k, b, layer, wsrc, scr)
    slab_src = lambda cc, tb, th: xnT_full[128*cc:128*cc+128,
                                           1024*tb+512*th:1024*tb+512*th+512]
    emit_attn_qkv(k, b, layer, slab_src, q_dram)
    emit_attn_core(k, b, layer, [brows[0], brows[1]], q_dram, ctx_out, bmax0=bmax0)
    k.waiter("G", [b.ctx_done])
    k.finalize()
    es.close()
    return nc


def build_outproj_launch(layer):
    nc = bass.Bass("TRN2", target_bir_lowering=False)
    x_in = nc.dram_tensor("x_in", [TLOC, C], F32, kind="ExternalInput").ap()
    x_out = nc.dram_tensor("x_out", [TLOC, C], F32, kind="ExternalOutput").ap()
    ctx_own = nc.dram_tensor("ctx_own", [C, TLOC], BF16, kind="ExternalInput").ap()
    wo = nc.dram_tensor("wo", [C, C], F32, kind="ExternalInput").ap()
    bias2 = nc.dram_tensor("bias2", [1, C], F32, kind="ExternalInput").ap()
    cn = NS(ident=nc.dram_tensor("cn_ident", [128, 128], BF16, kind="ExternalInput").ap(),
            ones2=nc.dram_tensor("cn_ones2", [34, 128], BF16, kind="ExternalInput").ap(),
            tri=nc.dram_tensor("cn_tri", [128, 2048], BF16, kind="ExternalInput").ap(),
            identF=nc.dram_tensor("cn_identF", [128, 128], F32, kind="ExternalInput").ap())
    woutb = (nc.dram_tensor("woutb", [C, C], BF16, kind="Internal").ap()
             if layer == 1 else None)
    es = ExitStack()
    k = K(nc, es)
    b = make_bufs(k)
    emit_const_loads(k, b, cn)
    wo_cast = None
    if layer == 1:
        cwo = k.DL(lambda g: g.dma_start(out=woutb, in_=wo), [])
        wo_cast = ("DL", cwo)
    emit_att2_build(k, b, bias2, 32 * layer)
    b.att2_ready = b.att2_guard
    b.sbsb_guard = b.att2_guard
    xl = k.dma("xin", lambda g: g.dma_start(
        out=b.x_sb[:, :].rearrange("p (a c) -> p a c", c=2048),
        in_=x_in.rearrange("(a p) c -> p a c", p=128)), [b.att2_guard])
    tt_done = [("xin", xl)] * NTT
    def ctx_loader(k2, b2):
        ld = k2.dma("ctx", lambda g: g.dma_start(
            out=b2.arena[:, A_CTX:A_CTX+16384].rearrange("p (a w) -> p a w", w=1024),
            in_=ctx_own.rearrange("(a p) w -> p a w", p=128)), [b2.ctxbuf_war])
        return ("ctx", ld)
    tt_done = emit_attn_outproj(k, b, layer, woutb if layer == 1 else wo,
                                ctx_loader, tt_done, wo_cast=wo_cast)
    st_last = k.dma("xout", lambda g: g.dma_start(
        out=x_out.rearrange("(a p) c -> p a c", p=128),
        in_=b.x_sb[:, :].rearrange("p (a c) -> p a c", c=2048)), tt_done)
    k.waiter("G", [("xout", st_last)])
    k.finalize()
    es.close()
    return nc


def build_fused(n_sub=6):
    nc = bass.Bass("TRN2", target_bir_lowering=False)
    RG = [list(range(NCORES))]
    x_in = nc.dram_tensor("x_in", [TLOC, C], F32, kind="ExternalInput").ap()
    x_out = nc.dram_tensor("x_out", [TLOC, C], F32, kind="ExternalOutput").ap()
    cn = NS(ident=nc.dram_tensor("cn_ident", [128, 128], BF16, kind="ExternalInput").ap(),
            ones2=nc.dram_tensor("cn_ones2", [34, 128], BF16, kind="ExternalInput").ap(),
            tri=nc.dram_tensor("cn_tri", [128, 2048], BF16, kind="ExternalInput").ap(),
            identF=nc.dram_tensor("cn_identF", [128, 128], F32, kind="ExternalInput").ap())
    brows = nc.dram_tensor("cn_brows", [2, 2, 2, SEQ], BF16, kind="ExternalInput").ap()
    bmax0 = nc.dram_tensor("cn_bmax0", [128, 32], F32, kind="ExternalInput").ap()
    w0 = NS(gate_w=nc.dram_tensor("gate_w", [8, C], F32, kind="ExternalInput").ap(),
            sgw=nc.dram_tensor("sgw", [1, C], F32, kind="ExternalInput").ap(),
            fc1=nc.dram_tensor("fc1", [8, F_MOE, C], F32, kind="ExternalInput").ap(),
            fc2=nc.dram_tensor("fc2", [8, F_MOE, C], F32, kind="ExternalInput").ap(),
            proj=nc.dram_tensor("proj", [8, C, F_MOE], F32, kind="ExternalInput").ap(),
            sfc1=nc.dram_tensor("sfc1", [F_SH, C], F32, kind="ExternalInput").ap(),
            sfc2=nc.dram_tensor("sfc2", [F_SH, C], F32, kind="ExternalInput").ap(),
            sproj=nc.dram_tensor("sproj", [C, F_SH], F32, kind="ExternalInput").ap())
    w1 = NS(gate_w=nc.dram_tensor("l1_gate_w", [I_L1, C], I32, kind="ExternalInput").ap(),
            up_w=nc.dram_tensor("l1_up_w", [I_L1, C], I32, kind="ExternalInput").ap(),
            down_w=nc.dram_tensor("l1_down_w", [C, I_L1], I32, kind="ExternalInput").ap(),
            gate_scale=nc.dram_tensor("l1_gate_scale", [I_L1], F32, kind="ExternalInput").ap(),
            up_scale=nc.dram_tensor("l1_up_scale", [I_L1], F32, kind="ExternalInput").ap(),
            down_scale=nc.dram_tensor("l1_down_scale", [C], F32, kind="ExternalInput").ap())
    wa0 = NS(wq_s=nc.dram_tensor("wq_s", [C, 256], F32, kind="ExternalInput").ap(),
             wk_s=nc.dram_tensor("wk_s", [C, 256], F32, kind="ExternalInput").ap(),
             wv_s=nc.dram_tensor("wv_s", [C, 256], F32, kind="ExternalInput").ap())
    wa1 = NS(wqkv_s=nc.dram_tensor("wqkv_s", [768, C], F32, kind="ExternalInput").ap(),
             bqkv_s=nc.dram_tensor("bqkv_s", [768], F32, kind="ExternalInput").ap())
    wo0 = nc.dram_tensor("wo0", [C, C], F32, kind="ExternalInput").ap()
    wo1 = nc.dram_tensor("wo1", [C, C], F32, kind="ExternalInput").ap()
    bo0 = nc.dram_tensor("bo0", [1, C], F32, kind="ExternalInput").ap()
    bo1 = nc.dram_tensor("bo1", [1, C], F32, kind="ExternalInput").ap()

    scr = NS(fc1s=nc.dram_tensor("fc1s", [8, 6, 128, 4096], BF16, kind="Internal").ap(),
             fc2s=nc.dram_tensor("fc2s", [8, 6, 128, 4096], BF16, kind="Internal").ap(),
             projs=nc.dram_tensor("projs", [8, 4, 128, 5632], BF16, kind="Internal").ap(),
             sfc1s=nc.dram_tensor("sfc1s", [4, 6, 128, 4096], BF16, kind="Internal").ap(),
             sfc2s=nc.dram_tensor("sfc2s", [4, 6, 128, 4096], BF16, kind="Internal").ap(),
             sprojs=nc.dram_tensor("sprojs", [4, 4, 128, 5632], BF16, kind="Internal").ap(),
             l1gs=nc.dram_tensor("l1gs", [8, 4, 128, 4096], BF16, kind="Internal").ap(),
             l1us=nc.dram_tensor("l1us", [8, 4, 128, 4096], BF16, kind="Internal").ap(),
             l1ds=nc.dram_tensor("l1ds", [8, 4, 128, 4096], BF16, kind="Internal").ap(),
             wqkvb=nc.dram_tensor("wqkvb", [768, C], BF16, kind="Internal").ap(),
             woutb=nc.dram_tensor("woutb", [C, C], BF16, kind="Internal").ap())
    xn_stage = nc.dram_tensor("xn_stage", [C, TLOC], BF16, kind="Internal")
    xnT_all = nc.dram_tensor("xnT_all", [NCORES * C, TLOC], BF16, kind="Internal")
    ctx_stage = nc.dram_tensor("ctx_stage", [2 * D, SEQ], BF16, kind="Internal")
    ctx_all = nc.dram_tensor("ctx_all", [C, SEQ], BF16, kind="Internal")
    q_dram = nc.dram_tensor("q_dram", [2 * D, SEQ], BF16, kind="Internal").ap()

    es = ExitStack()
    k = K(nc, es)
    b = make_bufs(k)
    emit_const_loads(k, b, cn)
    b.pid_ld = k.dma("cn", lambda g: g.dma_start(out=b.pid_sb[0:1, 0:1],
                                                 in_=nc.partition_id_tensor.ap()))
    cwo = k.DL(lambda g: g.dma_start(out=scr.woutb, in_=wo1), [])
    b.castdone["wout"] = ("DL", cwo)
    emit_att2_build(k, b, bo0, 0)
    emit_att2_build(k, b, bo1, 32)
    b.att2_ready = b.att2_guard
    b.sbsb_guard = b.att2_guard
    xl = k.dma("xin", lambda g: g.dma_start(
        out=b.x_sb[:, :].rearrange("p (a c) -> p a c", c=2048),
        in_=x_in.rearrange("(a p) c -> p a c", p=128)), b.att2_guard)
    tt_done = [("xin", xl)] * NTT

    def ln_phase(pg):
        emit_ln_all(k, b, [[tt_done[tt]] + pg for tt in range(NTT)])

    def attn_sublayer(layer, wsrc_a, wo_src):
        pg = phase_boundary(k, b)
        emit_attn_wprep(k, b, layer, wsrc_a, scr)
        ln_phase(pg)
        st = k.dma("xns", lambda g: g.dma_start(
            out=xn_stage.ap().rearrange("(a p) w -> p a w", p=128),
            in_=b.arena[:, A_XNT:A_XNT+16384].rearrange("p (a w) -> p a w", w=1024)),
            b.xnT_done + pg)
        ag1 = k.CC(lambda g: g.collective_compute(
            "AllGather", ALU.bypass, RG, ins=[xn_stage.ap().opt()],
            outs=[xnT_all.ap().opt()]), [("xns", st)] + pg)
        xnT_all_ap = xnT_all.ap()
        slab_src = lambda cc, tb, th: xnT_all_ap[
            2048*tb + 128*cc: 2048*tb + 128*cc + 128,
            512*th: 512*th + 512]
        b.ag_guard = ("CC", k.cnt("CC"))
        pg2 = phase_boundary(k, b)
        emit_attn_qkv(k, b, layer, slab_src, q_dram)
        emit_attn_core(k, b, layer, [brows[layer, 0], brows[layer, 1]],
                       q_dram, ctx_stage.ap(),
                       bmax0=bmax0 if layer == 0 else None)
        pgc = phase_boundary(k, b)
        ag2 = k.CC(lambda g: g.collective_compute(
            "AllGather", ALU.bypass, RG, ins=[ctx_stage.ap().opt()],
            outs=[ctx_all.ap().opt()]), [b.ctx_done] + pgc)
        b.ctxbuf_war = pgc + [("CC", k.cnt("CC"))]
        ctx_all_ap = ctx_all.ap()
        def ctx_loader(k2, b2):
            ent = k2.sem_ent("ctx")
            acc = {}
            for (sn, v) in k2._flat([b2.ctxbuf_war, ("cn", b2.pid_ld)]):
                if v > 0 and v > acc.get(sn, 0):
                    acc[sn] = v
            wl = sorted(acc.items())

            def f(g):
                with g.register() as r:
                    g.reg_load(r, b2.pid_sb[0:1, 0:1])
                    g.reg_mul(r, r, TLOC)
                    off = g.snap(r, min_val=0, max_val=TLOC * (NCORES - 1))
                    src_ap = ctx_all_ap[:, bass.ds(off, TLOC)].rearrange(
                        "(a p) w -> p a w", p=128)
                    dst_ap = b2.arena[:, A_CTX:A_CTX + 16384].rearrange(
                        "p (a w) -> p a w", w=1024)
                    g.dma_start(out=dst_ap, in_=src_ap).then_inc(ent[0], 16)
            k2.dq["gpsimd"].append((wl, f, None, 0, ent[1] + 16))
            ent[1] += 16
            return ("ctx", ent[1])
        emit_attn_outproj(k, b, layer, wo_src, ctx_loader, tt_done,
                          wo_cast=(b.castdone["wout"] if layer == 1 else None))
        phase_boundary(k, b)

    def ffn_sub(kind, w, second=False):
        pg = phase_boundary(k, b)
        ln_phase(pg)
        emit_ffn(k, b, kind, w, tt_done, scr=scr, second=second)
    subs = [lambda: ffn_sub("l0", w0),
            lambda: attn_sublayer(0, wa0, wo0),
            lambda: ffn_sub("l0", w0, second=True),
            lambda: ffn_sub("l1", w1),
            lambda: attn_sublayer(1, wa1, scr.woutb),
            lambda: ffn_sub("l1", w1, second=True)]
    for f in subs[:n_sub]:
        f()

    st_last = k.dma("xout", lambda g: g.dma_start(
        out=x_out.rearrange("(a p) c -> p a c", p=128),
        in_=b.x_sb[:, :].rearrange("p (a c) -> p a c", c=2048)), tt_done)
    k.waiter("G", [("xout", st_last)])
    k.finalize()
    es.close()
    return nc



_L0_KEYS = dict(gate_w="L0_ffn_gate_w", sgw="L0_ffn_shared_gate_w",
                fc1="L0_ffn_expert_fc1", fc2="L0_ffn_expert_fc2",
                proj="L0_ffn_expert_proj", sfc1="L0_ffn_shared_fc1",
                sfc2="L0_ffn_shared_fc2", sproj="L0_ffn_shared_proj")
_L1_KEYS = dict(gate_w="L1_ffn_gate_w", up_w="L1_ffn_up_w", down_w="L1_ffn_down_w",
                gate_scale="L1_ffn_gate_scale", up_scale="L1_ffn_up_scale",
                down_scale="L1_ffn_down_scale")

_CACHE = {}


def _get(name, builder):
    if name not in _CACHE:
        _CACHE[name] = builder()
    return _CACHE[name]


def host_bias_rows(layer, h):
    j = np.arange(SEQ, dtype=np.float64)
    slope = 2.0 ** (-(h + 1) / 2.0)
    if layer == 0:
        row = slope * j * (1.0 / np.sqrt(np.float64(D)))
    else:
        row = (j - (SEQ - 1)) * slope
    hi = row.astype(np.float32).astype(BF)
    lo = (row - hi.astype(np.float64)).astype(np.float32).astype(BF)
    return np.stack([hi, lo])


def host_bmax0(c):
    out = np.zeros((128, 32), dtype=np.float32)
    for h2 in range(2):
        slope = 2.0 ** (-(2 * c + h2 + 1) / 2.0)
        for i in range(16):
            out[:, 16*h2 + i] = np.float32(
                slope * (512.0 * i + 511.0) / np.sqrt(np.float64(D)))
    return out


def run_ffn_step(kind, x_full, inputs, with_xnT=False):
    nc = _get(("ffn", kind, with_xnT), lambda: build_ffn_launch(kind, with_xnT))
    cns = host_consts()
    keys = _L0_KEYS if kind == "l0" else _L1_KEYS
    wmap = {k2: np.ascontiguousarray(np.asarray(inputs[v])) for k2, v in keys.items()}
    if kind == "l0":
        wmap["sgw"] = wmap["sgw"].reshape(1, C)
    in_maps = [dict(x_in=np.ascontiguousarray(x_full[TLOC*c:TLOC*(c+1)]), **wmap, **cns)
               for c in range(NCORES)]
    res = run_bass_kernel_spmd(nc, in_maps, core_ids=list(range(NCORES)))
    x_new = np.concatenate([res.results[c]["x_out"] for c in range(NCORES)], 0)
    if with_xnT:
        xnT = np.concatenate([res.results[c]["xnT_out"] for c in range(NCORES)], 1)
        return x_new, xnT
    return x_new


def run_attn_step(layer, x_full, xnT_full, inputs):
    nc_a = _get(("attn", layer), lambda: build_attn_core_launch(layer))
    cns = host_consts()
    in_maps = []
    for c in range(NCORES):
        m = dict(xnT_full=np.ascontiguousarray(xnT_full), **cns)
        m["cn_brows"] = np.ascontiguousarray(
            np.stack([host_bias_rows(layer, 2*c), host_bias_rows(layer, 2*c+1)]))
        if layer == 0:
            m["cn_bmax0"] = host_bmax0(c)
            for nm, key in (("wq_s", "L0_attn_Wq"), ("wk_s", "L0_attn_Wk"),
                            ("wv_s", "L0_attn_Wv")):
                m[nm] = np.ascontiguousarray(np.asarray(inputs[key])[:, 256*c:256*c+256])
        else:
            Wqkv = np.asarray(inputs["L1_attn_Wqkv"])
            bq = np.asarray(inputs["L1_attn_b_qkv"])
            rows = np.concatenate([Wqkv[256*c:256*c+256],
                                   Wqkv[2048+256*c:2048+256*c+256],
                                   Wqkv[4096+256*c:4096+256*c+256]], 0)
            m["wqkv_s"] = np.ascontiguousarray(rows)
            m["bqkv_s"] = np.ascontiguousarray(np.concatenate(
                [bq[256*c:256*c+256], bq[2048+256*c:2048+256*c+256],
                 bq[4096+256*c:4096+256*c+256]]))
        in_maps.append(m)
    res = run_bass_kernel_spmd(nc_a, in_maps, core_ids=list(range(NCORES)))
    ctx_full = np.concatenate([res.results[c]["ctx_out"] for c in range(NCORES)], 0)

    nc_o = _get(("outproj", layer), lambda: build_outproj_launch(layer))
    wo = np.ascontiguousarray(np.asarray(
        inputs["L0_attn_Wo" if layer == 0 else "L1_attn_Wout"]))
    b2 = np.ascontiguousarray(np.asarray(
        inputs["L0_attn_bo" if layer == 0 else "L1_attn_b_out"]).reshape(1, C))
    in_maps = [dict(x_in=np.ascontiguousarray(x_full[TLOC*c:TLOC*(c+1)]),
                    ctx_own=np.ascontiguousarray(ctx_full[:, TLOC*c:TLOC*(c+1)]),
                    wo=wo, bias2=b2, **cns)
               for c in range(NCORES)]
    res = run_bass_kernel_spmd(nc_o, in_maps, core_ids=list(range(NCORES)))
    return np.concatenate([res.results[c]["x_out"] for c in range(NCORES)], 0)


def kernel_unfused(**inputs):
    x = np.ascontiguousarray(np.asarray(inputs["L0_attn_x"], np.float32).reshape(SEQ, C))
    x, xnT = run_ffn_step("l0", x, inputs, with_xnT=True)
    x = run_attn_step(0, x, xnT, inputs)
    x = run_ffn_step("l0", x, inputs)
    x, xnT = run_ffn_step("l1", x, inputs, with_xnT=True)
    x = run_attn_step(1, x, xnT, inputs)
    x = run_ffn_step("l1", x, inputs)
    return x.reshape(1, SEQ, C)


def kernel(**inputs, ):
    n_sub = int(inputs.pop("_n_sub", 6))
    x = np.ascontiguousarray(np.asarray(inputs["L0_attn_x"], np.float32).reshape(SEQ, C))
    nc = _get(("fused", n_sub), lambda: build_fused(n_sub))
    cns = host_consts()
    wmap = {}
    for k2, v in _L0_KEYS.items():
        wmap[k2] = np.ascontiguousarray(np.asarray(inputs[v]))
    wmap["sgw"] = wmap["sgw"].reshape(1, C)
    for k2, v in _L1_KEYS.items():
        wmap["l1_" + k2] = np.ascontiguousarray(np.asarray(inputs[v]))
    wmap["wo0"] = np.ascontiguousarray(np.asarray(inputs["L0_attn_Wo"]))
    wmap["wo1"] = np.ascontiguousarray(np.asarray(inputs["L1_attn_Wout"]))
    wmap["bo0"] = np.ascontiguousarray(np.asarray(inputs["L0_attn_bo"]).reshape(1, C))
    wmap["bo1"] = np.ascontiguousarray(np.asarray(inputs["L1_attn_b_out"]).reshape(1, C))
    Wqkv = np.asarray(inputs["L1_attn_Wqkv"])
    bq = np.asarray(inputs["L1_attn_b_qkv"])
    in_maps = []
    for c in range(NCORES):
        m = dict(x_in=np.ascontiguousarray(x[TLOC*c:TLOC*(c+1)]), **wmap, **cns)
        m["cn_brows"] = np.ascontiguousarray(np.stack([
            np.stack([host_bias_rows(0, 2*c), host_bias_rows(0, 2*c+1)]),
            np.stack([host_bias_rows(1, 2*c), host_bias_rows(1, 2*c+1)])]))
        m["cn_bmax0"] = host_bmax0(c)
        for nm, key in (("wq_s", "L0_attn_Wq"), ("wk_s", "L0_attn_Wk"),
                        ("wv_s", "L0_attn_Wv")):
            m[nm] = np.ascontiguousarray(np.asarray(inputs[key])[:, 256*c:256*c+256])
        m["wqkv_s"] = np.ascontiguousarray(np.concatenate(
            [Wqkv[256*c:256*c+256], Wqkv[2048+256*c:2048+256*c+256],
             Wqkv[4096+256*c:4096+256*c+256]], 0))
        m["bqkv_s"] = np.ascontiguousarray(np.concatenate(
            [bq[256*c:256*c+256], bq[2048+256*c:2048+256*c+256],
             bq[4096+256*c:4096+256*c+256]]))
        in_maps.append(m)
    res = run_bass_kernel_spmd(nc, in_maps, core_ids=list(range(NCORES)))
    out = np.concatenate([res.results[c]["x_out"] for c in range(NCORES)], 0)
    return out.reshape(1, SEQ, C)


# version 09
# speedup vs baseline: 1.1764x; 1.0054x over previous
import numpy as np
import ml_dtypes
from contextlib import ExitStack

import concourse.bass as bass
import concourse.mybir as mybir
from concourse.bass_utils import run_bass_kernel_spmd

F32 = mybir.dt.float32
BF16 = mybir.dt.bfloat16
I32 = mybir.dt.int32
AF = mybir.ActivationFunctionType
ALU = mybir.AluOpType
AX = mybir.AxisListType

SEQ = 8192
C = 2048
NHEAD = 16
D = 128
NCORES = 8
TLOC = SEQ // NCORES
NTT = TLOC // 128
F_MOE = 1408
F_SH = 5632
I_L1 = 8192
EPS = 1e-5
RSQD = float(1.0 / np.sqrt(np.float64(D)))
BF = ml_dtypes.bfloat16

A_XNT = 0
A_WG = 16384
A_WU = 24576
A_H = 32768
A_PJ = 44032
A_SU = 49664
A_RW = 50176
A_SGW = 50304
A_WN0 = 50320
A_Q = 0
A_K = 4096
A_V = 20480
A_WQKV = 37120
A_SLAB = 49408
A_P = 50432
A_PT = 51456
A_WST = 50432
A_CTX = 0
A_WON = 16384
A_WO = 24576

VW = 130
VH = 64 * VW


class K:

    ENGS = ("tensor", "vector", "scalar", "gpsimd", "sync")

    def __init__(self, nc, es):
        self.nc = nc
        self.es = es
        self.blk = es.enter_context(nc.Block())
        self.sems = {}
        for n in ("T", "V", "S", "G", "DL", "DQ", "CC"):
            self.sems[n] = [es.enter_context(nc.semaphore(n)), 0]
        self.dq = {e: [] for e in self.ENGS}

    def cnt(self, name):
        return self.sems[name][1]

    @staticmethod
    def _flat(waits):
        out = []
        for w in waits:
            if w is None:
                continue
            if isinstance(w, list):
                out.extend(t for t in w if t is not None)
            else:
                out.append(w)
        return out

    def _collapse(self, waits):
        acc = {}
        for (sn, v) in self._flat(waits):
            if v > 0 and v > acc.get(sn, 0):
                acc[sn] = v
        return sorted(acc.items())

    def sem_ent(self, name):
        if name not in self.sems:
            self.sems[name] = [self.es.enter_context(self.nc.semaphore(name)), 0]
        return self.sems[name]

    def _emit(self, eng, fn, waits, sem, n):
        ent = self.sem_ent(sem)
        selfwait = ent[1] + n if sem in ("DL", "DQ") else None
        self.dq[eng].append((self._collapse(waits), fn, ent[0], n, selfwait))
        ent[1] += n
        return ent[1]

    def dma(self, ch, fn, waits=(), q="gpsimd"):
        ent = self.sem_ent(ch)
        self.dq[q].append((self._collapse(waits), fn, ent[0], 16, None))
        ent[1] += 16
        return ent[1]

    def T(self, fn, waits=(), n=1):   return self._emit("tensor", fn, waits, "T", n)
    def V(self, fn, waits=(), n=1):   return self._emit("vector", fn, waits, "V", n)
    def S(self, fn, waits=(), n=1):   return self._emit("scalar", fn, waits, "S", n)
    def G(self, fn, waits=(), n=1):   return self._emit("gpsimd", fn, waits, "G", n)
    def DL(self, fn, waits=(), n=16): return self._emit("gpsimd", fn, waits, "DL", n)
    def DQ(self, fn, waits=(), n=16): return self._emit("sync", fn, waits, "DQ", n)
    def CC(self, fn, waits=(), n=1):  return self._emit("gpsimd", fn, waits, "CC", n)

    def DLmulti(self, fns, waits=()):
        ent = self.sems["DL"]
        semh = ent[0]

        total = ent[1] + 16 * len(fns)

        def f(g):
            for fn in fns:
                fn(g).then_inc(semh, 16)
        self.dq["gpsimd"].append((self._collapse(waits), f, None, 0, total))
        ent[1] = total
        return ent[1]

    ENG = dict(T="tensor", V="vector", S="scalar", G="gpsimd", Q="sync")

    def waiter(self, eng, waits):
        self.dq[self.ENG[eng]].append((self._collapse(waits), None, None, 0, None))

    def finalize(self):
        hmap = self.sems
        for eng in self.ENGS:
            items = self.dq[eng]
            if not items:
                continue

            def body(e, items=items):
                for (wl, fn, semh, n, selfwait) in items:
                    for (sn, v) in wl:
                        e.wait_ge(hmap[sn][0], v)
                    if fn is None:
                        continue
                    r = fn(e)
                    if r is not None and semh is not None:
                        r.then_inc(semh, n)
                    if selfwait is not None:
                        sw = hmap["DL"] if (semh is hmap["DL"][0] or semh is None) else hmap["DQ"]
                        e.wait_ge(sw[0], selfwait)
            getattr(self.blk, eng)(body)


class Rot:
    def __init__(self, n):
        self.n = n
        self.free = [[("T", 0)]] * n

    def war(self, i):
        return self.free[i % self.n]

    def done(self, i, sem, val):
        self.free[i % self.n] = [(sem, val)]

    def set_all(self, waits):
        self.free = [list(waits)] * self.n


class NS:
    def __init__(self, **kw):
        self.__dict__.update(kw)


def make_bufs(k):
    nc, es = k.nc, k.es
    b = NS(
        arena=es.enter_context(nc.sbuf_tensor("arena", [128, 53248], BF16)),
        x_sb=es.enter_context(nc.sbuf_tensor("x_sb", [128, 8 * 2048], F32)),
        xn_t=es.enter_context(nc.sbuf_tensor("xn_t", [128, 2048], BF16)),
        sq_t=es.enter_context(nc.sbuf_tensor("sq_t", [128, 2048], F32)),
        stats=es.enter_context(nc.sbuf_tensor("stats", [128, 64], F32)),
        comb=es.enter_context(nc.sbuf_tensor("comb", [128, 80], F32)),
        scl=es.enter_context(nc.sbuf_tensor("scl", [128, 304], F32)),
        bias_sb=es.enter_context(nc.sbuf_tensor("bias_sb", [128, 8192], BF16)),
        bmax_sb=es.enter_context(nc.sbuf_tensor("bmax_sb", [128, 32], F32)),
        att2=es.enter_context(nc.sbuf_tensor("att2", [128, 2048], BF16)),
        ident=es.enter_context(nc.sbuf_tensor("ident", [128, 128], BF16)),
        identF=es.enter_context(nc.sbuf_tensor("identF", [128, 128], F32)),
        ones2=es.enter_context(nc.sbuf_tensor("ones2", [34, 128], BF16)),
        tri=es.enter_context(nc.sbuf_tensor("tri", [128, 4 * 512], BF16)),
        ps=[es.enter_context(nc.psum_tensor("ps%d" % i, [128, 512], F32)) for i in range(4)],
        psE=es.enter_context(nc.psum_tensor("psE", [128, 512], F32)),
        psF=es.enter_context(nc.psum_tensor("psF", [128, 512], F32)),
        psT=es.enter_context(nc.psum_tensor("psT", [128, 2048], BF16)),
        pid_sb=es.enter_context(nc.sbuf_tensor("pid_sb", [1, 2], mybir.dt.uint32)),
    )
    b.psTb = Rot(2)
    b.ps_rot = Rot(4)
    b.psE_rot = Rot(1)
    b.psF_rot = Rot(1)
    b.wg_rot = Rot(2)
    b.wu_rot = Rot(2)
    b.wnat_rot = Rot(2)
    b.pjnat_rot = Rot(2)
    b.pj_war = ("T", 0)
    b.h_war = ("T", 0)
    b.su_war = ("V", 0)
    b.sq_war = ("S", 0)
    b.xn_war = ("T", 0)
    b.xnT_done = ("V", 0)
    b.psT_bid = 0
    b.slab_rot = Rot(2)
    b.p_rot = Rot(2)
    b.pt_rot = Rot(2)
    b.wo_rot = Rot(2)
    b.q_war = ("T", 0)
    b.wst_war = ("T", 0)
    b.won_war = ("T", 0)
    b.ctxbuf_war = ("T", 0)
    b.vones_guard = ("T", 0)
    b.bias_guard = ("T", 0)
    b.att2_guard = ("T", 0)
    b.att2_ready = ("T", 0)
    b.wqkv_ready = []
    b.qkvbias_ready = ("V", 0)
    b.kv_ready = []
    b.ag_guard = None
    b.castdone = {}
    b.sbsb_guard = None
    return b


def phase_boundary(k, b):
    pg = [(s, ent[1]) for s, ent in k.sems.items()]
    for r in (b.psTb, b.ps_rot, b.psE_rot, b.psF_rot, b.wg_rot, b.wu_rot,
              b.wnat_rot, b.pjnat_rot, b.slab_rot, b.p_rot, b.pt_rot, b.wo_rot):
        r.set_all(pg)
    for name in ("pj_war", "h_war", "su_war", "sq_war", "xn_war", "q_war",
                 "wst_war", "won_war", "ctxbuf_war", "vones_guard", "bias_guard"):
        setattr(b, name, list(pg))
    return pg


def AP_(b, c0, w):
    return b.arena[:, c0:c0 + w]



def temove_batch(k, b, srcs, dst_ap, waits, eng, width=128, in_parts=128):
    n = len(srcs)
    pad = width if width % 2 == 0 else width + 1
    assert n * pad <= 1024
    bid = b.psT_bid % 2
    b.psT_bid += 1
    war = b.psTb.war(bid)
    base = 1024 * bid
    t_d = 0
    for i, sfn in enumerate(srcs):
        t_d = k.T(lambda t, i=i, sfn=sfn: t.transpose(
            b.psT[:, base + pad * i: base + pad * i + width],
            sfn(), b.ident[0:in_parts, 0:in_parts]),
            (list(waits) + [war]) if i == 0 else [])
    if pad == width:
        src_ap = b.psT[:, base:base + n * width]
    else:
        src_ap = b.psT[:, base:base + n * pad].rearrange(
            "p (a w) -> p a w", w=pad)[:, :, 0:width]
    if eng == "V":
        e_d = k.V(lambda v: v.tensor_copy(dst_ap, src_ap), [("T", t_d)])
    else:
        e_d = k.S(lambda s: s.copy(dst_ap, src_ap), [("T", t_d)])
    b.psTb.done(bid, eng, e_d)
    return (eng, e_d)



def emit_ln_tile(k, b, tt, x_guards):
    st = b.stats
    SUM, MEAN, M2, E2, VAR, RSTD, NEGB = 0, 1, 2, 3, 4, 5, 6
    xs = b.x_sb[:, 2048 * tt: 2048 * tt + 2048]

    v_sum = k.V(lambda v: v.reduce_sum(st[:, SUM:SUM+1], xs, axis=AX.X), x_guards)
    s_sq = k.S(lambda s: s.activation(b.sq_t[:, :], xs, AF.Square,
                                      accum_out=st[:, M2:M2+1]),
               list(x_guards) + [b.sq_war])
    b.sq_war = [("S", s_sq)]
    v1 = k.V(lambda v: v.tensor_scalar(st[:, MEAN:MEAN+1], st[:, SUM:SUM+1],
                                       1.0 / 2048.0, None, ALU.mult), [("V", v_sum)])
    v2 = k.V(lambda v: v.tensor_scalar(st[:, E2:E2+1], st[:, M2:M2+1],
                                       1.0 / 2048.0, None, ALU.mult),
             [("S", s_sq), ("V", v1)])
    v3 = k.V(lambda v: v.tensor_tensor(st[:, VAR:VAR+1], st[:, MEAN:MEAN+1],
                                       st[:, MEAN:MEAN+1], ALU.mult), [("V", v2)])
    v4 = k.V(lambda v: v.tensor_tensor(st[:, VAR:VAR+1], st[:, E2:E2+1],
                                       st[:, VAR:VAR+1], ALU.subtract), [("V", v3)])
    v4b = k.V(lambda v: v.tensor_scalar(st[:, VAR:VAR+1], st[:, VAR:VAR+1],
                                        float(EPS), None, ALU.add), [("V", v4)])
    s_sd = k.S(lambda s: s.activation(st[:, RSTD:RSTD+1], st[:, VAR:VAR+1],
                                      AF.Sqrt), [("V", v4b)])
    v5 = k.V(lambda v: v.reciprocal(st[:, RSTD:RSTD+1], st[:, RSTD:RSTD+1]),
             [("S", s_sd)])
    v6 = k.V(lambda v: v.tensor_tensor(st[:, NEGB:NEGB+1], st[:, MEAN:MEAN+1],
                                       st[:, RSTD:RSTD+1], ALU.mult), [("V", v5)])
    v7 = k.V(lambda v: v.tensor_scalar(st[:, NEGB:NEGB+1], st[:, NEGB:NEGB+1],
                                       -1.0, None, ALU.mult), [("V", v6)])
    s_xn = k.S(lambda s: s.activation(b.xn_t[:, :], xs, AF.Identity,
                                      bias=st[:, NEGB:NEGB+1],
                                      scale=st[:, RSTD:RSTD+1]),
               [("V", v7), b.xn_war])
    last = ("V", 0)
    for half in range(2):
        c0 = 8 * half
        base = A_XNT + 1024 * c0 + 128 * tt
        dst = b.arena[:, base:base + 8192].rearrange(
            "p (a w) -> p a w", w=1024)[:, :, 0:128]
        srcs = [(lambda cc=cc: b.xn_t[:, 128 * cc:128 * cc + 128])
                for cc in range(c0, c0 + 8)]
        last = temove_batch(k, b, srcs, dst, [("S", s_xn)], "V" if half == 0 else "S")
    b.xn_war = [("T", k.cnt("T"))]
    return last


def emit_ln_all(k, b, x_guards_per_tt):
    evs = []
    for tt in range(NTT):
        evs.append(emit_ln_tile(k, b, tt, x_guards_per_tt[tt]))
    b.xnT_done = [("V", k.cnt("V")), ("S", k.cnt("S"))]
    return b.xnT_done



def _wprep(k, b, src, nat_slot, w_base, w_slot, fj2, rot_w, dq, store_to=None):
    nat_slot = nat_slot % 2
    nat = AP_(b, A_WN0, 2048) if nat_slot == 0 else b.xn_t[:, 0:2048]
    ld = k.dma("wn%d" % nat_slot, lambda g: g.dma_start(out=nat, in_=src),
               [b.wnat_rot.free[nat_slot]])
    nat_ready = ("wn%d" % nat_slot, ld)
    if dq is not None:
        scale_ap, bias_ap, ready = dq
        dqd = k.S(lambda s: s.activation(nat, nat, AF.Identity, bias=bias_ap,
                                         scale=scale_ap), [nat_ready, ready])
        nat_ready = ("S", dqd)
    if store_to is not None:
        k.DL(lambda g: g.dma_start(out=store_to, in_=nat), [nat_ready])
    for half in range(2):
        c0 = 8 * half
        base = w_base + 4096 * w_slot + 256 * c0 + 128 * fj2
        dst = b.arena[:, base:base + 8 * 256].rearrange(
            "p (a w) -> p a w", w=256)[:, :, 0:128]
        srcs = [(lambda cc=cc: nat[:, 128 * cc:128 * cc + 128])
                for cc in range(c0, c0 + 8)]
        temove_batch(k, b, srcs, dst, [nat_ready, rot_w.war(w_slot)],
                     "V" if half == 0 else "S")
    b.wnat_rot.free[nat_slot] = [("T", k.cnt("T"))]
    return [("V", k.cnt("V")), ("S", k.cnt("S"))]


def _wload_slab(k, b, slab_ap, w_base, slot, ch, cast_w):
    war = [cast_w, (b.wg_rot if w_base == A_WG else b.wu_rot).war(slot)]
    outs = []
    for hh, sfx in ((0, "a"), (1, "b")):
        cha = ch + sfx
        last = k.dma(cha, lambda g, hh=hh: g.dma_start(
            out=b.arena[:, w_base + 4096*slot + 2048*hh:
                        w_base + 4096*slot + 2048*hh + 2048],
            in_=slab_ap[:, 2048*hh:2048*hh + 2048]), war)
        outs.append((cha, last))
    return outs


def _emit_ffn_group(k, b, tt_done, nfj, scale_mode, gid,
                    gate_src=None, up_src=None, down_src=None,
                    dq_gate=None, dq_up=None, dq_down=None,
                    slab=None, cast_w=None, store=None):
    npair = (nfj + 1) // 2
    for fp in range(npair):
        slot = fp % 2
        if slab is not None:
            ev_g = _wload_slab(k, b, slab.gs[fp], A_WG, slot, "tg%d" % slot, cast_w)
            ev_u = _wload_slab(k, b, slab.us[fp], A_WU, slot, "tu%d" % slot, cast_w)
        else:
            ev_g = ev_u = None
        for fj2 in range(min(2, nfj - 2 * fp)):
            fj = 2 * fp + fj2
            if slab is None:
                ev_g = _wprep(k, b, gate_src(fj), 0, A_WG, slot, fj2,
                              b.wg_rot, dq_gate(fj) if dq_gate else None)
                ev_u = _wprep(k, b, up_src(fj), 1, A_WU, slot, fj2,
                              b.wu_rot, dq_up(fj) if dq_up else None)
            for th in range(2):
                def mmg(t, slot=slot, fj2=fj2, th=th):
                    r = None
                    for cc in range(16):
                        r = t.matmul(b.psE[:, 0:512],
                                     AP_(b, A_WG + 4096*slot + 256*cc + 128*fj2, 128),
                                     AP_(b, A_XNT + 1024*cc + 512*th, 512),
                                     start=(cc == 0), stop=(cc == 15))
                    return r
                g_d = k.T(mmg, ev_g + [b.xnT_done, b.psE_rot.war(0)])
                def mmu(t, slot=slot, fj2=fj2, th=th):
                    r = None
                    for cc in range(16):
                        r = t.matmul(b.psF[:, 0:512],
                                     AP_(b, A_WU + 4096*slot + 256*cc + 128*fj2, 128),
                                     AP_(b, A_XNT + 1024*cc + 512*th, 512),
                                     start=(cc == 0), stop=(cc == 15))
                    return r
                u_d = k.T(mmu, ev_u + [b.xnT_done, b.psF_rot.war(0)])
                hc = A_H + 1024 * fj + 512 * th
                sg = k.S(lambda s, hc=hc: s.activation(AP_(b, hc, 512), b.psE[:, 0:512],
                                                       AF.Silu), [("T", g_d), b.h_war])
                b.psE_rot.done(0, "S", sg)
                ue = k.V(lambda v: v.tensor_copy(AP_(b, A_SU, 512), b.psF[:, 0:512]),
                         [("T", u_d), b.su_war])
                b.psF_rot.done(0, "V", ue)
                hm = k.V(lambda v, hc=hc: v.tensor_tensor(
                    AP_(b, hc, 512), AP_(b, hc, 512), AP_(b, A_SU, 512), ALU.mult),
                    [("V", ue), ("S", sg)])
                b.su_war = [("V", hm)]
        if store is not None:
            sb_ch = "sbL0" if gid[0] in ("e", "s") else "sbL1"
            wev = [("V", k.cnt("V")), ("S", k.cnt("S"))]
            k.dma(sb_ch, lambda g, fp=fp, slot=slot: g.dma_start(
                out=store.gs[fp],
                in_=b.arena[:, A_WG + 4096*slot: A_WG + 4096*slot + 4096]), wev)
            g2 = k.dma(sb_ch, lambda g, fp=fp, slot=slot: g.dma_start(
                out=store.us[fp],
                in_=b.arena[:, A_WU + 4096*slot: A_WU + 4096*slot + 4096]), wev)
            b.wg_rot.done(slot, sb_ch, g2)
            b.wu_rot.done(slot, sb_ch, g2)
        else:
            b.wg_rot.done(slot, "T", k.cnt("T"))
            b.wu_rot.done(slot, "T", k.cnt("T"))
    h_all = ("V", k.cnt("V"))

    fw = 128 * nfj
    for cb in range(4):
        pj_waits = None
        if slab is not None:
            pj_waits = {}
            f0r = 0
            for r in range(4):
                sz = (nfj + 3 - r) // 4
                if sz == 0:
                    continue
                last = k.dma("pj%d" % r, lambda g, cb=cb, f0r=f0r, sz=sz:
                             g.dma_start(
                                 out=b.arena[:, A_PJ + 512*f0r:
                                             A_PJ + 512*(f0r + sz)],
                                 in_=slab.ds[cb][:, 512*f0r:512*(f0r + sz)]),
                             [cast_w, b.pj_war])
                pj_waits[f0r] = ("pj%d" % r, last)
                f0r += sz
            pj_ready = list(pj_waits.values())
        else:
            for cj in range(4):
                cq = 4 * cb + cj
                nslot = cq % 2
                war = b.wnat_rot.war(cq)
                nat = (AP_(b, A_WN0, fw) if nslot == 0
                       else b.xn_t[:, 0:fw])
                ldd = k.dma("wn%d" % nslot, lambda g, nat=nat, cq=cq: g.dma_start(
                    out=nat, in_=down_src(cq)), [war])
                nat_ready = ("wn%d" % nslot, ldd)
                if dq_down is not None:
                    scale_ap, bias_ap, ready = dq_down(cq)
                    dqd = k.S(lambda s, nat=nat, scale_ap=scale_ap, bias_ap=bias_ap:
                              s.activation(nat, nat, AF.Identity, bias=bias_ap,
                                           scale=scale_ap), [nat_ready, ready])
                    nat_ready = ("S", dqd)
                for f0 in range(0, nfj, 8):
                    nf = min(8, nfj - f0)
                    base = A_PJ + 512 * f0 + 128 * cj
                    dst = b.arena[:, base:base + 512 * nf].rearrange(
                        "p (a w) -> p a w", w=512)[:, :, 0:128] if nf > 1 else AP_(b, base, 128)
                    srcs = [(lambda nat=nat, fj=fj: nat[:, 128 * fj:128 * fj + 128])
                            for fj in range(f0, f0 + nf)]
                    temove_batch(k, b, srcs, dst, [nat_ready, b.pj_war],
                                 "V" if (cj + f0) % 2 == 0 else "S")
                b.wnat_rot.done(cq, "T", k.cnt("T"))
            pj_ready = [("V", k.cnt("V")), ("S", k.cnt("S"))]
            if store is not None:
                sb_ch = "sbL0" if gid[0] in ("e", "s") else "sbL1"
                k.dma(sb_ch, lambda g, cb=cb: g.dma_start(
                    out=store.ds[cb], in_=b.arena[:, A_PJ:A_PJ + 512*nfj]), pj_ready)
        for quad in range(2):
            for fj in range(nfj):
                w = []
                if fj == 0:
                    w = [h_all] + [b.ps_rot.war(i) for i in range(4)]
                    if pj_waits is None:
                        w = pj_ready + w
                if pj_waits is not None and quad == 0 and fj in pj_waits:
                    w = w + [pj_waits[fj]]
                def mmd(t, fj=fj, quad=quad, nfj=nfj):
                    r = None
                    for ti in range(4):
                        tt = 4 * quad + ti
                        r = t.matmul(b.ps[ti][:, 0:512],
                                     AP_(b, A_H + 1024 * fj + 128 * tt, 128),
                                     AP_(b, A_PJ + 512 * fj, 512),
                                     start=(fj == 0), stop=(fj == nfj - 1))
                    return r
                k.T(mmd, w)
            mm_fin = ("T", k.cnt("T"))
            for ti in range(4):
                tt = 4 * quad + ti
                xs = b.x_sb[:, 2048 * tt + 512 * cb: 2048 * tt + 512 * cb + 512]
                if scale_mode[0] == "comb":
                    sc_ap = b.comb[:, 8 * tt + scale_mode[1]: 8 * tt + scale_mode[1] + 1]
                    w_extra = b.comb_ready
                elif scale_mode[0] == "sgate":
                    sc_ap = b.comb[:, 64 + tt: 64 + tt + 1]
                    w_extra = b.comb_ready
                else:
                    sc_ap = scale_mode[1]
                    w_extra = []
                u_d = k.V(lambda v, ti=ti, xs=xs, sc_ap=sc_ap: v.scalar_tensor_tensor(
                    xs, b.ps[ti][:, 0:512], sc_ap, xs, op0=ALU.mult, op1=ALU.add),
                    [mm_fin, tt_done[tt], w_extra])
                tt_done[tt] = ("V", u_d)
                b.ps_rot.done(ti, "V", u_d)
        if store is not None and slab is None:
            sb_ch = "sbL0" if gid[0] in ("e", "s") else "sbL1"
            b.pj_war = [("T", k.cnt("T")), (sb_ch, k.cnt(sb_ch))]
        else:
            b.pj_war = [("T", k.cnt("T"))]
    b.h_war = [("T", k.cnt("T"))]


def _emit_router(k, b, wsrc):
    st = b.stats
    ld = k.dma("cn", lambda g: g.dma_start(out=b.xn_t[0:8, 0:2048], in_=wsrc.gate_w),
               [b.xnT_done, b.xn_war])
    srcs = [(lambda cc=cc: b.xn_t[0:8, 128 * cc:128 * cc + 128]) for cc in range(16)]
    temove_batch(k, b, srcs, AP_(b, A_RW, 128), [("cn", ld)], "V", width=8, in_parts=8)
    ld2 = k.dma("cn", lambda g: g.dma_start(out=b.xn_t[0:1, 0:2048], in_=wsrc.sgw),
                [("T", k.cnt("T"))])
    srcs = [(lambda cc=cc: b.xn_t[0:1, 128 * cc:128 * cc + 128]) for cc in range(16)]
    temove_batch(k, b, srcs, AP_(b, A_SGW, 16), [("cn", ld2)], "V", width=1, in_parts=1)
    b.xn_war = [("T", k.cnt("T"))]
    b.wnat_rot.free[1] = [("T", k.cnt("T"))] + list(b.xn_war)
    rw_ready = ("V", k.cnt("V"))

    P0, PM, M1, M2_, L0, SG = 16, 24, 32, 33, 34, 35
    for tt in range(NTT):
        def mm(t, tt=tt):
            r = None
            for cc in range(16):
                t.matmul(b.psE[:, 0:8], AP_(b, A_XNT + 1024 * cc + 128 * tt, 128),
                         AP_(b, A_RW + 8 * cc, 8), start=(cc == 0), stop=(cc == 15))
            for cc in range(16):
                r = t.matmul(b.psE[:, 8:9], AP_(b, A_XNT + 1024 * cc + 128 * tt, 128),
                             AP_(b, A_SGW + cc, 1), start=(cc == 0), stop=(cc == 15))
            return r
        m_d = k.T(mm, [rw_ready, b.xnT_done, b.psE_rot.war(0)])
        v1a = k.V(lambda v: v.reduce_max(st[:, M1:M1+1], b.psE[:, 0:8], axis=AX.X),
                  [("T", m_d)])
        v1 = k.V(lambda v: v.tensor_scalar(st[:, M1:M1+1], st[:, M1:M1+1], -1.0,
                                           None, ALU.mult), [("V", v1a)])
        s1 = k.S(lambda s: s.activation(st[:, P0:P0+8], b.psE[:, 0:8], AF.Exp,
                                        bias=st[:, M1:M1+1],
                                        accum_out=st[:, L0:L0+1]), [("V", v1)])
        v2 = k.V(lambda v: v.reciprocal(st[:, L0:L0+1], st[:, L0:L0+1]), [("S", s1)])
        v3 = k.V(lambda v: v.tensor_scalar(st[:, P0:P0+8], st[:, P0:P0+8],
                                           st[:, L0:L0+1], None, ALU.mult), [("V", v2)])
        v4 = k.V(lambda v: v.reduce_max(st[:, M1:M1+1], st[:, P0:P0+8], axis=AX.X),
                 [("V", v3)])
        v5 = k.V(lambda v: v.scalar_tensor_tensor(
            st[:, PM:PM+8], st[:, P0:P0+8], st[:, M1:M1+1], st[:, P0:P0+8],
            op0=ALU.is_lt, op1=ALU.mult), [("V", v4)])
        v6 = k.V(lambda v: v.reduce_max(st[:, M2_:M2_+1], st[:, PM:PM+8], axis=AX.X),
                 [("V", v5)])
        v7 = k.V(lambda v, tt=tt: v.scalar_tensor_tensor(
            b.comb[:, 8*tt:8*tt+8], st[:, P0:P0+8], st[:, M2_:M2_+1], st[:, P0:P0+8],
            op0=ALU.is_ge, op1=ALU.mult), [("V", v6)])
        v8 = k.V(lambda v, tt=tt: v.tensor_scalar(b.comb[:, 8*tt:8*tt+8],
                                           b.comb[:, 8*tt:8*tt+8], 0.5, None,
                                           ALU.mult), [("V", v7)])
        s2 = k.S(lambda s: s.activation(st[:, SG:SG+1], b.psE[:, 8:9], AF.Sigmoid),
                 [("T", m_d), ("S", s1)])
        v9 = k.V(lambda v, tt=tt: v.tensor_scalar(b.comb[:, 64+tt:64+tt+1], st[:, SG:SG+1],
                                           0.5, None, ALU.mult), [("S", s2), ("V", v8)])
        b.psE_rot.done(0, "V", v9)
    b.comb_ready = [("V", k.cnt("V"))]


def _emit_l1_scales(k, b, wsrc):
    for idx, (src_, col, n) in enumerate([(wsrc.gate_scale, 0, 64),
                                          (wsrc.up_scale, 64, 64),
                                          (wsrc.down_scale, 128, 16)]):
        ld = k.dma("cn", lambda g, src_=src_, n=n: g.dma_start(
            out=b.sq_t[0:n, 0:128], in_=src_.rearrange("(a p) -> a p", p=128)),
            [b.sq_war])
        war = b.psE_rot.war(0)
        t_d = k.T(lambda t, n=n: t.transpose(b.psE[:, 0:n], b.sq_t[0:n, 0:128],
                                             b.identF[0:n, 0:n]),
                  [("cn", ld), war])
        b.sq_war = [("T", t_d)]
        e_d = k.V(lambda v, col=col, n=n: v.tensor_copy(
            b.scl[:, col:col + n], b.psE[:, 0:n]), [("T", t_d)])
        b.psE_rot.done(0, "V", e_d)
    vb = k.V(lambda v: v.tensor_scalar(b.scl[:, 144:288], b.scl[:, 0:144],
                                       -128.0, None, ALU.mult),
             [("V", k.cnt("V"))])
    b.l1s_ready = [("V", vb)]


def _slab_ns(scr, kind, idx):
    if kind == "e":
        return NS(gs=[scr.fc1s[idx, fp] for fp in range(6)],
                  us=[scr.fc2s[idx, fp] for fp in range(6)],
                  ds=[scr.projs[idx, cb] for cb in range(4)])
    if kind == "s":
        return NS(gs=[scr.sfc1s[idx, fp] for fp in range(6)],
                  us=[scr.sfc2s[idx, fp] for fp in range(6)],
                  ds=[scr.sprojs[idx, cb] for cb in range(4)])
    return NS(gs=[scr.l1gs[idx, fp] for fp in range(4)],
              us=[scr.l1us[idx, fp] for fp in range(4)],
              ds=[scr.l1ds[idx, cb] for cb in range(4)])


def emit_ffn(k, b, kind, wsrc, tt_done, scr=None, second=False):
    if kind == "l0":
        _emit_router(k, b, wsrc)
        for e in range(8):
            sl = _slab_ns(scr, "e", e) if scr else None
            _emit_ffn_group(
                k, b, tt_done, nfj=11, scale_mode=("comb", e), gid=("e", e),
                gate_src=lambda fj, e=e: wsrc.fc1[e, 128*fj:128*fj+128, :],
                up_src=lambda fj, e=e: wsrc.fc2[e, 128*fj:128*fj+128, :],
                down_src=lambda cq, e=e: wsrc.proj[e, 128*cq:128*cq+128, :],
                slab=sl if second else None,
                cast_w=b.castdone.get("l0") if second else None,
                store=sl if (scr and not second) else None)
        for gi in range(4):
            sl = _slab_ns(scr, "s", gi) if scr else None
            fj0 = 11 * gi
            _emit_ffn_group(
                k, b, tt_done, nfj=11, scale_mode=("sgate", None), gid=("s", gi),
                gate_src=lambda fj, fj0=fj0: wsrc.sfc1[128*(fj0+fj):128*(fj0+fj)+128, :],
                up_src=lambda fj, fj0=fj0: wsrc.sfc2[128*(fj0+fj):128*(fj0+fj)+128, :],
                down_src=lambda cq, fj0=fj0: wsrc.sproj[128*cq:128*cq+128,
                                                        128*fj0:128*fj0+1408],
                slab=sl if second else None,
                cast_w=b.castdone.get("l0") if second else None,
                store=sl if (scr and not second) else None)
        if scr and not second:
            b.castdone["l0"] = ("sbL0", k.cnt("sbL0"))
    else:
        b.wnat_rot.free[1] = list(b.wnat_rot.free[1]) + list(b.xn_war)
        if not second:
            _emit_l1_scales(k, b, wsrc)
        for gi in range(8):
            sl = _slab_ns(scr, "l1", gi) if scr else None
            fj0 = 8 * gi
            def dqg(fj, fj0=fj0):
                i2 = fj0 + fj
                return (b.scl[:, i2:i2+1], b.scl[:, 144+i2:144+i2+1], b.l1s_ready)
            def dqu(fj, fj0=fj0):
                i2 = fj0 + fj
                return (b.scl[:, 64+i2:64+i2+1], b.scl[:, 208+i2:208+i2+1], b.l1s_ready)
            def dqd(cq):
                return (b.scl[:, 128+cq:128+cq+1], b.scl[:, 272+cq:272+cq+1], b.l1s_ready)
            _emit_ffn_group(
                k, b, tt_done, nfj=8, scale_mode=("const", 0.5), gid=("l1", gi),
                gate_src=lambda fj, fj0=fj0: wsrc.gate_w[128*(fj0+fj):128*(fj0+fj)+128, :],
                up_src=lambda fj, fj0=fj0: wsrc.up_w[128*(fj0+fj):128*(fj0+fj)+128, :],
                down_src=lambda cq, fj0=fj0: wsrc.down_w[128*cq:128*cq+128,
                                                         128*fj0:128*fj0+1024],
                dq_gate=None if second else dqg, dq_up=None if second else dqu,
                dq_down=None if second else dqd,
                slab=sl if second else None,
                cast_w=b.castdone.get("l1") if second else None,
                store=sl if (scr and not second) else None)
        if scr and not second:
            b.castdone["l1"] = ("sbL1", k.cnt("sbL1"))
    b.xn_war = [("T", k.cnt("T"))] + list(b.xn_war)
    return tt_done



def emit_att2_build(k, b, bias_dram, pbase):
    l1 = k.dma("cn", lambda g: g.dma_start(out=b.x_sb[0:1, 0:2048], in_=bias_dram),
               [b.att2_guard])
    s1 = k.S(lambda s: s.activation(b.xn_t[0:1, 0:2048], b.x_sb[0:1, 0:2048],
                                    AF.Identity), [("cn", l1), b.xn_war])
    l2 = k.dma("cn", lambda g: g.dma_start(out=b.att2[pbase:pbase+1, :],
                                    in_=b.xn_t[0:1, 0:2048]), [("S", s1)])
    v1 = k.V(lambda v: v.tensor_copy(b.x_sb[0:1, 2048:4096], b.xn_t[0:1, 0:2048]),
             [("S", s1)])
    v2 = k.V(lambda v: v.tensor_tensor(b.x_sb[0:1, 0:2048], b.x_sb[0:1, 0:2048],
                                       b.x_sb[0:1, 2048:4096], ALU.subtract),
             [("V", v1)])
    s2 = k.S(lambda s: s.activation(b.xn_t[0:1, 0:2048], b.x_sb[0:1, 0:2048],
                                    AF.Identity), [("V", v2), ("cn", l2)])
    l3 = k.dma("cn", lambda g: g.dma_start(out=b.att2[pbase+1:pbase+2, :],
                                    in_=b.xn_t[0:1, 0:2048]), [("S", s2)])
    b.att2_guard = [("cn", l3)]
    b.xn_war = [("S", s2)]


def emit_attn_wprep(k, b, layer, wsrc, scr=None):
    if layer == 0:
        last = 0
        for j2, wv in enumerate([wsrc.wq_s, wsrc.wk_s, wsrc.wv_s]):
            for cc in range(16):
                dst = b.arena[:, A_WQKV + 768*cc + 256*j2:
                              A_WQKV + 768*cc + 256*j2 + 256]
                last = k.dma("wq", lambda g, wv=wv, cc=cc, dst=dst: g.dma_start(
                    out=dst, in_=wv[128*cc:128*cc+128, :]), [b.wst_war])
        b.wqkv_ready = [("wq", last)]
    else:
        cw = k.DL(lambda g: g.dma_start(out=scr.wqkvb, in_=wsrc.wqkv_s), [])
        last = 0
        for cc in range(16):
            last = k.dma("wq", lambda q, cc=cc: q.dma_start_transpose(
                b.arena[:, A_WQKV + 768*cc: A_WQKV + 768*cc + 768],
                scr.wqkvb[0:768, 128*cc:128*cc+128]),
                ([("DL", cw), b.sbsb_guard, b.wst_war] if cc == 0 else []), q="sync")
        b.wqkv_ready = [("wq", last)]
        bl = 0
        for m in range(6):
            bl = k.dma("cn", lambda g, m=m: g.dma_start(
                out=b.comb[:, 72+m:72+m+1],
                in_=wsrc.bqkv_s[128*m:128*m+128].rearrange("(p a) -> p a", a=1)),
                [b.wst_war])
        vb = k.V(lambda v: v.tensor_scalar(b.comb[:, 78:80], b.comb[:, 72:74],
                                           RSQD, None, ALU.mult), [("cn", bl)])
        b.qkvbias_ready = [("V", vb)]


def emit_attn_qkv(k, b, layer, slab_src, q_dram):
    vo = []
    for h2 in range(2):
        ap = b.arena[:, A_V + VH*h2: A_V + VH*h2 + VH].rearrange(
            "p (a w) -> p a w", w=VW)[:, :, 128:129]
        vo.append(k.V(lambda v, ap=ap: v.memset(ap, 1.0), [b.vones_guard]))
    v_ones = ("V", vo[-1])
    qkv_bias = [b.qkvbias_ready] if layer == 1 else []

    for tb in range(8):
        for th in range(2):
            mm_last = 0
            for cc in range(16):
                slot = cc % 2
                ldw = k.dma("sl%d" % slot, lambda q, cc=cc, tb=tb, th=th, slot=slot:
                            q.dma_start(out=AP_(b, A_SLAB + 512*slot, 512),
                                        in_=slab_src(cc, tb, th)),
                            [b.slab_rot.war(slot), b.ag_guard], q="sync")
                wts = b.wqkv_ready + [("sl%d" % slot, ldw)]
                if cc == 0:
                    wts = wts + [b.ps_rot.war(i) for i in range(4)] +                           [b.psE_rot.war(0), b.psF_rot.war(0)]
                def mm(t, cc=cc, slot=slot):
                    r = None
                    for h2 in range(2):
                        t.matmul(b.ps[h2][:, 0:512],
                                 AP_(b, A_WQKV + 768*cc + 128*(2+h2), 128),
                                 AP_(b, A_SLAB + 512*slot, 512),
                                 start=(cc == 0), stop=(cc == 15))
                        t.matmul(b.ps[2+h2][:, 0:512],
                                 AP_(b, A_WQKV + 768*cc + 128*(4+h2), 128),
                                 AP_(b, A_SLAB + 512*slot, 512),
                                 start=(cc == 0), stop=(cc == 15))
                        r = t.matmul((b.psE if h2 == 0 else b.psF)[:, 0:512],
                                     AP_(b, A_WQKV + 768*cc + 128*h2, 128),
                                     AP_(b, A_SLAB + 512*slot, 512),
                                     start=(cc == 0), stop=(cc == 15))
                    return r
                mm_last = k.T(mm, wts + qkv_bias)
                b.slab_rot.done(slot, "T", mm_last)
            for h2 in range(2):
                kdst = AP_(b, A_K + 8192*h2 + 1024*tb + 512*th, 512)
                if layer == 1:
                    ek = k.S(lambda s, h2=h2, kdst=kdst: s.activation(
                        kdst, b.ps[h2][:, 0:512], AF.Identity,
                        bias=b.comb[:, 74+h2:74+h2+1]), [("T", mm_last)])
                else:
                    ek = k.S(lambda s, h2=h2, kdst=kdst: s.copy(
                        kdst, b.ps[h2][:, 0:512]), [("T", mm_last)])
                b.ps_rot.done(h2, "S", ek)
                ps_ = b.p_rot.war(0)
                stg = AP_(b, A_P, 512)
                if layer == 1:
                    ev = k.S(lambda s, h2=h2, stg=stg: s.activation(
                        stg, b.ps[2+h2][:, 0:512], AF.Identity,
                        bias=b.comb[:, 76+h2:76+h2+1]), [("T", mm_last), ps_])
                else:
                    ev = k.S(lambda s, h2=h2, stg=stg: s.copy(
                        stg, b.ps[2+h2][:, 0:512]), [("T", mm_last), ps_])
                b.ps_rot.done(2+h2, "S", ev)
                tile0 = 8*tb + 4*th
                base = A_V + VH*h2 + VW*tile0
                dst = b.arena[:, base:base + 4*VW].rearrange(
                    "p (a w) -> p a w", w=VW)[:, :, 0:128]
                srcs = [(lambda i=i, stg=stg: stg[:, 128*i:128*i+128])
                        for i in range(4)]
                temove_batch(k, b, srcs, dst, [("S", ev), v_ones], "V")
                b.p_rot.done(0, "T", k.cnt("T"))
                pq = b.pt_rot.war(0)
                qstg = AP_(b, A_PT, 512)
                qps = b.psE if h2 == 0 else b.psF
                if layer == 1:
                    eq = k.S(lambda s, qstg=qstg, qps=qps, h2=h2: s.activation(
                        qstg, qps[:, 0:512], AF.Identity, scale=RSQD,
                        bias=b.comb[:, 78+h2:78+h2+1]), [("T", mm_last), pq])
                else:
                    eq = k.S(lambda s, qstg=qstg, qps=qps: s.activation(
                        qstg, qps[:, 0:512], AF.Identity, scale=RSQD),
                        [("T", mm_last), pq])
                (b.psE_rot if h2 == 0 else b.psF_rot).done(0, "S", eq)
                dq = k.dma("qst", lambda g, h2=h2, tb=tb, th=th, qstg=qstg: g.dma_start(
                    out=q_dram[128*h2:128*h2+128, 1024*tb+512*th:1024*tb+512*th+512],
                    in_=qstg), [("S", eq)])
                b.pt_rot.done(0, "qst", dq)
    b.kv_ready = [("V", k.cnt("V")), ("S", k.cnt("S")), ("qst", k.cnt("qst"))]


def emit_attn_core(k, b, layer, bias_rows, q_dram, ctx_dram, bmax0=None):
    k.dma("cn", lambda g: g.dma_start(out=b.bias_sb[0:2, :], in_=bias_rows[0]),
          [b.bias_guard])
    k.dma("cn", lambda g: g.dma_start(out=b.bias_sb[32:34, :], in_=bias_rows[1]),
          [b.bias_guard])
    if layer == 0:
        k.dma("cn", lambda g: g.dma_start(out=b.bmax_sb[:, :], in_=bmax0),
              [b.bias_guard])
    bias_ready = bmax_ready = ("cn", k.cnt("cn"))
    ctxi = 0
    for quarter in range(2):
        for h2 in range(2):
            lq = k.dma("qld", lambda q, h2=h2, quarter=quarter: q.dma_start(
                out=AP_(b, A_Q, 4096),
                in_=q_dram[128*h2:128*h2+128, 4096*quarter:4096*quarter+4096]),
                [b.q_war] + b.kv_ready, q="sync")
            q_ready = ("qld", lq)
            for ql in range(32):
                qt = 32 * quarter + ql
                nkc = qt // 4 + 1
                rb = 32 * h2
                red = 0
                for i in range(nkc):
                    bank = b.psE if i % 2 == 0 else b.psF
                    rot = b.psE_rot if i % 2 == 0 else b.psF_rot
                    dvar = qt - 4 * i if i == nkc - 1 else None
                    def mma(t, i=i, bank=bank, ql=ql, rb=rb, dvar=dvar, h2=h2):
                        r = None
                        if layer == 1:
                            t.matmul(bank[:, 0:512], b.ones2[rb:rb+2, 0:128],
                                     b.bias_sb[rb:rb+2, 512*i:512*i+512],
                                     start=True, stop=False)
                        if dvar is not None:
                            t.matmul(bank[:, 0:512], b.ident[:, :],
                                     b.tri[:, 512*dvar:512*dvar+512],
                                     start=(layer == 0), stop=False)
                        return t.matmul(bank[:, 0:512],
                                        AP_(b, A_Q + 128*ql, 128),
                                        AP_(b, A_K + 8192*h2 + 512*i, 512),
                                        start=(layer == 0 and dvar is None),
                                        stop=True)
                    m_d = k.T(mma, [q_ready, bias_ready, rot.war(0)])
                    red = k.V(lambda v, i=i, bank=bank: v.reduce_max(
                        b.stats[:, 40+i:40+i+1], bank[:, 0:512], axis=AX.X),
                        [("T", m_d)])
                    rot.done(0, "V", red)
                if layer == 0:
                    for i in range(nkc):
                        red = k.V(lambda v, i=i, h2=h2: v.tensor_scalar(
                            b.stats[:, 40+i:40+i+1], b.stats[:, 40+i:40+i+1],
                            b.bmax_sb[:, 16*h2+i:16*h2+i+1], None, ALU.add),
                            [("V", red), bmax_ready])
                r1 = k.V(lambda v, nkc=nkc: v.reduce_max(
                    b.stats[:, 56:57], b.stats[:, 40:40+nkc], axis=AX.X),
                    [("V", red)])
                r2 = k.V(lambda v: v.tensor_scalar(
                    b.stats[:, 57:58], b.stats[:, 56:57], -1.0, None, ALU.mult),
                    [("V", r1)])
                negm = ("V", r2)
                pend = None
                pv_last = 0
                for i in range(nkc):
                    bank = b.psE if i % 2 == 0 else b.psF
                    rot = b.psE_rot if i % 2 == 0 else b.psF_rot
                    dvar = qt - 4 * i if i == nkc - 1 else None
                    def mmb(t, i=i, bank=bank, ql=ql, rb=rb, dvar=dvar, h2=h2):
                        t.matmul(bank[:, 0:512], b.ones2[rb:rb+2, 0:128],
                                 b.bias_sb[rb:rb+2, 512*i:512*i+512],
                                 start=True, stop=False)
                        if dvar is not None:
                            t.matmul(bank[:, 0:512], b.ident[:, :],
                                     b.tri[:, 512*dvar:512*dvar+512],
                                     start=False, stop=False)
                        return t.matmul(bank[:, 0:512],
                                        AP_(b, A_Q + 128*ql, 128),
                                        AP_(b, A_K + 8192*h2 + 512*i, 512),
                                        start=False, stop=True)
                    m_d = k.T(mmb, [rot.war(0)])
                    if pend is not None:
                        pv_last = _attn_pb_tail(k, b, pend, ctxi, h2, nkc)
                    pslot = i % 2
                    pw = b.p_rot.war(pslot)
                    ex = k.S(lambda s, bank=bank, pslot=pslot: s.activation(
                        AP_(b, A_P + 512*pslot, 512), bank[:, 0:512], AF.Exp,
                        bias=b.stats[:, 57:58]), [("T", m_d), negm, pw])
                    rot.done(0, "S", ex)
                    pend = (i, pslot, ex)
                pv_last = _attn_pb_tail(k, b, pend, ctxi, h2, nkc)
                rl = k.V(lambda v, ci=ctxi: v.reciprocal(
                    b.stats[:, 58:59], b.ps[ci][:, 128:129]), [("T", pv_last)])
                pw = b.p_rot.war(0)
                fs = k.S(lambda s, ci=ctxi: s.activation(
                    AP_(b, A_P, 128), b.ps[ci][:, 0:128], AF.Identity,
                    scale=b.stats[:, 58:59]), [("V", rl), pw])
                b.ps_rot.done(ctxi, "S", fs)
                ptw = b.pt_rot.war(0)
                ev = temove_batch(k, b, [lambda: AP_(b, A_P, 128)],
                                  AP_(b, A_PT, 128), [("S", fs), ptw], "V")
                b.p_rot.done(0, "T", k.cnt("T"))
                dc = k.dma("cst", lambda g, h2=h2, qt=qt: g.dma_start(
                    out=ctx_dram[128*h2:128*h2+128, 128*qt:128*qt+128],
                    in_=AP_(b, A_PT, 128)), [ev])
                b.pt_rot.done(0, "cst", dc)
                ctxi = (ctxi + 1) % 4
            b.q_war = [("T", k.cnt("T"))]
    b.ctx_done = [("cst", k.cnt("cst"))]


def _attn_pb_tail(k, b, pend, ctxi, h2, nkc):
    i, pslot, ex = pend
    p_ready = ("S", ex)
    ptw = b.pt_rot.war(pslot)
    srcs = [(lambda j=j, pslot=pslot: AP_(b, A_P + 512*pslot + 128*j, 128))
            for j in range(4)]
    ev = temove_batch(k, b, srcs, AP_(b, A_PT + 512*pslot, 512),
                      [p_ready, ptw], "V")
    b.p_rot.done(pslot, "T", k.cnt("T"))
    def mpv(t, i=i, pslot=pslot, h2=h2, ci=ctxi, nkc=nkc):
        r = None
        for j in range(4):
            tkb = 4*i + j
            r = t.matmul(b.ps[ci][:, 0:129],
                         AP_(b, A_PT + 512*pslot + 128*j, 128),
                         b.arena[:, A_V + VH*h2 + VW*tkb:
                                 A_V + VH*h2 + VW*tkb + 129],
                         start=(i == 0 and j == 0),
                         stop=(i == nkc - 1 and j == 3))
        return r
    w = [ev]
    if i == 0:
        w.append(b.ps_rot.war(ctxi))
    pv_last = k.T(mpv, w)
    b.pt_rot.done(pslot, "T", pv_last)
    return pv_last


def emit_attn_outproj(k, b, layer, wo_src, ctx_loader, tt_done, wo_cast=None):
    ctx_ready = ctx_loader(k, b)
    rb2 = 32 * layer
    for ob in range(4):
        for quad in range(2):
            for dc in range(16):
                slot = dc % 2
                wdst = AP_(b, A_WO + 512*slot, 512)
                if layer == 0:
                    lw = k.dma("wo%d" % slot, lambda g, ob=ob, dc=dc, wdst=wdst:
                               g.dma_start(out=wdst,
                                           in_=wo_src[128*dc:128*dc+128, 512*ob:512*ob+512]),
                               [b.wo_rot.war(slot)])
                    w_ready = [("wo%d" % slot, lw)]
                else:
                    lw = k.dma("wo%d" % slot, lambda q, ob=ob, dc=dc, wdst=wdst:
                               q.dma_start_transpose(
                                   wdst, wo_src[512*ob:512*ob+512, 128*dc:128*dc+128]),
                               [b.wo_rot.war(slot), b.sbsb_guard, wo_cast], q="sync")
                    w_ready = [("wo%d" % slot, lw)]
                wts = w_ready + [ctx_ready]
                if dc == 0:
                    wts += [b.ps_rot.war(i) for i in range(4)]
                def mmo(t, ob=ob, quad=quad, dc=dc, slot=slot, rb2=rb2):
                    r = None
                    if dc == 0:
                        for ti in range(4):
                            t.matmul(b.ps[ti][:, 0:512], b.ones2[rb2:rb2+2, 0:128],
                                     b.att2[rb2:rb2+2, 512*ob:512*ob+512],
                                     start=True, stop=False)
                    for ti in range(4):
                        tt = 4*quad + ti
                        r = t.matmul(b.ps[ti][:, 0:512],
                                     AP_(b, A_CTX + 1024*dc + 128*tt, 128),
                                     AP_(b, A_WO + 512*slot, 512),
                                     start=False, stop=(dc == 15))
                    return r
                mo = k.T(mmo, wts + [b.att2_ready])
                b.wo_rot.done(slot, "T", mo)
            mm_fin = ("T", k.cnt("T"))
            for ti in range(4):
                tt = 4*quad + ti
                xs = b.x_sb[:, 2048*tt + 512*ob: 2048*tt + 512*ob + 512]
                u_d = k.V(lambda v, ti=ti, xs=xs: v.tensor_tensor(
                    xs, xs, b.ps[ti][:, 0:512], ALU.add),
                    [mm_fin, tt_done[tt]])
                tt_done[tt] = ("V", u_d)
                b.ps_rot.done(ti, "V", u_d)
    b.ctxbuf_war = [("T", k.cnt("T"))]
    return tt_done



def host_consts():
    ident = np.eye(128, dtype=BF)
    ones2 = np.ones((34, 128), dtype=BF)
    tri = np.zeros((128, 4 * 512), dtype=BF)
    for d in range(4):
        i = np.arange(128)[:, None]
        jj = np.arange(512)[None, :]
        tri[:, 512*d:512*d+512] = np.where((jj - i) <= 128 * d, 0.0, -30000.0).astype(BF)
    return dict(cn_ident=ident, cn_ones2=ones2, cn_tri=tri,
                cn_identF=np.eye(128, dtype=np.float32))


def emit_const_loads(k, b, cn):
    l1 = k.DL(lambda g: g.dma_start(out=b.ident[:, :], in_=cn.ident))
    l2 = k.DL(lambda g: g.dma_start(out=b.ones2[:, :], in_=cn.ones2))
    l3 = k.DL(lambda g: g.dma_start(out=b.tri[:, :], in_=cn.tri))
    l4 = k.DL(lambda g: g.dma_start(out=b.identF[:, :], in_=cn.identF))
    return ("DL", l4)



def build_ffn_launch(kind, with_xnT=False):
    nc = bass.Bass("TRN2", target_bir_lowering=False)
    x_in = nc.dram_tensor("x_in", [TLOC, C], F32, kind="ExternalInput").ap()
    x_out = nc.dram_tensor("x_out", [TLOC, C], F32, kind="ExternalOutput").ap()
    xnT_out = (nc.dram_tensor("xnT_out", [C, TLOC], BF16, kind="ExternalOutput").ap()
               if with_xnT else None)
    cn = NS(ident=nc.dram_tensor("cn_ident", [128, 128], BF16, kind="ExternalInput").ap(),
            ones2=nc.dram_tensor("cn_ones2", [34, 128], BF16, kind="ExternalInput").ap(),
            tri=nc.dram_tensor("cn_tri", [128, 2048], BF16, kind="ExternalInput").ap(),
            identF=nc.dram_tensor("cn_identF", [128, 128], F32, kind="ExternalInput").ap())
    if kind == "l0":
        wsrc = NS(
            gate_w=nc.dram_tensor("gate_w", [8, C], F32, kind="ExternalInput").ap(),
            sgw=nc.dram_tensor("sgw", [1, C], F32, kind="ExternalInput").ap(),
            fc1=nc.dram_tensor("fc1", [8, F_MOE, C], F32, kind="ExternalInput").ap(),
            fc2=nc.dram_tensor("fc2", [8, F_MOE, C], F32, kind="ExternalInput").ap(),
            proj=nc.dram_tensor("proj", [8, C, F_MOE], F32, kind="ExternalInput").ap(),
            sfc1=nc.dram_tensor("sfc1", [F_SH, C], F32, kind="ExternalInput").ap(),
            sfc2=nc.dram_tensor("sfc2", [F_SH, C], F32, kind="ExternalInput").ap(),
            sproj=nc.dram_tensor("sproj", [C, F_SH], F32, kind="ExternalInput").ap(),
        )
    else:
        wsrc = NS(
            gate_w=nc.dram_tensor("gate_w", [I_L1, C], I32, kind="ExternalInput").ap(),
            up_w=nc.dram_tensor("up_w", [I_L1, C], I32, kind="ExternalInput").ap(),
            down_w=nc.dram_tensor("down_w", [C, I_L1], I32, kind="ExternalInput").ap(),
            gate_scale=nc.dram_tensor("gate_scale", [I_L1], F32, kind="ExternalInput").ap(),
            up_scale=nc.dram_tensor("up_scale", [I_L1], F32, kind="ExternalInput").ap(),
            down_scale=nc.dram_tensor("down_scale", [C], F32, kind="ExternalInput").ap(),
        )
    es = ExitStack()
    k = K(nc, es)
    b = make_bufs(k)
    emit_const_loads(k, b, cn)
    b.sbsb_guard = ("T", 0)
    xl = k.dma("xin", lambda g: g.dma_start(
        out=b.x_sb[:, :].rearrange("p (a c) -> p a c", c=2048),
        in_=x_in.rearrange("(a p) c -> p a c", p=128)))
    tt_done = [("xin", xl)] * NTT
    for tt in range(NTT):
        emit_ln_tile(k, b, tt, [("xin", xl)])
    b.xnT_done = [("V", k.cnt("V")), ("S", k.cnt("S"))]
    tt_done = emit_ffn(k, b, kind, wsrc, tt_done)
    st_last = k.dma("xout", lambda g: g.dma_start(
        out=x_out.rearrange("(a p) c -> p a c", p=128),
        in_=b.x_sb[:, :].rearrange("p (a c) -> p a c", c=2048)), tt_done)
    if with_xnT:
        tguard = ("T", k.cnt("T"))
        for tt in range(NTT):
            emit_ln_tile(k, b, tt, [tt_done[tt], tguard])
        xnT_done = [("V", k.cnt("V")), ("S", k.cnt("S"))]
        st2 = k.dma("xno", lambda g: g.dma_start(
            out=xnT_out.rearrange("(a p) w -> p a w", p=128),
            in_=b.arena[:, A_XNT:A_XNT+16384].rearrange("p (a w) -> p a w", w=1024)),
            xnT_done)
        k.waiter("G", [("xout", st_last), ("xno", st2)])
    else:
        k.waiter("G", [("xout", st_last)])
    k.finalize()
    es.close()
    return nc


def build_attn_core_launch(layer):
    nc = bass.Bass("TRN2", target_bir_lowering=False)
    xnT_full = nc.dram_tensor("xnT_full", [C, SEQ], BF16, kind="ExternalInput").ap()
    brows = nc.dram_tensor("cn_brows", [2, 2, SEQ], BF16, kind="ExternalInput").ap()
    bmax0 = nc.dram_tensor("cn_bmax0", [128, 32], F32, kind="ExternalInput").ap() \
        if layer == 0 else None
    ctx_out = nc.dram_tensor("ctx_out", [2 * D, SEQ], BF16, kind="ExternalOutput").ap()
    cn = NS(ident=nc.dram_tensor("cn_ident", [128, 128], BF16, kind="ExternalInput").ap(),
            ones2=nc.dram_tensor("cn_ones2", [34, 128], BF16, kind="ExternalInput").ap(),
            tri=nc.dram_tensor("cn_tri", [128, 2048], BF16, kind="ExternalInput").ap(),
            identF=nc.dram_tensor("cn_identF", [128, 128], F32, kind="ExternalInput").ap())
    if layer == 0:
        wsrc = NS(wq_s=nc.dram_tensor("wq_s", [C, 256], F32, kind="ExternalInput").ap(),
                  wk_s=nc.dram_tensor("wk_s", [C, 256], F32, kind="ExternalInput").ap(),
                  wv_s=nc.dram_tensor("wv_s", [C, 256], F32, kind="ExternalInput").ap())
    else:
        wsrc = NS(wqkv_s=nc.dram_tensor("wqkv_s", [768, C], F32, kind="ExternalInput").ap(),
                  bqkv_s=nc.dram_tensor("bqkv_s", [768], F32, kind="ExternalInput").ap())
    q_dram = nc.dram_tensor("q_dram", [2 * D, SEQ], BF16, kind="Internal").ap()
    scr = NS(wqkvb=nc.dram_tensor("wqkvb", [768, C], BF16, kind="Internal").ap()) \
        if layer == 1 else None
    es = ExitStack()
    k = K(nc, es)
    b = make_bufs(k)
    emit_const_loads(k, b, cn)
    b.sbsb_guard = ("T", 0)
    emit_attn_wprep(k, b, layer, wsrc, scr)
    slab_src = lambda cc, tb, th: xnT_full[128*cc:128*cc+128,
                                           1024*tb+512*th:1024*tb+512*th+512]
    emit_attn_qkv(k, b, layer, slab_src, q_dram)
    emit_attn_core(k, b, layer, [brows[0], brows[1]], q_dram, ctx_out, bmax0=bmax0)
    k.waiter("G", [b.ctx_done])
    k.finalize()
    es.close()
    return nc


def build_outproj_launch(layer):
    nc = bass.Bass("TRN2", target_bir_lowering=False)
    x_in = nc.dram_tensor("x_in", [TLOC, C], F32, kind="ExternalInput").ap()
    x_out = nc.dram_tensor("x_out", [TLOC, C], F32, kind="ExternalOutput").ap()
    ctx_own = nc.dram_tensor("ctx_own", [C, TLOC], BF16, kind="ExternalInput").ap()
    wo = nc.dram_tensor("wo", [C, C], F32, kind="ExternalInput").ap()
    bias2 = nc.dram_tensor("bias2", [1, C], F32, kind="ExternalInput").ap()
    cn = NS(ident=nc.dram_tensor("cn_ident", [128, 128], BF16, kind="ExternalInput").ap(),
            ones2=nc.dram_tensor("cn_ones2", [34, 128], BF16, kind="ExternalInput").ap(),
            tri=nc.dram_tensor("cn_tri", [128, 2048], BF16, kind="ExternalInput").ap(),
            identF=nc.dram_tensor("cn_identF", [128, 128], F32, kind="ExternalInput").ap())
    woutb = (nc.dram_tensor("woutb", [C, C], BF16, kind="Internal").ap()
             if layer == 1 else None)
    es = ExitStack()
    k = K(nc, es)
    b = make_bufs(k)
    emit_const_loads(k, b, cn)
    wo_cast = None
    if layer == 1:
        cwo = k.DL(lambda g: g.dma_start(out=woutb, in_=wo), [])
        wo_cast = ("DL", cwo)
    emit_att2_build(k, b, bias2, 32 * layer)
    b.att2_ready = b.att2_guard
    b.sbsb_guard = b.att2_guard
    xl = k.dma("xin", lambda g: g.dma_start(
        out=b.x_sb[:, :].rearrange("p (a c) -> p a c", c=2048),
        in_=x_in.rearrange("(a p) c -> p a c", p=128)), [b.att2_guard])
    tt_done = [("xin", xl)] * NTT
    def ctx_loader(k2, b2):
        ld = k2.dma("ctx", lambda g: g.dma_start(
            out=b2.arena[:, A_CTX:A_CTX+16384].rearrange("p (a w) -> p a w", w=1024),
            in_=ctx_own.rearrange("(a p) w -> p a w", p=128)), [b2.ctxbuf_war])
        return ("ctx", ld)
    tt_done = emit_attn_outproj(k, b, layer, woutb if layer == 1 else wo,
                                ctx_loader, tt_done, wo_cast=wo_cast)
    st_last = k.dma("xout", lambda g: g.dma_start(
        out=x_out.rearrange("(a p) c -> p a c", p=128),
        in_=b.x_sb[:, :].rearrange("p (a c) -> p a c", c=2048)), tt_done)
    k.waiter("G", [("xout", st_last)])
    k.finalize()
    es.close()
    return nc


def build_fused(n_sub=6):
    nc = bass.Bass("TRN2", target_bir_lowering=False)
    RG = [list(range(NCORES))]
    x_in = nc.dram_tensor("x_in", [TLOC, C], F32, kind="ExternalInput").ap()
    x_out = nc.dram_tensor("x_out", [TLOC, C], F32, kind="ExternalOutput").ap()
    cn = NS(ident=nc.dram_tensor("cn_ident", [128, 128], BF16, kind="ExternalInput").ap(),
            ones2=nc.dram_tensor("cn_ones2", [34, 128], BF16, kind="ExternalInput").ap(),
            tri=nc.dram_tensor("cn_tri", [128, 2048], BF16, kind="ExternalInput").ap(),
            identF=nc.dram_tensor("cn_identF", [128, 128], F32, kind="ExternalInput").ap())
    brows = nc.dram_tensor("cn_brows", [2, 2, 2, SEQ], BF16, kind="ExternalInput").ap()
    bmax0 = nc.dram_tensor("cn_bmax0", [128, 32], F32, kind="ExternalInput").ap()
    w0 = NS(gate_w=nc.dram_tensor("gate_w", [8, C], F32, kind="ExternalInput").ap(),
            sgw=nc.dram_tensor("sgw", [1, C], F32, kind="ExternalInput").ap(),
            fc1=nc.dram_tensor("fc1", [8, F_MOE, C], F32, kind="ExternalInput").ap(),
            fc2=nc.dram_tensor("fc2", [8, F_MOE, C], F32, kind="ExternalInput").ap(),
            proj=nc.dram_tensor("proj", [8, C, F_MOE], F32, kind="ExternalInput").ap(),
            sfc1=nc.dram_tensor("sfc1", [F_SH, C], F32, kind="ExternalInput").ap(),
            sfc2=nc.dram_tensor("sfc2", [F_SH, C], F32, kind="ExternalInput").ap(),
            sproj=nc.dram_tensor("sproj", [C, F_SH], F32, kind="ExternalInput").ap())
    w1 = NS(gate_w=nc.dram_tensor("l1_gate_w", [I_L1, C], I32, kind="ExternalInput").ap(),
            up_w=nc.dram_tensor("l1_up_w", [I_L1, C], I32, kind="ExternalInput").ap(),
            down_w=nc.dram_tensor("l1_down_w", [C, I_L1], I32, kind="ExternalInput").ap(),
            gate_scale=nc.dram_tensor("l1_gate_scale", [I_L1], F32, kind="ExternalInput").ap(),
            up_scale=nc.dram_tensor("l1_up_scale", [I_L1], F32, kind="ExternalInput").ap(),
            down_scale=nc.dram_tensor("l1_down_scale", [C], F32, kind="ExternalInput").ap())
    wa0 = NS(wq_s=nc.dram_tensor("wq_s", [C, 256], F32, kind="ExternalInput").ap(),
             wk_s=nc.dram_tensor("wk_s", [C, 256], F32, kind="ExternalInput").ap(),
             wv_s=nc.dram_tensor("wv_s", [C, 256], F32, kind="ExternalInput").ap())
    wa1 = NS(wqkv_s=nc.dram_tensor("wqkv_s", [768, C], F32, kind="ExternalInput").ap(),
             bqkv_s=nc.dram_tensor("bqkv_s", [768], F32, kind="ExternalInput").ap())
    wo0 = nc.dram_tensor("wo0", [C, C], F32, kind="ExternalInput").ap()
    wo1 = nc.dram_tensor("wo1", [C, C], F32, kind="ExternalInput").ap()
    bo0 = nc.dram_tensor("bo0", [1, C], F32, kind="ExternalInput").ap()
    bo1 = nc.dram_tensor("bo1", [1, C], F32, kind="ExternalInput").ap()

    scr = NS(fc1s=nc.dram_tensor("fc1s", [8, 6, 128, 4096], BF16, kind="Internal").ap(),
             fc2s=nc.dram_tensor("fc2s", [8, 6, 128, 4096], BF16, kind="Internal").ap(),
             projs=nc.dram_tensor("projs", [8, 4, 128, 5632], BF16, kind="Internal").ap(),
             sfc1s=nc.dram_tensor("sfc1s", [4, 6, 128, 4096], BF16, kind="Internal").ap(),
             sfc2s=nc.dram_tensor("sfc2s", [4, 6, 128, 4096], BF16, kind="Internal").ap(),
             sprojs=nc.dram_tensor("sprojs", [4, 4, 128, 5632], BF16, kind="Internal").ap(),
             l1gs=nc.dram_tensor("l1gs", [8, 4, 128, 4096], BF16, kind="Internal").ap(),
             l1us=nc.dram_tensor("l1us", [8, 4, 128, 4096], BF16, kind="Internal").ap(),
             l1ds=nc.dram_tensor("l1ds", [8, 4, 128, 4096], BF16, kind="Internal").ap(),
             wqkvb=nc.dram_tensor("wqkvb", [768, C], BF16, kind="Internal").ap(),
             woutb=nc.dram_tensor("woutb", [C, C], BF16, kind="Internal").ap())
    xn_stage = nc.dram_tensor("xn_stage", [C, TLOC], BF16, kind="Internal")
    xnT_all = nc.dram_tensor("xnT_all", [NCORES * C, TLOC], BF16, kind="Internal")
    ctx_stage = nc.dram_tensor("ctx_stage", [2 * D, SEQ], BF16, kind="Internal")
    ctx_all = nc.dram_tensor("ctx_all", [C, SEQ], BF16, kind="Internal")
    q_dram = nc.dram_tensor("q_dram", [2 * D, SEQ], BF16, kind="Internal").ap()

    es = ExitStack()
    k = K(nc, es)
    b = make_bufs(k)
    emit_const_loads(k, b, cn)
    b.pid_ld = k.dma("cn", lambda g: g.dma_start(out=b.pid_sb[0:1, 0:1],
                                                 in_=nc.partition_id_tensor.ap()))
    cwo = k.DL(lambda g: g.dma_start(out=scr.woutb, in_=wo1), [])
    b.castdone["wout"] = ("DL", cwo)
    emit_att2_build(k, b, bo0, 0)
    emit_att2_build(k, b, bo1, 32)
    b.att2_ready = b.att2_guard
    b.sbsb_guard = b.att2_guard
    xl = k.dma("xin", lambda g: g.dma_start(
        out=b.x_sb[:, :].rearrange("p (a c) -> p a c", c=2048),
        in_=x_in.rearrange("(a p) c -> p a c", p=128)), b.att2_guard)
    tt_done = [("xin", xl)] * NTT

    def ln_phase(pg):
        emit_ln_all(k, b, [[tt_done[tt]] + pg for tt in range(NTT)])

    def attn_sublayer(layer, wsrc_a, wo_src):
        pg = phase_boundary(k, b)
        emit_attn_wprep(k, b, layer, wsrc_a, scr)
        ln_phase(pg)
        st = k.dma("xns", lambda g: g.dma_start(
            out=xn_stage.ap().rearrange("(a p) w -> p a w", p=128),
            in_=b.arena[:, A_XNT:A_XNT+16384].rearrange("p (a w) -> p a w", w=1024)),
            b.xnT_done + pg)
        ag1 = k.CC(lambda g: g.collective_compute(
            "AllGather", ALU.bypass, RG, ins=[xn_stage.ap().opt()],
            outs=[xnT_all.ap().opt()]), [("xns", st)] + pg)
        xnT_all_ap = xnT_all.ap()
        slab_src = lambda cc, tb, th: xnT_all_ap[
            2048*tb + 128*cc: 2048*tb + 128*cc + 128,
            512*th: 512*th + 512]
        b.ag_guard = ("CC", k.cnt("CC"))
        pg2 = phase_boundary(k, b)
        emit_attn_qkv(k, b, layer, slab_src, q_dram)
        emit_attn_core(k, b, layer, [brows[layer, 0], brows[layer, 1]],
                       q_dram, ctx_stage.ap(),
                       bmax0=bmax0 if layer == 0 else None)
        pgc = phase_boundary(k, b)
        ag2 = k.CC(lambda g: g.collective_compute(
            "AllGather", ALU.bypass, RG, ins=[ctx_stage.ap().opt()],
            outs=[ctx_all.ap().opt()]), [b.ctx_done] + pgc)
        b.ctxbuf_war = pgc + [("CC", k.cnt("CC"))]
        ctx_all_ap = ctx_all.ap()
        def ctx_loader(k2, b2):
            ent = k2.sem_ent("ctx")
            acc = {}
            for (sn, v) in k2._flat([b2.ctxbuf_war, ("cn", b2.pid_ld)]):
                if v > 0 and v > acc.get(sn, 0):
                    acc[sn] = v
            wl = sorted(acc.items())

            def f(g):
                with g.register() as r:
                    g.reg_load(r, b2.pid_sb[0:1, 0:1])
                    g.reg_mul(r, r, TLOC)
                    off = g.snap(r, min_val=0, max_val=TLOC * (NCORES - 1))
                    src_ap = ctx_all_ap[:, bass.ds(off, TLOC)].rearrange(
                        "(a p) w -> p a w", p=128)
                    dst_ap = b2.arena[:, A_CTX:A_CTX + 16384].rearrange(
                        "p (a w) -> p a w", w=1024)
                    g.dma_start(out=dst_ap, in_=src_ap).then_inc(ent[0], 16)
            k2.dq["gpsimd"].append((wl, f, None, 0, ent[1] + 16))
            ent[1] += 16
            return ("ctx", ent[1])
        emit_attn_outproj(k, b, layer, wo_src, ctx_loader, tt_done,
                          wo_cast=(b.castdone["wout"] if layer == 1 else None))
        phase_boundary(k, b)

    def ffn_sub(kind, w, second=False):
        pg = phase_boundary(k, b)
        ln_phase(pg)
        emit_ffn(k, b, kind, w, tt_done, scr=scr, second=second)
    subs = [lambda: ffn_sub("l0", w0),
            lambda: attn_sublayer(0, wa0, wo0),
            lambda: ffn_sub("l0", w0, second=True),
            lambda: ffn_sub("l1", w1),
            lambda: attn_sublayer(1, wa1, scr.woutb),
            lambda: ffn_sub("l1", w1, second=True)]
    for f in subs[:n_sub]:
        f()

    st_last = k.dma("xout", lambda g: g.dma_start(
        out=x_out.rearrange("(a p) c -> p a c", p=128),
        in_=b.x_sb[:, :].rearrange("p (a c) -> p a c", c=2048)), tt_done)
    k.waiter("G", [("xout", st_last)])
    k.finalize()
    es.close()
    return nc



_L0_KEYS = dict(gate_w="L0_ffn_gate_w", sgw="L0_ffn_shared_gate_w",
                fc1="L0_ffn_expert_fc1", fc2="L0_ffn_expert_fc2",
                proj="L0_ffn_expert_proj", sfc1="L0_ffn_shared_fc1",
                sfc2="L0_ffn_shared_fc2", sproj="L0_ffn_shared_proj")
_L1_KEYS = dict(gate_w="L1_ffn_gate_w", up_w="L1_ffn_up_w", down_w="L1_ffn_down_w",
                gate_scale="L1_ffn_gate_scale", up_scale="L1_ffn_up_scale",
                down_scale="L1_ffn_down_scale")

_CACHE = {}


def _get(name, builder):
    if name not in _CACHE:
        _CACHE[name] = builder()
    return _CACHE[name]


def host_bias_rows(layer, h):
    j = np.arange(SEQ, dtype=np.float64)
    slope = 2.0 ** (-(h + 1) / 2.0)
    if layer == 0:
        row = slope * j * (1.0 / np.sqrt(np.float64(D)))
    else:
        row = (j - (SEQ - 1)) * slope
    hi = row.astype(np.float32).astype(BF)
    lo = (row - hi.astype(np.float64)).astype(np.float32).astype(BF)
    return np.stack([hi, lo])


def host_bmax0(c):
    out = np.zeros((128, 32), dtype=np.float32)
    for h2 in range(2):
        slope = 2.0 ** (-(2 * c + h2 + 1) / 2.0)
        for i in range(16):
            out[:, 16*h2 + i] = np.float32(
                slope * (512.0 * i + 511.0) / np.sqrt(np.float64(D)))
    return out


def run_ffn_step(kind, x_full, inputs, with_xnT=False):
    nc = _get(("ffn", kind, with_xnT), lambda: build_ffn_launch(kind, with_xnT))
    cns = host_consts()
    keys = _L0_KEYS if kind == "l0" else _L1_KEYS
    wmap = {k2: np.ascontiguousarray(np.asarray(inputs[v])) for k2, v in keys.items()}
    if kind == "l0":
        wmap["sgw"] = wmap["sgw"].reshape(1, C)
    in_maps = [dict(x_in=np.ascontiguousarray(x_full[TLOC*c:TLOC*(c+1)]), **wmap, **cns)
               for c in range(NCORES)]
    res = run_bass_kernel_spmd(nc, in_maps, core_ids=list(range(NCORES)))
    x_new = np.concatenate([res.results[c]["x_out"] for c in range(NCORES)], 0)
    if with_xnT:
        xnT = np.concatenate([res.results[c]["xnT_out"] for c in range(NCORES)], 1)
        return x_new, xnT
    return x_new


def run_attn_step(layer, x_full, xnT_full, inputs):
    nc_a = _get(("attn", layer), lambda: build_attn_core_launch(layer))
    cns = host_consts()
    in_maps = []
    for c in range(NCORES):
        m = dict(xnT_full=np.ascontiguousarray(xnT_full), **cns)
        m["cn_brows"] = np.ascontiguousarray(
            np.stack([host_bias_rows(layer, 2*c), host_bias_rows(layer, 2*c+1)]))
        if layer == 0:
            m["cn_bmax0"] = host_bmax0(c)
            for nm, key in (("wq_s", "L0_attn_Wq"), ("wk_s", "L0_attn_Wk"),
                            ("wv_s", "L0_attn_Wv")):
                m[nm] = np.ascontiguousarray(np.asarray(inputs[key])[:, 256*c:256*c+256])
        else:
            Wqkv = np.asarray(inputs["L1_attn_Wqkv"])
            bq = np.asarray(inputs["L1_attn_b_qkv"])
            rows = np.concatenate([Wqkv[256*c:256*c+256],
                                   Wqkv[2048+256*c:2048+256*c+256],
                                   Wqkv[4096+256*c:4096+256*c+256]], 0)
            m["wqkv_s"] = np.ascontiguousarray(rows)
            m["bqkv_s"] = np.ascontiguousarray(np.concatenate(
                [bq[256*c:256*c+256], bq[2048+256*c:2048+256*c+256],
                 bq[4096+256*c:4096+256*c+256]]))
        in_maps.append(m)
    res = run_bass_kernel_spmd(nc_a, in_maps, core_ids=list(range(NCORES)))
    ctx_full = np.concatenate([res.results[c]["ctx_out"] for c in range(NCORES)], 0)

    nc_o = _get(("outproj", layer), lambda: build_outproj_launch(layer))
    wo = np.ascontiguousarray(np.asarray(
        inputs["L0_attn_Wo" if layer == 0 else "L1_attn_Wout"]))
    b2 = np.ascontiguousarray(np.asarray(
        inputs["L0_attn_bo" if layer == 0 else "L1_attn_b_out"]).reshape(1, C))
    in_maps = [dict(x_in=np.ascontiguousarray(x_full[TLOC*c:TLOC*(c+1)]),
                    ctx_own=np.ascontiguousarray(ctx_full[:, TLOC*c:TLOC*(c+1)]),
                    wo=wo, bias2=b2, **cns)
               for c in range(NCORES)]
    res = run_bass_kernel_spmd(nc_o, in_maps, core_ids=list(range(NCORES)))
    return np.concatenate([res.results[c]["x_out"] for c in range(NCORES)], 0)


def kernel_unfused(**inputs):
    x = np.ascontiguousarray(np.asarray(inputs["L0_attn_x"], np.float32).reshape(SEQ, C))
    x, xnT = run_ffn_step("l0", x, inputs, with_xnT=True)
    x = run_attn_step(0, x, xnT, inputs)
    x = run_ffn_step("l0", x, inputs)
    x, xnT = run_ffn_step("l1", x, inputs, with_xnT=True)
    x = run_attn_step(1, x, xnT, inputs)
    x = run_ffn_step("l1", x, inputs)
    return x.reshape(1, SEQ, C)


def kernel(**inputs, ):
    n_sub = int(inputs.pop("_n_sub", 6))
    x = np.ascontiguousarray(np.asarray(inputs["L0_attn_x"], np.float32).reshape(SEQ, C))
    nc = _get(("fused", n_sub), lambda: build_fused(n_sub))
    cns = host_consts()
    wmap = {}
    for k2, v in _L0_KEYS.items():
        wmap[k2] = np.ascontiguousarray(np.asarray(inputs[v]))
    wmap["sgw"] = wmap["sgw"].reshape(1, C)
    for k2, v in _L1_KEYS.items():
        wmap["l1_" + k2] = np.ascontiguousarray(np.asarray(inputs[v]))
    wmap["wo0"] = np.ascontiguousarray(np.asarray(inputs["L0_attn_Wo"]))
    wmap["wo1"] = np.ascontiguousarray(np.asarray(inputs["L1_attn_Wout"]))
    wmap["bo0"] = np.ascontiguousarray(np.asarray(inputs["L0_attn_bo"]).reshape(1, C))
    wmap["bo1"] = np.ascontiguousarray(np.asarray(inputs["L1_attn_b_out"]).reshape(1, C))
    Wqkv = np.asarray(inputs["L1_attn_Wqkv"])
    bq = np.asarray(inputs["L1_attn_b_qkv"])
    in_maps = []
    for c in range(NCORES):
        m = dict(x_in=np.ascontiguousarray(x[TLOC*c:TLOC*(c+1)]), **wmap, **cns)
        m["cn_brows"] = np.ascontiguousarray(np.stack([
            np.stack([host_bias_rows(0, 2*c), host_bias_rows(0, 2*c+1)]),
            np.stack([host_bias_rows(1, 2*c), host_bias_rows(1, 2*c+1)])]))
        m["cn_bmax0"] = host_bmax0(c)
        for nm, key in (("wq_s", "L0_attn_Wq"), ("wk_s", "L0_attn_Wk"),
                        ("wv_s", "L0_attn_Wv")):
            m[nm] = np.ascontiguousarray(np.asarray(inputs[key])[:, 256*c:256*c+256])
        m["wqkv_s"] = np.ascontiguousarray(np.concatenate(
            [Wqkv[256*c:256*c+256], Wqkv[2048+256*c:2048+256*c+256],
             Wqkv[4096+256*c:4096+256*c+256]], 0))
        m["bqkv_s"] = np.ascontiguousarray(np.concatenate(
            [bq[256*c:256*c+256], bq[2048+256*c:2048+256*c+256],
             bq[4096+256*c:4096+256*c+256]]))
        in_maps.append(m)
    res = run_bass_kernel_spmd(nc, in_maps, core_ids=list(range(NCORES)))
    out = np.concatenate([res.results[c]["x_out"] for c in range(NCORES)], 0)
    return out.reshape(1, SEQ, C)
